# version 05
# baseline (speedup 1.0000x reference)
import dataclasses
import numpy as np
import ml_dtypes
from contextlib import ExitStack

import concourse.bass as bass
import concourse.mybir as mybir
from concourse.bass_utils import run_bass_kernel_spmd

F32 = mybir.dt.float32
BF16 = mybir.dt.bfloat16
I32 = mybir.dt.int32
AL = mybir.AluOpType
AF = mybir.ActivationFunctionType

B = 2
L = 16384
DM = 1024
DI = 2048
DH = 1024
S = 16
DTR = 64
HID = 1408
NE = 8
TC = 512
NCH = L // TC
NDT = DH // 128
NKM = DM // 128
NHT = HID // 128
HL = L // 2
EPS = 1e-5
FBLK = 512
FB2 = 256
BIGNEG = -1.0e30


class Em:

    ENGS = ("sync", "act", "dve", "pe", "pool")
    NSTREAM = {"sync": 4, "act": 2, "pool": 2}

    def __init__(self, nc, ctx):
        self.nc = nc
        self.names = list(self.ENGS)
        for e, n in self.NSTREAM.items():
            self.names += ["%s.%d" % (e, j) for j in range(n)]
        self.sems = {x: ctx.enter_context(
            nc.semaphore("sem_" + x.replace(".", "_"))) for x in self.names}
        self.counts = {x: 0 for x in self.names}
        self.rr = {e: 0 for e in self.NSTREAM}
        self.last_w = {}
        self.readers = {}
        self.marks = {e: {p: 0 for p in self.names} for e in self.ENGS}
        self.thunks = {e: [] for e in self.ENGS}

    def _need(self, eng, dep, needs):
        e2, c2, was_dma = dep
        if self.marks[eng][e2] >= c2:
            return
        needs[e2] = max(needs.get(e2, 0), c2)

    def op(self, eng, fn, reads=(), writes=(), dma=False):
        extra = getattr(self, "act_extra_read", None)
        if extra is not None and eng != "sync" and extra in self.last_w:
            reads = tuple(reads) + (extra,)
        needs = {}
        for k in reads:
            d = self.last_w.get(k)
            if d:
                self._need(eng, d, needs)
        for k in writes:
            d = self.last_w.get(k)
            if d:
                self._need(eng, d, needs)
            for d in self.readers.get(k, ()):
                self._need(eng, d, needs)
        if dma:
            tgt = "%s.%d" % (eng, self.rr[eng])
            self.rr[eng] = (self.rr[eng] + 1) % self.NSTREAM[eng]
            if self.counts[tgt] > 0:
                self._need(eng, (tgt, self.counts[tgt], True), needs)
        else:
            tgt = eng
        for e2, c2 in sorted(needs.items()):
            self.marks[eng][e2] = c2
            sem = self.sems[e2]
            self.thunks[eng].append(lambda E, s=sem, c=c2: E.wait_ge(s, c))
        inc = 16 if dma else 1
        self.counts[tgt] += inc
        sem = self.sems[tgt]

        def run(E, fn=fn, sem=sem, inc=inc):
            fn(E).then_inc(sem, inc)

        self.thunks[eng].append(run)
        tok = (tgt, self.counts[tgt], dma)
        for k in writes:
            self.last_w[k] = tok
            self.readers[k] = []
        for k in reads:
            self.readers.setdefault(k, []).append(tok)

    def raw(self, eng, fn):
        self.thunks[eng].append(fn)

    def phase(self):
        snap = dict(self.counts)
        for e in self.ENGS:
            for p in self.names:
                if p == e or snap[p] == 0 or self.marks[e][p] >= snap[p]:
                    continue
                self.marks[e][p] = snap[p]
                sem, c2 = self.sems[p], snap[p]
                self.thunks[e].append(lambda E, s=sem, c=c2: E.wait_ge(s, c))


class Arena:

    def __init__(self, tensor, nbytes):
        self.t = tensor
        self.n = nbytes
        self.off = 0

    def reset(self, off=0):
        self.off = off

    def alloc(self, shape, dt=F32):
        size = {F32: 4, BF16: 2, I32: 4}[dt]
        nb = shape[1] * size
        off = (self.off + 63) & ~63
        assert off + nb <= self.n, "arena overflow: %d + %d > %d" % (off, nb, self.n)
        self.off = off + nb
        view = self.t[0:shape[0], off:off + nb].bitcast(dt)
        return view


def build_nc(nch=NCH, debug=False):
    nc = bass.Bass("TRN2", target_bir_lowering=False)

    def inp(name, shape, dt=F32):
        return nc.dram_tensor(name, list(shape), dt, kind="ExternalInput").ap()

    x_ln = inp("x_ln", (HL, DM))
    x_my = inp("x_my", (4096, DM))
    x_myT = inp("x_myT", (DM, 4096))
    w_inT = inp("w_inT", (2, DM, 2 * DH))
    conv_diag = inp("conv_diag", (2, 4, 128, NDT * 128))
    conv_b = inp("conv_b", (128, 2 * NDT))
    w_xpT = inp("w_xpT", (2, DH, 128))
    w_dtT = inp("w_dtT", (2, DTR, DH))
    dt_b = inp("dt_b", (128, 2 * NDT))
    Alog_r = inp("Alog_r", (128, 2 * NDT * S))
    D_r = inp("D_r", (128, 2 * NDT))
    w_mo = inp("w_mo", (2, DM, DH))
    w_opT = inp("w_opT", (2, DM, DM))
    out_bias = inp("out_bias", (128, NKM))
    gamma = inp("gamma", (1, DM))
    beta = inp("beta", (1, DM))
    gate_wT = inp("gate_wT", (DM, NE))
    eb_row = inp("eb_row", (1, NE))
    w_upT = inp("w_upT", (NE + 1, DM, NHT * 128))
    w_downT = inp("w_downT", (NE + 1, NHT * 128, DM))
    cfg = inp("cfg", (1, 8), I32)
    magics = inp("magics", (1, 8), I32)
    offs = inp("offs", (1, 2048), I32)

    out = nc.dram_tensor("out", [DM, 4096], F32, kind="ExternalOutput").ap()
    dbg = {}
    if debug:
        for nm, shp, dt_ in [("d_n2", (DM, HL), BF16), ("d_xdbl", (2 * 128, L), F32),
                             ("d_delta", (DH, TC), F32),
                             ("d_partial", (NCH * DM, TC), F32),
                             ("d_stream2", (DM, 4096), F32),
                             ("d_wgt", (128, 4 * NE), F32),
                             ("d_y", (DH, TC), F32), ("d_xc", (DH, L), F32),
                             ("d_pm", (8 * DM, TC), F32),
                             ("d_x0", (128, DM), F32), ("d_x1", (128, DM), F32),
                             ("d_x2", (128, DM), F32), ("d_cols", (128, 8), F32),
                             ("d_xi", (DH, TC), BF16), ("d_n2t", (DM, TC), BF16),
                             ("d_wt", (2 * DH, DM), BF16),
                             ("d_wtb", (128, DM), BF16),
                             ("d_mob", (128, NKM * DH), BF16),
                             ("d_opb", (128, NKM * DM), BF16)]:
            dbg[nm] = nc.dram_tensor(nm, list(shp), dt_, kind="ExternalOutput").ap()

    xcd = nc.dram_tensor("xcd", [2, DH, L], F32)
    szd = nc.dram_tensor("szd", [2, DH, L], F32)
    xdbld = nc.dram_tensor("xdbld", [2, 128, L], F32)
    partiald = nc.dram_tensor("partiald", [2, NCH, DM, TC], F32)
    n2loc = nc.dram_tensor("n2loc", [DM, HL], BF16)
    xdpart = nc.dram_tensor("xdpart", [2, 128, L], F32)
    xdpart2 = nc.dram_tensor("xdpart2", [2, 128, L], F32)
    pmined = nc.dram_tensor("pmined", [8, DM, TC], F32)
    ppartd = nc.dram_tensor("ppartd", [8, DM, TC], F32)
    s2d = nc.dram_tensor("s2d", [DM, 4096], F32)
    wmoT = nc.dram_tensor("wmoT", [2, DH, DM], BF16)
    sh_n2 = nc.dram_tensor("sh_n2", [2, DM, HL], BF16, addr_space="Shared")
    sh_xdbl = nc.dram_tensor("sh_xdbl", [2, 2, 128, L], F32, addr_space="Shared")
    sh_part = nc.dram_tensor("sh_part", [2, 16, DM, TC], F32, addr_space="Shared")
    sh_flag = nc.dram_tensor("sh_flag", [2, 16], I32, addr_space="Shared")

    identd = nc.inline_tensor(np.eye(128, dtype=ml_dtypes.bfloat16), "identbf").ap()
    onesd = nc.inline_tensor(np.ones((1, 128), dtype=ml_dtypes.bfloat16), "onesbf").ap()
    onescd = nc.inline_tensor(np.ones((128, 1), dtype=ml_dtypes.bfloat16), "onescbf").ap()
    p8_np = np.broadcast_to(np.arange(NE, 0, -1, dtype=np.float32), (128, NE)).copy()
    p8d = nc.inline_tensor(p8_np, "p8rowd").ap()
    cst_np = np.broadcast_to(
        np.array([EPS, 1.0, -1.0, -0.5, 1.0 / DM, 0.0, 0.5, 2.0],
                 dtype=np.float32), (128, 8)).copy()
    cstd = nc.inline_tensor(cst_np, "cstd").ap()

    def sym(full_ap, offset, rows, rstride, cols):
        Vec = type(full_ap.ap)
        return dataclasses.replace(full_ap, offset=offset,
                                   ap=Vec([[rstride, rows], [1, cols]]))

    with ExitStack() as CTX:
        em = Em(nc, CTX)
        sb = lambda name, shape, dt=F32: CTX.enter_context(
            nc.sbuf_tensor(name, list(shape), dt))

        ident = sb("ident", (128, 128), BF16)
        onesb = sb("onesb", (1, 128), BF16)
        onescb = sb("onescb", (128, 1), BF16)
        p8row = sb("p8row", (128, NE))
        cfgS = sb("cfgS", (1, 8), I32)
        magS = sb("magS", (1, 8), I32)
        spoll = sb("spoll", (1, 4), I32)
        convbS = sb("convbS", (128, 2 * NDT))
        dtbS = sb("dtbS", (128, 2 * NDT))
        AbarS = sb("AbarS", (128, 2 * NDT * S))
        DS = sb("DS", (128, 2 * NDT))
        outbS = sb("outbS", (128, NKM))
        gamS = sb("gamS", (128, DM))
        betS = sb("betS", (128, DM))
        ebbS = sb("ebbS", (128, NE))
        carry = sb("carry", (128, NDT * S))
        arena_t = sb("arena", (128, 191488), mybir.dt.uint8)
        ar = Arena(arena_t, 191488)
        PB = [CTX.enter_context(nc.psum_tensor("pb%d" % i, [128, 512]))
              for i in range(7)]
        PT = CTX.enter_context(nc.psum_tensor("ptb", [128, 128], BF16))

        st = CTX.enter_context(nc.semaphore("st_out"))
        psem = CTX.enter_context(nc.semaphore("psem"))

        offS = sb("offS", (1, 2048), I32)
        cstS = sb("cstS", (128, 8))
        for _ci, _cv in enumerate([EPS, 1.0, -1.0, -0.5, 1.0 / DM, 0.0, 0.5, 2.0]):
            nc.const_aps.register(F32, _cv, cstS[:, _ci:_ci + 1])
        em.act_extra_read = "k_cst"
        EngT = mybir.EngineType
        ENGT = {"sync": EngT.SP, "act": EngT.Activation, "dve": EngT.DVE,
                "pe": EngT.PE, "pool": EngT.Pool}
        r_nd = nc.alloc_register(EngT.SP, "r_nd")
        r_pv = nc.alloc_register(EngT.SP, "r_pv")
        r_pc = nc.alloc_register(EngT.SP, "r_pc")
        r_mg = nc.alloc_register(EngT.SP, "r_mg")
        r_off = {e: nc.alloc_register(ENGT[e], "r_off_" + e)
                 for e in ("sync", "act", "pool")}
        r_of2 = {e: nc.alloc_register(ENGT[e], "r_of2_" + e)
                 for e in ("sync", "act", "pool")}
        offspec = []

        def roff(base, const):
            offspec.append((base, const))
            return len(offspec) - 1

        def dma(key_w, out_ap, in_ap, reads=(), eng="sync"):
            em.op(eng, lambda E: E.dma_start(out=out_ap, in_=in_ap),
                  reads=reads, writes=(key_w,), dma=True)

        dma("k_cst", cstS[:], cstd)
        dma("k_ident", ident[:], identd)
        dma("k_ones", onesb[:], onesd)
        dma("k_onesc", onescb[:], onescd)
        dma("k_p8", p8row[:], p8d)
        dma("k_cfg", cfgS[:], cfg)
        dma("k_mag", magS[:], magics)
        dma("k_convb", convbS[:], conv_b)
        dma("k_dtb", dtbS[:], dt_b)
        dma("k_alog", AbarS[:], Alog_r)
        dma("k_D", DS[:], D_r)
        dma("k_outb", outbS[:], out_bias)
        Vec = type(gamma.ap)
        dma("k_gam", gamS[:],
            dataclasses.replace(gamma, offset=0, ap=Vec([[0, 128], [1, DM]])))
        dma("k_bet", betS[:],
            dataclasses.replace(beta, offset=0, ap=Vec([[0, 128], [1, DM]])))
        dma("k_eb", ebbS[:],
            dataclasses.replace(eb_row, offset=0, ap=Vec([[0, 128], [1, NE]])))

        em.op("act", lambda E: E.activation(AbarS[:], AbarS[:], AF.Exp),
              reads=("k_alog",), writes=("k_alog",))
        em.op("dve", lambda E: E.tensor_scalar_mul(AbarS[:], AbarS[:], -1.0),
              reads=("k_alog",), writes=("k_alog",))

        dma("k_offs", offS[:], offs)
        em.op("sync", lambda E: E.reg_mov(r_pc, 0) and None or E.nop(),
              reads=("k_offs",), writes=("k_regs",))

        def dyn_copy(key_w, dst_full, dst_base, dst_c, src_full, src_base,
                     src_c, nelem, reads,
                     engs=("sync", "act", "pool", "sync")):
            per = nelem // len(engs)
            assert per * len(engs) == nelem
            VecD = type(dst_full.ap)
            VecS = type(src_full.ap)
            for pi, eng in enumerate(engs):
                di = roff(dst_base, dst_c + pi * per) if dst_base else None
                si = roff(src_base, src_c + pi * per) if src_base else None

                def fn(E, di=di, si=si, pi=pi, eng=eng):
                    if di is not None:
                        E.load(r_off[eng], offS[0:1, di:di + 1])
                        do = r_off[eng]
                    else:
                        do = dst_c + pi * per
                    if si is not None:
                        E.load(r_of2[eng], offS[0:1, si:si + 1])
                        so = r_of2[eng]
                    else:
                        so = src_c + pi * per
                    dap = dataclasses.replace(dst_full, offset=do,
                                              ap=VecD([[1, per]]))
                    sap = dataclasses.replace(src_full, offset=so,
                                              ap=VecS([[1, per]]))
                    return E.dma_start(out=dap, in_=sap)
                em.op(eng, fn, reads=tuple(reads) + ("k_offs",),
                      writes=(key_w,), dma=True)

        def barrier(phase, data_keys):
            fl = sh_flag.ap()
            Vec = type(fl.ap)
            idx1 = roff("flagw", phase)

            def wrf(E, idx1=idx1):
                E.load(r_off["sync"], offS[0:1, idx1:idx1 + 1])
                myf = dataclasses.replace(fl, offset=r_off["sync"],
                                          ap=Vec([[1, 1]]))
                return E.dma_start(out=myf, in_=magS[0:1, phase:phase + 1])
            em.op("sync", wrf, reads=tuple(data_keys) + ("k_mag", "k_offs"),
                  writes=("k_flg%d" % phase,), dma=True)
            idx2 = roff("flagr", phase)

            def poll(E, idx2=idx2):
                E.load(r_of2["sync"], offS[0:1, idx2:idx2 + 1])
                otf = dataclasses.replace(fl, offset=r_of2["sync"],
                                          ap=Vec([[1, 1]]))
                E.load(r_mg, magS[0:1, phase:phase + 1])
                E.reg_mov(r_nd, 1)
                with E.While(r_nd):
                    E.dma_start(out=spoll[0:1, 0:1], in_=otf).then_inc(psem, 16)
                    E.reg_add(r_pc, r_pc, 16)
                    E.wait_ge(psem, r_pc)
                    E.load(r_pv, spoll[0:1, 0:1])
                    E.reg_alu(r_nd, r_pv, r_mg, AL.bitwise_xor)
            em.raw("sync", poll)
            em.op("sync", lambda E, phase=phase: E.dma_start(
                out=spoll[0:1, 1:2], in_=magS[0:1, phase:phase + 1]),
                reads=(), writes=("k_pair%d" % phase,), dma=True)

        def ln_cols(colA, xt_key, xtile, scr, dst_key_sfx):
            em.op("act", lambda E: E.activation(
                scr[:, 0:DM], xtile[:, 0:DM], AF.Square, accum_out=colA[:, 0:1]),
                reads=(xt_key,), writes=("c0" + dst_key_sfx, "cjunk"))
            em.op("act", lambda E: E.activation(
                scr[:, 0:DM], xtile[:, 0:DM], AF.Identity, accum_out=colA[:, 1:2]),
                reads=(xt_key,), writes=("c1" + dst_key_sfx, "cjunk"))
            em.op("dve", lambda E: E.tensor_scalar_mul(
                colA[:, 2:3], colA[:, 1:2], 1.0 / DM),
                reads=("c1" + dst_key_sfx,), writes=("c2" + dst_key_sfx,))
            em.op("dve", lambda E: E.tensor_tensor(
                colA[:, 3:4], colA[:, 2:3], colA[:, 2:3], AL.mult),
                reads=("c2" + dst_key_sfx,), writes=("c3" + dst_key_sfx,))
            em.op("dve", lambda E: E.tensor_scalar(
                colA[:, 4:5], colA[:, 0:1], 1.0 / DM, None, AL.mult),
                reads=("c0" + dst_key_sfx,), writes=("c4" + dst_key_sfx,))
            em.op("dve", lambda E: E.tensor_tensor(
                colA[:, 4:5], colA[:, 4:5], colA[:, 3:4], AL.subtract),
                reads=("c3" + dst_key_sfx, "c4" + dst_key_sfx),
                writes=("c4" + dst_key_sfx,))
            em.op("act", lambda E: E.activation(
                colA[:, 5:6], colA[:, 4:5], AF.Ln, bias=EPS),
                reads=("c4" + dst_key_sfx,), writes=("c5" + dst_key_sfx,))
            em.op("act", lambda E: E.activation(
                colA[:, 5:6], colA[:, 5:6], AF.Exp, scale=-0.5),
                reads=("c5" + dst_key_sfx,), writes=("c5" + dst_key_sfx,))
            em.op("dve", lambda E: E.tensor_tensor(
                colA[:, 6:7], colA[:, 2:3], colA[:, 5:6], AL.mult),
                reads=("c2" + dst_key_sfx, "c5" + dst_key_sfx),
                writes=("c6" + dst_key_sfx,))
            em.op("dve", lambda E: E.tensor_scalar_mul(
                colA[:, 6:7], colA[:, 6:7], -1.0),
                reads=("c6" + dst_key_sfx,), writes=("c6" + dst_key_sfx,))

        if True:
            ar.reset()
            xt = ar.alloc((128, DM))
            x1t = ar.alloc((128, DM))
            n2t = ar.alloc((128, DM), BF16)
            n2fm = ar.alloc((128, DM), BF16)
            colA = ar.alloc((128, 8))
            lnscr = ar.alloc((128, DM))
            pt = PT

            for i in range(HL // 128):
                dma("p0x", xt[:], sym(x_ln, i * 128 * DM, 128, DM, DM))
                if debug and i == 0:
                    dma("dbg0", dbg["d_x0"], xt[:], reads=("p0x",))
                ln_cols(colA, "p0x", xt, lnscr, "p0")
                if debug and i == 0:
                    dma("dbg1b", dbg["d_cols"], colA[:],
                        reads=tuple("c%dp0" % q for q in range(7)))
                em.op("dve", lambda E: E.tensor_scalar(
                    x1t[:], xt[:], colA[:, 5:6], colA[:, 6:7], AL.mult, AL.add),
                    reads=("p0x", "c5p0", "c6p0"), writes=("p0x1",))
                if debug and i == 0:
                    dma("dbg1", dbg["d_x1"], x1t[:], reads=("p0x1",))
                ln_cols(colA, "p0x1", x1t, lnscr, "p0")
                em.op("dve", lambda E: E.tensor_scalar(
                    x1t[:], x1t[:], colA[:, 5:6], colA[:, 6:7], AL.mult, AL.add),
                    reads=("p0x1", "c5p0", "c6p0"), writes=("p0x1",))
                em.op("dve", lambda E: E.tensor_tensor(
                    x1t[:], x1t[:], gamS[:], AL.mult),
                    reads=("p0x1", "k_gam"), writes=("p0x1",))
                em.op("dve", lambda E: E.tensor_tensor(
                    n2t[:], x1t[:], betS[:], AL.add),
                    reads=("p0x1", "k_bet"), writes=("p0n2",))
                if debug and i == 0:
                    dma("dbg2", dbg["d_x2"], x1t[:], reads=("p0x1",))
                for j in range(NKM):
                    em.op("pe", lambda E, j=j: E.transpose(
                        pt[:], n2t[:, j * 128:(j + 1) * 128], ident[:]),
                        reads=("p0n2", "k_ident"), writes=("p0pt",))
                    em.op("act", lambda E, j=j: E.activation(
                        n2fm[:, j * 128:(j + 1) * 128], pt[:], AF.Copy),
                        reads=("p0pt",), writes=("p0fm%d" % j,))
                for j in range(NKM):
                    dma("k_n2loc", sym(n2loc.ap(), (j * 128) * HL + i * 128,
                                       128, HL, 128),
                        n2fm[:, j * 128:(j + 1) * 128],
                        reads=("p0fm%d" % j,))

        dyn_copy("k_shn2w", sh_n2.ap(), "n2w", 0, n2loc.ap(), None, 0,
                 DM * HL, ("k_n2loc",))
        barrier(0, ["k_shn2w"])

        if True:
            em.phase()
            ar.reset()
            stg = ar.alloc((128, DM))
            mo_b = ar.alloc((128, NKM * DH), BF16)
            op_b = ar.alloc((128, NKM * DM), BF16)
            wtPW = ar.alloc((128, DM), BF16)
            pw = PB[0]
            for d in range(2):
                for m in range(NKM):
                    dma("pwst", stg[:, 0:DH], sym(
                        w_mo, d * DM * DH + m * 128 * DH, 128, DH, DH))
                    em.op("dve", lambda E, m=m: E.tensor_copy(
                        mo_b[:, m * DH:(m + 1) * DH], stg[:, 0:DH]),
                        reads=("pwst",), writes=("pwmo%d" % m,))
                    dma("pwst", stg[:], sym(
                        w_opT, d * DM * DM + m * 128 * DM, 128, DM, DM))
                    em.op("dve", lambda E, m=m: E.tensor_copy(
                        op_b[:, m * DM:(m + 1) * DM], stg[:]),
                        reads=("pwst",), writes=("pwop%d" % m,))
                rd = tuple("pwmo%d" % m for m in range(NKM)) + \
                    tuple("pwop%d" % m for m in range(NKM))
                for kt in range(NDT):
                    for half in range(2):
                        for m in range(NKM):
                            em.op("pe", lambda E, kt=kt, half=half, m=m: E.matmul(
                                pw[:],
                                mo_b[:, m * DH + kt * 128:m * DH + (kt + 1) * 128],
                                op_b[:, m * DM + half * 512:m * DM + (half + 1) * 512],
                                start=(m == 0), stop=(m == NKM - 1)),
                                reads=rd, writes=("pwps",))
                        em.op("act", lambda E, half=half: E.activation(
                            wtPW[:, half * 512:(half + 1) * 512], pw[:], AF.Copy),
                            reads=("pwps",), writes=("pwwt",))
                    dma("pwout", sym(wmoT.ap(), d * DH * DM + kt * 128 * DM,
                                     128, DM, DM), wtPW[:], reads=("pwwt",))
            if debug:
                dma("dbgwtb", dbg["d_wtb"], wtPW[:], reads=("pwwt",))
                dma("dbgmob", dbg["d_mob"], mo_b[:],
                    reads=tuple("pwmo%d" % m for m in range(NKM)))
                dma("dbgopb", dbg["d_opb"], op_b[:],
                    reads=tuple("pwop%d" % m for m in range(NKM)))

        if True:
            em.phase()
            ar.reset()
            w_in_b = ar.alloc((128, NKM * 2 * DH), BF16)
            wstg = ar.alloc((128, 2 * DH))
            cdg = ar.alloc((128, 4 * NDT * 128), BF16)
            cstg = ar.alloc((128, NDT * 128))
            xpw = ar.alloc((128, NDT * 128), BF16)
            xpstg = ar.alloc((128, 128))
            n2bP = [[ar.alloc((128, TC), BF16) for k in range(NKM)]
                    for p in range(2)]
            n2rP = [[ar.alloc((128, TC), BF16) for k in range(NKM)]
                    for p in range(2)]
            xib = [[ar.alloc((128, TC + 4), BF16)
                    for t in range(NDT)] for p in range(2)]
            xcf = ar.alloc((128, TC))
            xcb = ar.alloc((128, NDT * TC), BF16)
            szf = ar.alloc((128, TC))
            xdacc = ar.alloc((128, TC))
            p_in = [PB[0], PB[1]]
            p_c = [PB[2], PB[3]]
            p_x = PB[4]

            for d in range(2):
                for kt in range(NKM):
                    dma("m1wstg", wstg[:], sym(
                        w_inT, d * DM * 2 * DH + kt * 128 * 2 * DH, 128,
                        2 * DH, 2 * DH))
                    em.op("dve", lambda E, kt=kt: E.tensor_copy(
                        w_in_b[:, kt * 2 * DH:(kt + 1) * 2 * DH], wstg[:]),
                        reads=("m1wstg",), writes=("m1wb%d" % kt,))
                for tap in range(4):
                    dma("m1cstg", cstg[:], sym(
                        conv_diag, (d * 4 + tap) * 128 * NDT * 128, 128,
                        NDT * 128, NDT * 128))
                    em.op("dve", lambda E, tap=tap: E.tensor_copy(
                        cdg[:, tap * NDT * 128:(tap + 1) * NDT * 128], cstg[:]),
                        reads=("m1cstg",), writes=("m1cdg%d" % tap,))
                for t in range(NDT):
                    dma("m1xpstg", xpstg[:], sym(
                        w_xpT, d * DH * 128 + t * 128 * 128, 128, 128, 128))
                    em.op("dve", lambda E, t=t: E.tensor_copy(
                        xpw[:, t * 128:(t + 1) * 128], xpstg[:]),
                        reads=("m1xpstg",), writes=("m1xpw%d" % t,))
                wrd = tuple("m1wb%d" % kt for kt in range(NKM))
                crd = tuple("m1cdg%d" % t for t in range(4))
                xprd = tuple("m1xpw%d" % t for t in range(NDT))

                for i in range(nch):
                    g = i if d == 0 else (nch - 1 - i)
                    par = i % 2
                    slot = g // (HL // TC)
                    col = (g % (HL // TC)) * TC
                    n2b = n2bP[par]
                    n2r = n2rP[par]
                    n2rd = []
                    for kt in range(NKM):
                        off = slot * DM * HL + kt * 128 * HL + col
                        dma("m1n2_%d_%d" % (par, kt), n2b[kt][:],
                            sym(sh_n2.ap(), off, 128, HL, TC),
                            reads=("k_pair0",))
                        if d == 1:
                            em.op("dve", lambda E, kt=kt, n2b=n2b, n2r=n2r:
                                  E.tensor_copy(
                                      n2r[kt][:], n2b[kt][:, ::-1]),
                                  reads=("m1n2_%d_%d" % (par, kt),),
                                  writes=("m1n2r%d_%d" % (par, kt),))
                            n2rd.append("m1n2r%d_%d" % (par, kt))
                        else:
                            n2rd.append("m1n2_%d_%d" % (par, kt))
                    n2use = n2r if d == 1 else n2b
                    n2rd = tuple(n2rd)
                    if debug and d == 0 and i == 0:
                        for kt in range(NKM):
                            dma("dn2t", sym(dbg["d_n2t"], kt * 128 * TC,
                                            128, TC, TC), n2b[kt][:],
                                reads=("m1n2_%d_%d" % (par, kt),))

                    for t in range(NDT):
                        pb, kpb = p_in[t % 2], "m1pi%d" % (t % 2)
                        for kt in range(NKM):
                            em.op("pe", lambda E, kt=kt, t=t, pb=pb,
                                  n2u=n2use: E.matmul(
                                pb[:], w_in_b[:, kt * 2 * DH + t * 128:
                                              kt * 2 * DH + (t + 1) * 128],
                                n2u[kt][:],
                                start=(kt == 0), stop=(kt == NKM - 1)),
                                reads=wrd + n2rd, writes=(kpb,))
                        if i == 0:
                            em.op("dve", lambda E, par=par, t=t: E.memset(
                                xib[par][t][:, 1:4], 0.0),
                                writes=("m1hal%d_%d" % (par, t),))
                        else:
                            em.op("dve", lambda E, par=par, t=t: E.tensor_copy(
                                xib[par][t][:, 1:4],
                                xib[1 - par][t][:, TC + 1:TC + 4]),
                                reads=("m1xib%d_%d" % (1 - par, t),),
                                writes=("m1hal%d_%d" % (par, t),))
                        em.op("act", lambda E, par=par, t=t, pb=pb: E.activation(
                            xib[par][t][:, 4:TC + 4], pb[:], AF.Copy),
                            reads=(kpb,), writes=("m1xib%d_%d" % (par, t),))
                        if debug and d == 0 and i == 0:
                            dma("dxi", sym(dbg["d_xi"], t * 128 * TC, 128, TC, TC),
                                xib[par][t][:, 4:TC + 4],
                                reads=("m1xib%d_%d" % (par, t),))
                        pc, kpc = p_c[t % 2], "m1pc%d" % (t % 2)
                        for tap in range(4):
                            em.op("pe", lambda E, tap=tap, t=t, par=par, pc=pc:
                                  E.matmul(
                                      pc[:],
                                      cdg[:, (tap * NDT + t) * 128:
                                          (tap * NDT + t + 1) * 128],
                                      xib[par][t][:, 1 + tap:1 + tap + TC],
                                      start=(tap == 0), stop=(tap == 3)),
                                  reads=crd + ("m1xib%d_%d" % (par, t),
                                               "m1hal%d_%d" % (par, t)),
                                  writes=(kpc,))
                        em.op("act", lambda E, d=d, t=t, pc=pc: E.activation(
                            xcf[:], pc[:], AF.Silu,
                            bias=convbS[:, d * NDT + t:d * NDT + t + 1]),
                            reads=(kpc, "k_convb"), writes=("m1xcf",))
                        dma("m1xcw", sym(xcd.ap(),
                                         d * DH * L + t * 128 * L + i * TC,
                                         128, L, TC), xcf[:],
                            reads=("m1xcf",), eng="act")
                        em.op("dve", lambda E, t=t: E.tensor_copy(
                            xcb[:, t * TC:(t + 1) * TC], xcf[:]),
                            reads=("m1xcf",), writes=("m1xcb%d" % t,))
                    for t in range(NDT):
                        pb, kpb = p_in[t % 2], "m1pi%d" % (t % 2)
                        for kt in range(NKM):
                            em.op("pe", lambda E, kt=kt, t=t, pb=pb,
                                  n2u=n2use: E.matmul(
                                pb[:], w_in_b[:, kt * 2 * DH + DH + t * 128:
                                              kt * 2 * DH + DH + (t + 1) * 128],
                                n2u[kt][:],
                                start=(kt == 0), stop=(kt == NKM - 1)),
                                reads=wrd + n2rd, writes=(kpb,))
                        em.op("act", lambda E, pb=pb: E.activation(
                            szf[:], pb[:], AF.Silu),
                            reads=(kpb,), writes=("m1szf",))
                        dma("m1szw", sym(szd.ap(),
                                         d * DH * L + t * 128 * L + i * TC,
                                         128, L, TC), szf[:],
                            reads=("m1szf",), eng="act")
                    xcrd = tuple("m1xcb%d" % t for t in range(NDT))
                    for t in range(NDT):
                        em.op("pe", lambda E, t=t: E.matmul(
                            p_x[:], xpw[:, t * 128:(t + 1) * 128],
                            xcb[:, t * TC:(t + 1) * TC],
                            start=(t == 0), stop=(t == NDT - 1)),
                            reads=xprd + xcrd, writes=("m1px",))
                    em.op("act", lambda E: E.activation(xdacc[:], p_x[:], AF.Copy),
                          reads=("m1px",), writes=("m1xda",))

                    dma("k_xdp", sym(xdpart.ap(), d * 128 * L + i * TC,
                                     128, L, TC), xdacc[:],
                        reads=("m1xda",))

        dyn_copy("k_shxw", sh_xdbl.ap(), "xdw", 0, xdpart.ap(), None, 0,
                 2 * 128 * L, ("k_xdp",))
        barrier(1, ["k_shxw"])

        if True:
            em.phase()
            ar.reset()
            xa = ar.alloc((128, 2048))
            xb = ar.alloc((128, 2048))
            dyn_copy("k_xd2", xdpart2.ap(), None, 0, sh_xdbl.ap(), "xdr", 0,
                     2 * 128 * L, ("k_pair1",))
            for d in range(2):
                for i in range(L // 2048):
                    dma("pxa", xa[:], sym(xdpart.ap(), d * 128 * L + i * 2048,
                                          128, L, 2048), reads=("k_xdp",))
                    dma("pxb", xb[:], sym(xdpart2.ap(), d * 128 * L + i * 2048,
                                          128, L, 2048), reads=("k_xd2",))
                    em.op("dve", lambda E: E.tensor_add(xa[:], xa[:], xb[:]),
                          reads=("pxa", "pxb"), writes=("pxa",))
                    dma("pxw", sym(xdbld.ap(), d * 128 * L + i * 2048,
                                   128, L, 2048), xa[:], reads=("pxa",))

        if True:
            em.phase()
            ar.reset()
            wdt_f = ar.alloc((DTR, DH))
            wdt_b = ar.alloc((DTR, DH), BF16)
            wt_b = ar.alloc((128, NDT * DM), BF16)
            xdb = ar.alloc((128, TC))
            dtb16 = ar.alloc((DTR, TC), BF16)
            bc16 = ar.alloc((48, TC), BF16)
            stgB = ar.alloc((1, S * TC), BF16)
            stgC = ar.alloc((1, S * TC), BF16)
            bcB = ar.alloc((128, S * TC), BF16)
            bcC = ar.alloc((128, S * TC), BF16)
            delt = [ar.alloc((128, TC)) for t in range(NDT)]
            dub = [ar.alloc((128, TC), BF16) for t in range(NDT)]
            xct = [ar.alloc((128, TC)) for t in range(NDT)]
            szt = [ar.alloc((128, TC)) for t in range(NDT)]
            Hbuf = ar.alloc((128, S * TC), BF16)
            atmpS = [ar.alloc((128, TC)) for _ in range(2)]
            btmpS = [ar.alloc((128, TC), BF16) for _ in range(2)]
            httmpS = [ar.alloc((128, TC), BF16) for _ in range(2)]
            t1 = ar.alloc((128, TC))
            sptmp = ar.alloc((128, TC))
            ybf = [ar.alloc((128, TC), BF16) for t in range(NDT)]
            potile = ar.alloc((128, TC))
            aotile = ar.alloc((128, TC))
            p_bc = [PB[0], PB[1]]
            p_d = PB[2]
            p_y = [PB[3], PB[4]]
            p_o = [PB[5], PB[6]]

            for d in range(2):
                dma("m2wdtf", wdt_f[:], sym(w_dtT, d * DTR * DH, DTR, DH, DH))
                em.op("dve", lambda E: E.tensor_copy(wdt_b[:], wdt_f[:]),
                      reads=("m2wdtf",), writes=("m2wdt",))
                for kt in range(NDT):
                    dma("m2wt%d" % kt, wt_b[:, kt * DM:(kt + 1) * DM], sym(
                        wmoT.ap(), d * DH * DM + kt * 128 * DM, 128, DM, DM),
                        reads=("pwout",))
                wtrd = tuple("m2wt%d" % kt for kt in range(NDT))
                em.op("dve", lambda E: E.memset(carry[:], 0.0),
                      writes=("m2carry",))

                for i in range(nch):
                    dma("m2xdb", xdb[:], sym(
                        xdbld.ap(), d * 128 * L + i * TC, 128, L, TC),
                        reads=("pxw",))
                    for t in range(NDT):
                        dma("m2xc%d" % t, xct[t][:], sym(
                            xcd.ap(), d * DH * L + t * 128 * L + i * TC,
                            128, L, TC), reads=("m1xcw",))
                        dma("m2szL%d" % t, szt[t][:], sym(
                            szd.ap(), d * DH * L + t * 128 * L + i * TC,
                            128, L, TC), reads=("m1szw",))
                    em.op("dve", lambda E: E.tensor_copy(bc16[:], xdb[0:48, :]),
                          reads=("m2xdb",), writes=("m2bc16",))
                    em.op("sync", lambda E: E.dma_start(
                        out=stgB[:], in_=bc16[0:16, :]),
                        reads=("m2bc16",), writes=("m2stgB",), dma=True)
                    em.op("sync", lambda E: E.dma_start(
                        out=stgC[:], in_=bc16[32:48, :]),
                        reads=("m2bc16",), writes=("m2stgC",), dma=True)
                    for s in range(S):
                        pb, kpb = p_bc[s % 2], "m2pbc%d" % (s % 2)
                        em.op("pe", lambda E, s=s, pb=pb: E.matmul(
                            pb[:], onesb[:], stgB[0:1, s * TC:(s + 1) * TC],
                            start=True, stop=True),
                            reads=("m2stgB", "k_ones"), writes=(kpb,))
                        em.op("act", lambda E, s=s, pb=pb: E.activation(
                            bcB[:, s * TC:(s + 1) * TC], pb[:], AF.Copy),
                            reads=(kpb,), writes=("m2bcB%d" % s,))
                        em.op("pe", lambda E, s=s, pb=pb: E.matmul(
                            pb[:], onesb[:], stgC[0:1, s * TC:(s + 1) * TC],
                            start=True, stop=True),
                            reads=("m2stgC", "k_ones"), writes=(kpb,))
                        em.op("act", lambda E, s=s, pb=pb: E.activation(
                            bcC[:, s * TC:(s + 1) * TC], pb[:], AF.Copy),
                            reads=(kpb,), writes=("m2bcC%d" % s,))
                    em.op("dve", lambda E: E.tensor_copy(dtb16[:], xdb[64:128, :]),
                          reads=("m2xdb",), writes=("m2dt16",))
                    for t in range(NDT):
                        em.op("pe", lambda E, t=t: E.matmul(
                            p_d[:], wdt_b[:, t * 128:(t + 1) * 128], dtb16[:],
                            start=True, stop=True),
                            reads=("m2wdt", "m2dt16"), writes=("m2pd",))
                        em.op("act", lambda E, d=d, t=t: E.activation(
                            sptmp[:], p_d[:], AF.Relu,
                            bias=dtbS[:, d * NDT + t:d * NDT + t + 1]),
                            reads=("m2pd", "k_dtb"), writes=("m2spr",))
                        em.op("act", lambda E, d=d, t=t: E.activation(
                            delt[t][:], p_d[:], AF.Abs,
                            bias=dtbS[:, d * NDT + t:d * NDT + t + 1]),
                            reads=("m2pd", "k_dtb"), writes=("m2del%d" % t,))
                        em.op("act", lambda E, t=t: E.activation(
                            delt[t][:], delt[t][:], AF.Exp, scale=-1.0),
                            reads=("m2del%d" % t,), writes=("m2del%d" % t,))
                        em.op("act", lambda E, t=t: E.activation(
                            delt[t][:], delt[t][:], AF.Ln, bias=1.0),
                            reads=("m2del%d" % t,), writes=("m2del%d" % t,))
                        em.op("pool", lambda E, t=t: E.tensor_add(
                            delt[t][:], delt[t][:], sptmp[:]),
                            reads=("m2spr", "m2del%d" % t),
                            writes=("m2del%d" % t,))
                        em.op("pool", lambda E, t=t: E.tensor_mul(
                            dub[t][:], delt[t][:], xct[t][:]),
                            reads=("m2del%d" % t, "m2xc%d" % t),
                            writes=("m2du%d" % t,))
                    for t in range(NDT):
                        py, kpy = p_y[t % 2], "m2py%d" % (t % 2)
                        for s in range(S):
                            par = s % 2
                            acol = AbarS[:, d * 128 + t * S + s:
                                         d * 128 + t * S + s + 1]
                            em.op("act", lambda E, t=t, acol=acol, par=par:
                                  E.activation(
                                      atmpS[par][:], delt[t][:], AF.Exp,
                                      scale=acol),
                                  reads=("m2del%d" % t, "k_alog"),
                                  writes=("m2atmp%d" % par,))
                            em.op("dve", lambda E, t=t, s=s, par=par:
                                  E.tensor_mul(
                                      btmpS[par][:], dub[t][:],
                                      bcB[:, s * TC:(s + 1) * TC]),
                                  reads=("m2du%d" % t, "m2bcB%d" % s),
                                  writes=("m2bt%d" % par,))
                            em.op("dve", lambda E, t=t, s=s, par=par:
                                  E.tensor_tensor_scan(
                                      Hbuf[:, s * TC:(s + 1) * TC],
                                      atmpS[par][:], btmpS[par][:],
                                      carry[:, t * S + s:t * S + s + 1],
                                      AL.mult, AL.add),
                                  reads=("m2atmp%d" % par, "m2bt%d" % par,
                                         "m2carry"),
                                  writes=("m2H%d" % s,))
                            em.op("pool" if par == 0 else "dve",
                                  lambda E, s=s, par=par: E.tensor_mul(
                                      httmpS[par][:],
                                      Hbuf[:, s * TC:(s + 1) * TC],
                                      bcC[:, s * TC:(s + 1) * TC]),
                                  reads=("m2H%d" % s, "m2bcC%d" % s),
                                  writes=("m2htt%d" % par,))
                            em.op("pe", lambda E, s=s, py=py, par=par: E.matmul(
                                py[:], ident[:], httmpS[par][:],
                                start=(s == 0), stop=(s == S - 1)),
                                reads=("m2htt%d" % par, "k_ident"),
                                writes=(kpy,))
                        em.op("dve", lambda E, t=t: E.tensor_copy(
                            carry[:, t * S:(t + 1) * S], Hbuf[:, TC - 1::TC]),
                            reads=tuple("m2H%d" % s for s in range(S)),
                            writes=("m2carry",))
                        em.op("dve", lambda E, t=t, d=d, py=py:
                              E.scalar_tensor_tensor(
                                  t1[:], xct[t][:],
                                  DS[:, d * NDT + t:d * NDT + t + 1],
                                  py[:], AL.mult, AL.add),
                              reads=("m2xc%d" % t, "k_D", kpy), writes=("m2t1",))
                        em.op("dve", lambda E, t=t: E.tensor_mul(
                            ybf[t][:], t1[:], szt[t][:]),
                            reads=("m2t1", "m2szL%d" % t), writes=("m2y%d" % t,))
                    yrd = tuple("m2y%d" % t for t in range(NDT))
                    if debug and d == 0 and i == 0:
                        for t in range(NDT):
                            em.op("dve", lambda E, t=t: E.tensor_copy(
                                t1[:], ybf[t][:]),
                                reads=("m2y%d" % t,), writes=("m2t1",))
                            dma("dbgy", sym(dbg["d_y"], t * 128 * TC, 128, TC, TC),
                                t1[:], reads=("m2t1",))
                            dma("dbgd", sym(dbg["d_delta"], t * 128 * TC,
                                            128, TC, TC), delt[t][:],
                                reads=("m2del%d" % t,))
                    for dm in range(NKM):
                        po, kpo = p_o[dm % 2], "m2pou%d" % (dm % 2)
                        for kt in range(NDT):
                            em.op("pe", lambda E, dm=dm, kt=kt, po=po: E.matmul(
                                po[:], wt_b[:, kt * DM + dm * 128:
                                            kt * DM + (dm + 1) * 128],
                                ybf[kt][:], start=(kt == 0), stop=(kt == NDT - 1)),
                                reads=wtrd + yrd, writes=(kpo,))
                        if d == 0:
                            em.op("act", lambda E, po=po: E.activation(
                                potile[:], po[:], AF.Copy),
                                reads=(kpo,), writes=("m2pot",))
                            gc = i
                            dma("m2p0_%d_%d" % (gc, dm),
                                sym(partiald.ap(),
                                    gc * DM * TC + dm * 128 * TC, 128, TC, TC),
                                potile[:], reads=("m2pot",), eng="act")
                        else:
                            em.op("act", lambda E, po=po: E.activation(
                                potile[:, ::-1], po[:], AF.Copy),
                                reads=(kpo,), writes=("m2pot",))
                            gc = nch - 1 - i
                            dma("m2ao", aotile[:],
                                sym(partiald.ap(),
                                    gc * DM * TC + dm * 128 * TC, 128, TC, TC),
                                reads=("m2p0_%d_%d" % (gc, dm),))
                            em.op("dve", lambda E: E.tensor_add(
                                potile[:], potile[:], aotile[:]),
                                reads=("m2pot", "m2ao"), writes=("m2pot",))
                            dma("m2pw",
                                sym(partiald.ap(),
                                    NCH * DM * TC + gc * DM * TC +
                                    dm * 128 * TC, 128, TC, TC),
                                potile[:], reads=("m2pot",), eng="act")

        em.phase()
        dyn_copy("k_shpw", sh_part.ap(), "shw0", 0, partiald.ap(), "tokp", 0,
                 16 * DM * TC, ("m2pw",))
        barrier(2, ["k_shpw"])
        dyn_copy("k_pm", pmined.ap(), None, 0, sh_part.ap(), "f1a0", 0,
                 8 * DM * TC, ("k_pair2",))
        dyn_copy("k_pp", ppartd.ap(), None, 0, sh_part.ap(), "f1b0", 0,
                 8 * DM * TC, ("k_pair2",))

        if True:
            em.phase()
            ar.reset()
            faF1 = ar.alloc((128, FBLK))
            fbtF1 = ar.alloc((128, FBLK))
            xmtF1 = ar.alloc((128, DM))
            x1tF1 = ar.alloc((128, DM))
            x1hiF1 = ar.alloc((128, DM), BF16)
            x1loF1 = ar.alloc((128, DM), BF16)
            colAF1 = ar.alloc((128, 8))
            lnscrF1 = ar.alloc((128, DM))
            n1fmF1 = ar.alloc((128, NKM * FBLK))
            ptbF1 = PT

            for blk in range(4096 // FBLK):
                for tt in range(FBLK // 128):
                    i0 = blk * FBLK + tt * 128
                    dma("f1xm", xmtF1[:], sym(x_my, i0 * DM, 128, DM, DM))
                    ln_cols(colAF1, "f1xm", xmtF1, lnscrF1, "f1")
                    em.op("dve", lambda E: E.tensor_scalar(
                        x1tF1[:], xmtF1[:], colAF1[:, 5:6], colAF1[:, 6:7],
                        AL.mult, AL.add),
                        reads=("f1xm", "c5f1", "c6f1"), writes=("f1x1",))
                    em.op("dve", lambda E: E.tensor_copy(x1hiF1[:], x1tF1[:]),
                          reads=("f1x1",), writes=("f1hi",))
                    em.op("dve", lambda E: E.tensor_sub(x1tF1[:], x1tF1[:], x1hiF1[:]),
                          reads=("f1x1", "f1hi"), writes=("f1x1",))
                    em.op("dve", lambda E: E.tensor_copy(x1loF1[:], x1tF1[:]),
                          reads=("f1x1",), writes=("f1lo",))
                    for j in range(NKM):
                        em.op("pe", lambda E, j=j: E.transpose(
                            ptbF1[:], x1hiF1[:, j * 128:(j + 1) * 128], ident[:]),
                            reads=("f1hi", "k_ident"), writes=("f1pt",))
                        em.op("act", lambda E, j=j, tt=tt: E.activation(
                            n1fmF1[:, j * FBLK + tt * 128:j * FBLK + (tt + 1) * 128],
                            ptbF1[:], AF.Copy),
                            reads=("f1pt",), writes=("f1n1_%d_%d" % (j, tt),))
                        em.op("pe", lambda E, j=j: E.transpose(
                            ptbF1[:], x1loF1[:, j * 128:(j + 1) * 128], ident[:]),
                            reads=("f1lo", "k_ident"), writes=("f1pt",))
                        em.op("dve", lambda E, j=j, tt=tt: E.tensor_tensor(
                            n1fmF1[:, j * FBLK + tt * 128:j * FBLK + (tt + 1) * 128],
                            n1fmF1[:, j * FBLK + tt * 128:j * FBLK + (tt + 1) * 128],
                            ptbF1[:], AL.add),
                            reads=("f1pt", "f1n1_%d_%d" % (j, tt)),
                            writes=("f1n1_%d_%d" % (j, tt),))
                n1rd = tuple("f1n1_%d_%d" % (j, tt)
                             for j in range(NKM) for tt in range(FBLK // 128))
                for dm in range(NKM):
                    dma("f1pa", faF1[:], sym(pmined.ap(),
                                           blk * DM * TC + dm * 128 * TC,
                                           128, TC, TC), reads=("k_pm",))
                    dma("f1pb", fbtF1[:], sym(ppartd.ap(),
                                            blk * DM * TC + dm * 128 * TC,
                                            128, TC, TC), reads=("k_pp",))
                    em.op("dve", lambda E: E.tensor_add(faF1[:], faF1[:], fbtF1[:]),
                          reads=("f1pa", "f1pb"), writes=("f1pa",))
                    em.op("act", lambda E, dm=dm: E.activation(
                        faF1[:], faF1[:], AF.Identity, bias=outbS[:, dm:dm + 1]),
                        reads=("f1pa", "k_outb"), writes=("f1pa",))
                    em.op("dve", lambda E, dm=dm: E.tensor_add(
                        faF1[:], faF1[:], n1fmF1[:, dm * FBLK:(dm + 1) * FBLK]),
                        reads=("f1pa",) + n1rd, writes=("f1pa",))
                    dma("f1pb", fbtF1[:], sym(x_myT, dm * 128 * 4096 + blk * FBLK,
                                            128, 4096, FBLK))
                    em.op("dve", lambda E: E.tensor_add(faF1[:], faF1[:], fbtF1[:]),
                          reads=("f1pa", "f1pb"), writes=("f1pa",))
                    dma("f1s2w", sym(s2d.ap(), dm * 128 * 4096 + blk * FBLK,
                                     128, 4096, FBLK), faF1[:], reads=("f1pa",))
                    if debug:
                        dma("dbgs2", sym(dbg["d_stream2"],
                                         dm * 128 * 4096 + blk * FBLK,
                                         128, 4096, FBLK), faF1[:],
                            reads=("f1pa",))

        em.phase()
        ar.reset()
        x3hi = ar.alloc((128, NKM * 4096), BF16)
        x3lo = ar.alloc((128, NKM * 4096), BF16)
        f2a_base = ar.off
        wgd = nc.dram_tensor("wgd", [2, NE, 4096], BF16)
        hd = nc.dram_tensor("hd", [NE + 1, NHT * 128, 4096], BF16)
        oaccd = nc.dram_tensor("oaccd", [DM, 4096], F32)

        if True:
            s2t = [ar.alloc((128, FBLK)) for j in range(NKM)]
            hi_t = ar.alloc((128, FBLK), BF16)
            lo_t = ar.alloc((128, FBLK), BF16)
            sq_t = ar.alloc((128, FBLK))
            rowA = ar.alloc((1, 4 * FBLK))
            rhib = ar.alloc((1, FBLK), BF16)
            rlob = ar.alloc((1, FBLK), BF16)
            mbc = ar.alloc((128, FBLK))
            rbc = ar.alloc((128, FBLK))
            x3f = ar.alloc((128, FBLK))
            gwh = ar.alloc((128, NKM * NE), BF16)
            gwl = ar.alloc((128, NKM * NE), BF16)
            gstg = ar.alloc((128, NE))
            glr = ar.alloc((128, 2 * NE))
            sc_tm = ar.alloc((128, NE))
            bia = ar.alloc((128, NE))
            m1c = ar.alloc((128, 6))
            msk = ar.alloc((128, 3 * NE))
            wgt = ar.alloc((128, NE))
            wgh = ar.alloc((128, NE), BF16)
            wgl = ar.alloc((128, NE), BF16)
            ghtm = ar.alloc((NE, 128), BF16)
            gltm = ar.alloc((NE, 128), BF16)
            p_s = PB[0][0:1, :]
            p_q = PB[1][0:1, :]
            p_b = PB[2]
            p_g = PB[3][0:NE, :]
            p_t = PT

            def bcast_row_exact(dst_key, dst_tile, rowf32_key, rowf32, width):
                em.op("dve", lambda E: E.tensor_copy(rhib[0:1, 0:width], rowf32),
                      reads=(rowf32_key,), writes=("bchi",))
                em.op("dve", lambda E: E.tensor_sub(rowf32, rowf32,
                                                    rhib[0:1, 0:width]),
                      reads=(rowf32_key, "bchi"), writes=(rowf32_key,))
                em.op("dve", lambda E: E.tensor_copy(rlob[0:1, 0:width], rowf32),
                      reads=(rowf32_key,), writes=("bclo",))
                em.op("pe", lambda E: E.matmul(p_b[:, 0:width], onesb[:],
                                               rhib[0:1, 0:width],
                                               start=True, stop=False),
                      reads=("bchi", "k_ones"), writes=("f2pbk",))
                em.op("pe", lambda E: E.matmul(p_b[:, 0:width], onesb[:],
                                               rlob[0:1, 0:width],
                                               start=False, stop=True),
                      reads=("bclo", "k_ones"), writes=("f2pbk",))
                em.op("act", lambda E: E.activation(dst_tile[:, 0:width],
                                                    p_b[:, 0:width], AF.Copy),
                      reads=("f2pbk",), writes=(dst_key,))

            for j in range(NKM):
                dma("f2gstg", gstg[:], sym(gate_wT, j * 128 * NE, 128, NE, NE))
                em.op("dve", lambda E, j=j: E.tensor_copy(
                    gwh[:, j * NE:(j + 1) * NE], gstg[:]),
                    reads=("f2gstg",), writes=("f2gwh",))
                em.op("dve", lambda E, j=j: E.tensor_sub(
                    gstg[:], gstg[:], gwh[:, j * NE:(j + 1) * NE]),
                    reads=("f2gstg", "f2gwh"), writes=("f2gstg",))
                em.op("dve", lambda E, j=j: E.tensor_copy(
                    gwl[:, j * NE:(j + 1) * NE], gstg[:]),
                    reads=("f2gstg",), writes=("f2gwl",))

            for blk in range(4096 // FBLK):
                s2rd = []
                for j in range(NKM):
                    dma("f2s2_%d" % j, s2t[j][:], sym(
                        s2d.ap(), j * 128 * 4096 + blk * FBLK, 128, 4096, FBLK),
                        reads=("f1s2w",))
                    s2rd.append("f2s2_%d" % j)
                s2rd = tuple(s2rd)
                for j in range(NKM):
                    em.op("dve", lambda E, j=j: E.tensor_copy(hi_t[:], s2t[j][:]),
                          reads=(s2rd[j],), writes=("f2hit",))
                    em.op("dve", lambda E, j=j: E.tensor_sub(
                        sq_t[:], s2t[j][:], hi_t[:]),
                        reads=(s2rd[j], "f2hit"), writes=("f2sq",))
                    em.op("dve", lambda E: E.tensor_copy(lo_t[:], sq_t[:]),
                          reads=("f2sq",), writes=("f2lot",))
                    em.op("pe", lambda E, j=j: E.matmul(
                        p_s[:], onescb[:], hi_t[:], start=(j == 0), stop=False),
                        reads=("f2hit", "k_onesc"), writes=("f2psk",))
                    em.op("pe", lambda E, j=j: E.matmul(
                        p_s[:], onescb[:], lo_t[:], start=False,
                        stop=(j == NKM - 1)),
                        reads=("f2lot", "k_onesc"), writes=("f2psk",))
                    em.op("act", lambda E, j=j: E.activation(
                        sq_t[:], s2t[j][:], AF.Square),
                        reads=(s2rd[j],), writes=("f2sq",))
                    em.op("dve", lambda E: E.tensor_copy(hi_t[:], sq_t[:]),
                          reads=("f2sq",), writes=("f2hit",))
                    em.op("dve", lambda E: E.tensor_sub(sq_t[:], sq_t[:], hi_t[:]),
                          reads=("f2sq", "f2hit"), writes=("f2sq",))
                    em.op("dve", lambda E: E.tensor_copy(lo_t[:], sq_t[:]),
                          reads=("f2sq",), writes=("f2lot",))
                    em.op("pe", lambda E, j=j: E.matmul(
                        p_q[:], onescb[:], hi_t[:], start=(j == 0), stop=False),
                        reads=("f2hit", "k_onesc"), writes=("f2pqk",))
                    em.op("pe", lambda E, j=j: E.matmul(
                        p_q[:], onescb[:], lo_t[:], start=False,
                        stop=(j == NKM - 1)),
                        reads=("f2lot", "k_onesc"), writes=("f2pqk",))
                em.op("act", lambda E: E.activation(
                    rowA[0:1, 0:FBLK], p_s[:], AF.Copy, scale=1.0 / DM),
                    reads=("f2psk",), writes=("f2mu",))
                em.op("act", lambda E: E.activation(
                    rowA[0:1, FBLK:2 * FBLK], p_q[:], AF.Copy, scale=1.0 / DM),
                    reads=("f2pqk",), writes=("f2ex",))
                em.op("dve", lambda E: E.tensor_tensor(
                    rowA[0:1, 2 * FBLK:3 * FBLK], rowA[0:1, 0:FBLK],
                    rowA[0:1, 0:FBLK], AL.mult),
                    reads=("f2mu",), writes=("f2mu2",))
                em.op("dve", lambda E: E.tensor_tensor(
                    rowA[0:1, FBLK:2 * FBLK], rowA[0:1, FBLK:2 * FBLK],
                    rowA[0:1, 2 * FBLK:3 * FBLK], AL.subtract),
                    reads=("f2ex", "f2mu2"), writes=("f2ex",))
                em.op("act", lambda E: E.activation(
                    rowA[0:1, 3 * FBLK:4 * FBLK], rowA[0:1, FBLK:2 * FBLK],
                    AF.Ln, bias=EPS),
                    reads=("f2ex",), writes=("f2rr",))
                em.op("act", lambda E: E.activation(
                    rowA[0:1, 3 * FBLK:4 * FBLK], rowA[0:1, 3 * FBLK:4 * FBLK],
                    AF.Exp, scale=-0.5),
                    reads=("f2rr",), writes=("f2rr",))
                bcast_row_exact("f2mbc", mbc, "f2mu", rowA[0:1, 0:FBLK], FBLK)
                bcast_row_exact("f2rbc", rbc, "f2rr",
                                rowA[0:1, 3 * FBLK:4 * FBLK], FBLK)
                for j in range(NKM):
                    em.op("dve", lambda E, j=j: E.tensor_sub(
                        x3f[:], s2t[j][:], mbc[:]),
                        reads=(s2rd[j], "f2mbc"), writes=("f2x3f",))
                    em.op("dve", lambda E: E.tensor_mul(x3f[:], x3f[:], rbc[:]),
                          reads=("f2x3f", "f2rbc"), writes=("f2x3f",))
                    sl_hi = x3hi[:, j * 4096 + blk * FBLK:
                                 j * 4096 + (blk + 1) * FBLK]
                    sl_lo = x3lo[:, j * 4096 + blk * FBLK:
                                 j * 4096 + (blk + 1) * FBLK]
                    em.op("dve", lambda E, sl_hi=sl_hi: E.tensor_copy(
                        sl_hi, x3f[:]),
                        reads=("f2x3f",), writes=("x3hi_%d_%d" % (j, blk),))
                    em.op("dve", lambda E, sl_hi=sl_hi: E.tensor_sub(
                        x3f[:], x3f[:], sl_hi),
                        reads=("f2x3f", "x3hi_%d_%d" % (j, blk)),
                        writes=("f2x3f",))
                    em.op("dve", lambda E, sl_lo=sl_lo: E.tensor_copy(
                        sl_lo, x3f[:]),
                        reads=("f2x3f",), writes=("x3lo_%d_%d" % (j, blk),))
                x3rd = tuple("x3hi_%d_%d" % (j, blk) for j in range(NKM))
                x3rdl = tuple("x3lo_%d_%d" % (j, blk) for j in range(NKM))
                for j in range(NKM):
                    hi_sl = x3hi[:, j * 4096 + blk * FBLK:
                                 j * 4096 + (blk + 1) * FBLK]
                    lo_sl = x3lo[:, j * 4096 + blk * FBLK:
                                 j * 4096 + (blk + 1) * FBLK]
                    em.op("pe", lambda E, j=j, hi_sl=hi_sl: E.matmul(
                        p_g[:], gwh[:, j * NE:(j + 1) * NE], hi_sl,
                        start=(j == 0), stop=False),
                        reads=("f2gwh",) + x3rd, writes=("f2pgk",))
                    em.op("pe", lambda E, j=j, hi_sl=hi_sl: E.matmul(
                        p_g[:], gwl[:, j * NE:(j + 1) * NE], hi_sl,
                        start=False, stop=False),
                        reads=("f2gwl",) + x3rd, writes=("f2pgk",))
                    em.op("pe", lambda E, j=j, lo_sl=lo_sl: E.matmul(
                        p_g[:], gwh[:, j * NE:(j + 1) * NE], lo_sl,
                        start=False, stop=(j == NKM - 1)),
                        reads=("f2gwh",) + x3rdl, writes=("f2pgk",))
                em.op("act", lambda E: E.activation(
                    sq_t[0:NE, :], p_g[:], AF.Copy),
                    reads=("f2pgk",), writes=("f2sq",))
                em.op("dve", lambda E: E.tensor_copy(hi_t[0:NE, :], sq_t[0:NE, :]),
                      reads=("f2sq",), writes=("f2hit",))
                em.op("dve", lambda E: E.tensor_sub(
                    sq_t[0:NE, :], sq_t[0:NE, :], hi_t[0:NE, :]),
                    reads=("f2sq", "f2hit"), writes=("f2sq",))
                em.op("dve", lambda E: E.tensor_copy(lo_t[0:NE, :], sq_t[0:NE, :]),
                      reads=("f2sq",), writes=("f2lot",))
                for tt in range(FBLK // 128):
                    em.op("pe", lambda E, tt=tt: E.transpose(
                        p_t[:, 0:NE], hi_t[0:NE, tt * 128:(tt + 1) * 128],
                        ident[0:NE, 0:NE]),
                        reads=("f2hit", "k_ident"), writes=("f2ptk",))
                    em.op("act", lambda E: E.activation(
                        glr[:, 0:NE], p_t[:, 0:NE], AF.Copy),
                        reads=("f2ptk",), writes=("f2glr0",))
                    em.op("pe", lambda E, tt=tt: E.transpose(
                        p_t[:, 0:NE], lo_t[0:NE, tt * 128:(tt + 1) * 128],
                        ident[0:NE, 0:NE]),
                        reads=("f2lot", "k_ident"), writes=("f2ptk",))
                    em.op("act", lambda E: E.activation(
                        glr[:, NE:2 * NE], p_t[:, 0:NE], AF.Copy),
                        reads=("f2ptk",), writes=("f2glrl",))
                    em.op("dve", lambda E: E.tensor_add(
                        glr[:, 0:NE], glr[:, 0:NE], glr[:, NE:2 * NE]),
                        reads=("f2glr0", "f2glrl"), writes=("f2glr0",))
                    em.op("act", lambda E: E.activation(
                        sc_tm[:], glr[:, 0:NE], AF.Sigmoid),
                        reads=("f2glr0",), writes=("f2sctm",))
                    em.op("dve", lambda E: E.tensor_add(bia[:], sc_tm[:], ebbS[:]),
                          reads=("f2sctm", "k_eb"), writes=("f2bia",))
                    em.op("dve", lambda E: E.reduce_max(
                        m1c[:, 0:1], bia[:], axis=mybir.AxisListType.X),
                        reads=("f2bia",), writes=("f2m1",))
                    em.op("dve", lambda E: E.tensor_scalar(
                        msk[:, 0:NE], bia[:], m1c[:, 0:1], None, AL.is_equal),
                        reads=("f2bia", "f2m1"), writes=("f2msk1",))
                    em.op("dve", lambda E: E.tensor_mul(
                        msk[:, 0:NE], msk[:, 0:NE], p8row[:]),
                        reads=("f2msk1", "k_p8"), writes=("f2msk1",))
                    em.op("dve", lambda E: E.reduce_max(
                        m1c[:, 1:2], msk[:, 0:NE], axis=mybir.AxisListType.X),
                        reads=("f2msk1",), writes=("f2pm1",))
                    em.op("dve", lambda E: E.tensor_scalar(
                        msk[:, 0:NE], msk[:, 0:NE], m1c[:, 1:2], None,
                        AL.is_equal),
                        reads=("f2msk1", "f2pm1"), writes=("f2msk1",))
                    em.op("dve", lambda E: E.scalar_tensor_tensor(
                        msk[:, NE:2 * NE], msk[:, 0:NE], BIGNEG, bia[:],
                        AL.mult, AL.add),
                        reads=("f2msk1", "f2bia"), writes=("f2msk2",))
                    em.op("dve", lambda E: E.reduce_max(
                        m1c[:, 2:3], msk[:, NE:2 * NE], axis=mybir.AxisListType.X),
                        reads=("f2msk2",), writes=("f2m2",))
                    em.op("dve", lambda E: E.tensor_scalar(
                        msk[:, NE:2 * NE], msk[:, NE:2 * NE], m1c[:, 2:3], None,
                        AL.is_equal),
                        reads=("f2msk2", "f2m2"), writes=("f2msk2",))
                    em.op("dve", lambda E: E.tensor_mul(
                        msk[:, NE:2 * NE], msk[:, NE:2 * NE], p8row[:]),
                        reads=("f2msk2", "k_p8"), writes=("f2msk2",))
                    em.op("dve", lambda E: E.reduce_max(
                        m1c[:, 3:4], msk[:, NE:2 * NE], axis=mybir.AxisListType.X),
                        reads=("f2msk2",), writes=("f2pm2",))
                    em.op("dve", lambda E: E.tensor_scalar(
                        msk[:, NE:2 * NE], msk[:, NE:2 * NE], m1c[:, 3:4], None,
                        AL.is_equal),
                        reads=("f2msk2", "f2pm2"), writes=("f2msk2",))
                    em.op("dve", lambda E: E.tensor_add(
                        msk[:, 2 * NE:3 * NE], msk[:, 0:NE], msk[:, NE:2 * NE]),
                        reads=("f2msk1", "f2msk2"), writes=("f2msks",))
                    em.op("dve", lambda E: E.tensor_mul(
                        wgt[:], msk[:, 2 * NE:3 * NE], sc_tm[:]),
                        reads=("f2msks", "f2sctm"), writes=("f2wgt",))
                    em.op("dve", lambda E: E.tensor_mul(wgt[:], wgt[:], sc_tm[:]),
                          reads=("f2wgt", "f2sctm"), writes=("f2wgt",))
                    if debug and blk == 0 and tt == 0:
                        dma("dbgw", dbg["d_wgt"][:, 0:NE], wgt[:],
                            reads=("f2wgt",))
                    em.op("dve", lambda E: E.tensor_copy(wgh[:], wgt[:]),
                          reads=("f2wgt",), writes=("f2wgh",))
                    em.op("dve", lambda E: E.tensor_sub(wgt[:], wgt[:], wgh[:]),
                          reads=("f2wgt", "f2wgh"), writes=("f2wgt",))
                    em.op("dve", lambda E: E.tensor_copy(wgl[:], wgt[:]),
                          reads=("f2wgt",), writes=("f2wgl",))
                    em.op("pe", lambda E: E.transpose(
                        p_t[0:NE, :], wgh[:], ident[:]),
                        reads=("f2wgh", "k_ident"), writes=("f2ptk",))
                    em.op("act", lambda E: E.activation(
                        ghtm[:], p_t[0:NE, :], AF.Copy),
                        reads=("f2ptk",), writes=("f2wth",))
                    em.op("pe", lambda E: E.transpose(
                        p_t[0:NE, :], wgl[:], ident[:]),
                        reads=("f2wgl", "k_ident"), writes=("f2ptk",))
                    em.op("act", lambda E: E.activation(
                        gltm[:], p_t[0:NE, :], AF.Copy),
                        reads=("f2ptk",), writes=("f2wtl",))
                    t0 = blk * FBLK + tt * 128

                    def wsth(E, t0=t0):
                        dst = sym(wgd.ap(), t0, NE, 4096, 128)
                        return E.dma_start(out=dst, in_=ghtm[:])
                    em.op("sync", wsth, reads=("f2wth",),
                          writes=("wgS_%d_%d" % (blk, tt),), dma=True)

                    def wstl(E, t0=t0):
                        dst = sym(wgd.ap(), NE * 4096 + t0, NE, 4096, 128)
                        return E.dma_start(out=dst, in_=gltm[:])
                    em.op("sync", wstl, reads=("f2wtl",),
                          writes=("wgS_%d_%d" % (blk, tt),), dma=True)

        wgkeys = tuple("wgS_%d_%d" % (b_, t_) for b_ in range(4096 // FBLK)
                       for t_ in range(FBLK // 128))
        x3keys = tuple("x3hi_%d_%d" % (j, b_) for j in range(NKM)
                       for b_ in range(4096 // FBLK))

        if True:
            em.phase()
            ar.reset(65536)
            wupP = [ar.alloc((128, NKM * NHT * 128), BF16) for p in range(2)]
            wstgB = ar.alloc((128, NHT * 128))
            hsq = ar.alloc((128, FBLK))
            hbf = ar.alloc((128, FBLK), BF16)
            wgrow = ar.alloc((1, 2 * FBLK), BF16)
            p_h = [PB[0], PB[1]]
            p_w = PB[2]

            for g in range(NE + 1):
                gp = g % 2
                wup = wupP[gp]
                for j in range(NKM):
                    dma("f2wustg", wstgB[:], sym(
                        w_upT, g * DM * NHT * 128 + j * 128 * NHT * 128,
                        128, NHT * 128, NHT * 128))
                    em.op("dve", lambda E, j=j, wup=wup: E.tensor_copy(
                        wup[:, j * NHT * 128:(j + 1) * NHT * 128], wstgB[:]),
                        reads=("f2wustg",), writes=("f2wup%d_%d" % (gp, j),))
                uprd = tuple("f2wup%d_%d" % (gp, j) for j in range(NKM))
                for blk in range(4096 // FBLK):
                    if g < NE:
                        dma("f2wgrow", wgrow[0:1, 0:FBLK], sym(
                            wgd.ap(), g * 4096 + blk * FBLK, 1, 4096, FBLK),
                            reads=wgkeys)
                        dma("f2wgrow2", wgrow[0:1, FBLK:2 * FBLK], sym(
                            wgd.ap(), NE * 4096 + g * 4096 + blk * FBLK,
                            1, 4096, FBLK), reads=wgkeys)
                        em.op("pe", lambda E: E.matmul(
                            p_w[:], onesb[:], wgrow[0:1, 0:FBLK],
                            start=True, stop=False),
                            reads=("f2wgrow", "k_ones"), writes=("f2pwk",))
                        em.op("pe", lambda E: E.matmul(
                            p_w[:], onesb[:], wgrow[0:1, FBLK:2 * FBLK],
                            start=False, stop=True),
                            reads=("f2wgrow2", "k_ones"), writes=("f2pwk",))
                    for ht in range(NHT):
                        ph, kph = p_h[ht % 2], "f2ph%d" % (ht % 2)
                        for j in range(NKM):
                            hi_sl = x3hi[:, j * 4096 + blk * FBLK:
                                         j * 4096 + (blk + 1) * FBLK]
                            em.op("pe", lambda E, ht=ht, j=j, hi_sl=hi_sl,
                                  ph=ph, wup=wup: E.matmul(
                                      ph[:], wup[:, j * NHT * 128 + ht * 128:
                                                  j * NHT * 128 + (ht + 1) * 128],
                                      hi_sl, start=(j == 0), stop=(j == NKM - 1)),
                                  reads=uprd + x3keys, writes=(kph,))
                        em.op("act", lambda E, ph=ph: E.activation(
                            hsq[:], ph[:], AF.Relu),
                            reads=(kph,), writes=("f2hr",))
                        em.op("act", lambda E: E.activation(
                            hsq[:], hsq[:], AF.Square),
                            reads=("f2hr",), writes=("f2hr",))
                        if g < NE:
                            em.op("dve", lambda E: E.tensor_mul(
                                hbf[:], hsq[:], p_w[:]),
                                reads=("f2hr", "f2pwk"), writes=("f2hbf",))
                        else:
                            em.op("dve", lambda E: E.tensor_copy(hbf[:], hsq[:]),
                                  reads=("f2hr",), writes=("f2hbf",))
                        dma("f2hw", sym(hd.ap(),
                                        g * NHT * 128 * 4096 + ht * 128 * 4096 +
                                        blk * FBLK, 128, 4096, FBLK), hbf[:],
                            reads=("f2hbf",))

        em.op("sync", lambda E: E.dma_start(out=oaccd.ap(), in_=s2d.ap()),
              reads=("f1s2w",), writes=("oacc",), dma=True)
        if True:
            em.phase()
            ar.reset()
            wdnP = [ar.alloc((128, NHT * DM), BF16) for p in range(2)]
            wstgC = ar.alloc((128, DM))
            hsbP = [ar.alloc((128, NHT * FBLK), BF16) for p in range(2)]
            foP = [ar.alloc((128, FBLK)) for p in range(2)]
            aoP = [ar.alloc((128, FBLK)) for p in range(2)]
            p_u = [PB[0], PB[1]]

            for g in range(NE + 1):
                gp = g % 2
                wdn = wdnP[gp]
                for kt in range(NHT):
                    dma("f2wdstg", wstgC[:], sym(
                        w_downT, g * NHT * 128 * DM + kt * 128 * DM,
                        128, DM, DM))
                    em.op("dve", lambda E, kt=kt, wdn=wdn: E.tensor_copy(
                        wdn[:, kt * DM:(kt + 1) * DM], wstgC[:]),
                        reads=("f2wdstg",), writes=("f2wdn%d_%d" % (gp, kt),))
                dnrd = tuple("f2wdn%d_%d" % (gp, kt) for kt in range(NHT))
                for blk in range(4096 // FBLK):
                    bp = blk % 2
                    hsb = hsbP[bp]
                    hrd = []
                    for ht in range(NHT):
                        dma("f2hsb%d_%d" % (bp, ht),
                            hsb[:, ht * FBLK:(ht + 1) * FBLK],
                            sym(hd.ap(), g * NHT * 128 * 4096 + ht * 128 * 4096 +
                                blk * FBLK, 128, 4096, FBLK),
                            reads=("f2hw",))
                        hrd.append("f2hsb%d_%d" % (bp, ht))
                    hrd = tuple(hrd)
                    for dm in range(NKM):
                        pu, kpu = p_u[dm % 2], "f2pu%d" % (dm % 2)
                        fo = foP[dm % 2]
                        ao = aoP[dm % 2]
                        kfo = "f2fo%d" % (dm % 2)
                        kao = "f2ao%d" % (dm % 2)
                        for kt in range(NHT):
                            em.op("pe", lambda E, dm=dm, kt=kt, pu=pu, wdn=wdn,
                                  hsb=hsb: E.matmul(
                                      pu[:], wdn[:, kt * DM + dm * 128:
                                                  kt * DM + (dm + 1) * 128],
                                      hsb[:, kt * FBLK:(kt + 1) * FBLK],
                                      start=(kt == 0), stop=(kt == NHT - 1)),
                                  reads=dnrd + hrd, writes=(kpu,))
                        em.op("act", lambda E, pu=pu, fo=fo: E.activation(
                            fo[:], pu[:], AF.Copy),
                            reads=(kpu,), writes=(kfo,))
                        okey = "oacc_%d_%d" % (blk, dm)
                        dma(kao, ao[:], sym(
                            oaccd.ap(), dm * 128 * 4096 + blk * FBLK,
                            128, 4096, FBLK), reads=("oacc", okey))
                        em.op("dve", lambda E, fo=fo, ao=ao: E.tensor_add(
                            fo[:], fo[:], ao[:]),
                            reads=(kfo, kao), writes=(kfo,))
                        if g < NE:
                            dma(okey, sym(oaccd.ap(),
                                          dm * 128 * 4096 + blk * FBLK,
                                          128, 4096, FBLK), fo[:],
                                reads=(kfo,))
                        else:
                            def wout(E, dm=dm, blk=blk, fo=fo):
                                return E.dma_start(
                                    out=sym(out, dm * 128 * 4096 + blk * FBLK,
                                            128, 4096, FBLK), in_=fo[:])
                            em.op("sync", wout, reads=(kfo,),
                                  writes=("kout",), dma=True)


        if debug:
            Vecw = type(wmoT.ap().ap)
            em.op("sync", lambda E: E.dma_start(
                out=dbg["d_wt"],
                in_=dataclasses.replace(wmoT.ap(), offset=0,
                                        ap=Vecw([[1, 2 * DH * DM]]))),
                reads=("pwout",), writes=("dbgx",), dma=True)
            em.op("sync", lambda E: E.dma_start(out=dbg["d_n2"], in_=n2loc.ap()),
                  reads=("k_n2loc",), writes=("dbgx",), dma=True)
            em.op("sync", lambda E: E.dma_start(out=dbg["d_xdbl"],
                                                in_=xdbld.ap()),
                  reads=("pxw",), writes=("dbgx",), dma=True)
            Vecp = type(partiald.ap().ap)
            em.op("sync", lambda E: E.dma_start(
                out=dbg["d_partial"],
                in_=dataclasses.replace(partiald.ap(), offset=NCH * DM * TC,
                                        ap=Vecp([[1, NCH * DM * TC]]))),
                reads=("m2pw",), writes=("dbgx",), dma=True)
            em.op("sync", lambda E: E.dma_start(
                out=dbg["d_xc"],
                in_=dataclasses.replace(xcd.ap(), offset=0,
                                        ap=Vecp([[1, DH * L]]))),
                reads=("m1xcw",), writes=("dbgx",), dma=True)
            em.op("sync", lambda E: E.dma_start(out=dbg["d_pm"],
                                                in_=pmined.ap()),
                  reads=("k_pm",), writes=("dbgx",), dma=True)

        for x in em.names:
            if x in Em.ENGS or em.counts[x] == 0:
                continue
            em.raw("sync", lambda E, s=em.sems[x], c=em.counts[x]:
                   E.wait_ge(s, c))

        with nc.Block() as block:
            block.sync(lambda E: [t(E) for t in em.thunks["sync"]])
            block.scalar(lambda E: [t(E) for t in em.thunks["act"]])
            block.vector(lambda E: [t(E) for t in em.thunks["dve"]])
            block.tensor(lambda E: [t(E) for t in em.thunks["pe"]])
            block.gpsimd(lambda E: [t(E) for t in em.thunks["pool"]])

    return nc, offspec


def _noinc():
    return None


_MAGICS = np.array([[0x5A5A0101, 0x5A5A0202, 0x5A5A0303, 0x5A5A0404,
                     0, 0, 0, 0]], dtype=np.int32)
_NC_CACHE = {}


def _prep_core_inputs(inputs, c):
    f32 = np.float32
    side = c % 2
    k = c // 2
    batch = k % 2
    dup = k // 2
    tok_off = dup * 8192
    myfin = tok_off + side * 4096

    x = np.ascontiguousarray(np.asarray(inputs["L0_attn_x"], dtype=f32))
    d = {}
    d["x_ln"] = np.ascontiguousarray(x[batch, side * 8192:(side + 1) * 8192])
    xm = np.ascontiguousarray(x[batch, myfin:myfin + 4096])
    d["x_my"] = xm
    d["x_myT"] = np.ascontiguousarray(xm.T)

    w_inT = np.zeros((2, DM, 2 * DH), f32)
    conv_diag = np.zeros((2, 4, 128, NDT * 128), f32)
    conv_bias = np.zeros((128, 2 * NDT), f32)
    w_xpT = np.zeros((2, DH, 128), f32)
    w_dtT = np.zeros((2, DTR, DH), f32)
    dt_bias = np.zeros((128, 2 * NDT), f32)
    Alog_r = np.zeros((128, 2 * NDT * S), f32)
    D_r = np.zeros((128, 2 * NDT), f32)
    w_mo = np.zeros((2, DM, DH), f32)
    w_opT = np.zeros((2, DM, DM), f32)
    out_w = np.asarray(inputs["L0_attn_out_w"], dtype=f32)
    idx = np.arange(128)
    for di, tag in enumerate(("f", "b")):
        in_w = np.asarray(inputs["L0_attn_in_w_" + tag], dtype=f32)
        sl = np.concatenate([in_w[side * DH:(side + 1) * DH],
                             in_w[DI + side * DH:DI + (side + 1) * DH]], axis=0)
        w_inT[di] = sl.T
        cw = np.asarray(inputs["L0_attn_conv_w_" + tag], dtype=f32)[:, 0, :]
        cw = cw[side * DH:(side + 1) * DH]
        for tap in range(4):
            for dt in range(NDT):
                conv_diag[di, tap, idx, dt * 128 + idx] = cw[dt * 128 + idx, tap]
        cb = np.asarray(inputs["L0_attn_conv_b_" + tag], dtype=f32)
        conv_bias[:, di * NDT:(di + 1) * NDT] = \
            cb[side * DH:(side + 1) * DH].reshape(NDT, 128).T
        xp = np.asarray(inputs["L0_attn_xproj_w_" + tag], dtype=f32)
        xps = xp[:, side * DH:(side + 1) * DH]
        w_xpT[di][:, 0:16] = xps[64:80].T
        w_xpT[di][:, 32:48] = xps[80:96].T
        w_xpT[di][:, 64:128] = xps[0:64].T
        dw = np.asarray(inputs["L0_attn_dt_w_" + tag], dtype=f32)
        w_dtT[di] = dw[side * DH:(side + 1) * DH].T
        db = np.asarray(inputs["L0_attn_dt_b_" + tag], dtype=f32)
        dt_bias[:, di * NDT:(di + 1) * NDT] = \
            db[side * DH:(side + 1) * DH].reshape(NDT, 128).T
        Al = np.asarray(inputs["L0_attn_A_log_" + tag], dtype=f32)
        Al = Al[side * DH:(side + 1) * DH]
        Alog_r[:, di * 128:(di + 1) * 128] = \
            Al.reshape(NDT, 128, S).transpose(1, 0, 2).reshape(128, NDT * S)
        Dv = np.asarray(inputs["L0_attn_D_" + tag], dtype=f32)
        D_r[:, di * NDT:(di + 1) * NDT] = \
            Dv[side * DH:(side + 1) * DH].reshape(NDT, 128).T
        mo = np.asarray(inputs["L0_attn_mout_w_" + tag], dtype=f32)
        w_mo[di] = mo[:, side * DH:(side + 1) * DH]
        w_opT[di] = out_w[:, di * DM:(di + 1) * DM].T
    d["w_inT"] = w_inT
    d["conv_diag"] = conv_diag
    d["conv_b"] = conv_bias
    d["w_xpT"] = w_xpT
    d["w_dtT"] = w_dtT
    d["dt_b"] = dt_bias
    d["Alog_r"] = Alog_r
    d["D_r"] = D_r
    d["w_mo"] = w_mo
    d["w_opT"] = w_opT
    ob = np.asarray(inputs["L0_attn_out_b"], dtype=f32)
    d["out_bias"] = np.ascontiguousarray(ob.reshape(NKM, 128).T)
    d["gamma"] = np.asarray(inputs["L0_attn_gamma"], dtype=f32).reshape(1, DM)
    d["beta"] = np.asarray(inputs["L0_attn_beta"], dtype=f32).reshape(1, DM)
    d["gate_wT"] = np.ascontiguousarray(
        np.asarray(inputs["L0_ffn_gate_w"], dtype=f32).T)
    d["eb_row"] = np.asarray(
        inputs["L0_ffn_expert_bias"], dtype=f32).reshape(1, NE)
    wu = np.asarray(inputs["L0_ffn_w_up"], dtype=f32)
    wd = np.asarray(inputs["L0_ffn_w_down"], dtype=f32)
    swu = np.asarray(inputs["L0_ffn_shared_w_up"], dtype=f32)
    swd = np.asarray(inputs["L0_ffn_shared_w_down"], dtype=f32)
    w_upT = np.zeros((NE + 1, DM, NHT * 128), f32)
    w_downT = np.zeros((NE + 1, NHT * 128, DM), f32)
    for g in range(NE):
        w_upT[g] = wu[g].T
        w_downT[g] = wd[g].T
    w_upT[NE] = swu.T
    w_downT[NE] = swd.T
    d["w_upT"] = w_upT
    d["w_downT"] = w_downT
    d["cfg"] = np.array([[side, 1 - side, tok_off, side * 4096,
                          0, 0, 0, 0]], dtype=np.int32)
    d["magics"] = _MAGICS
    return d


_BASEVALS = None


def _base_value(name, c):
    side = c % 2
    k = c // 2
    dup = k // 2
    vals = {
        "n2w": side * (DM * HL),
        "xdw": side * (2 * 128 * L),
        "xdr": (1 - side) * (2 * 128 * L),
        "tokp": NCH * DM * TC + dup * 16 * (DM * TC),
        "shw0": side * (16 * DM * TC),
        "f1a0": side * (16 * DM * TC) + side * 8 * (DM * TC),
        "f1b0": (1 - side) * (16 * DM * TC) + side * 8 * (DM * TC),
        "flagw": side * 16,
        "flagr": (1 - side) * 16,
    }
    return vals[name]


def kernel(**inputs):
    if "nc" not in _NC_CACHE:
        _NC_CACHE["nc"], _NC_CACHE["offspec"] = build_nc()
    nc = _NC_CACHE["nc"]
    offspec = _NC_CACHE["offspec"]
    in_maps = [_prep_core_inputs(inputs, c) for c in range(8)]
    for c in range(8):
        tbl = np.zeros((1, 2048), np.int32)
        for idx, (base, const) in enumerate(offspec):
            tbl[0, idx] = _base_value(base, c) + const
        in_maps[c]["offs"] = tbl
    res = run_bass_kernel_spmd(nc, in_maps, core_ids=list(range(8)))
    outp = np.zeros((B, L, DM), np.float32)
    for c in range(8):
        side = c % 2
        k = c // 2
        batch = k % 2
        dup = k // 2
        myfin = dup * 8192 + side * 4096
        outp[batch, myfin:myfin + 4096, :] = res.results[c]["out"].T
    return outp


# version 06 (final)
# speedup vs baseline: 1.0508x; 1.0508x over previous
import dataclasses
import numpy as np
import ml_dtypes
from contextlib import ExitStack

import concourse.bass as bass
import concourse.mybir as mybir
from concourse.bass_utils import run_bass_kernel_spmd

F32 = mybir.dt.float32
BF16 = mybir.dt.bfloat16
I32 = mybir.dt.int32
AL = mybir.AluOpType
AF = mybir.ActivationFunctionType

B = 2
L = 16384
DM = 1024
DI = 2048
DH = 1024
S = 16
DTR = 64
HID = 1408
NE = 8
TC = 512
NCH = L // TC
NDT = DH // 128
NKM = DM // 128
NHT = HID // 128
HL = L // 2
EPS = 1e-5
FBLK = 512
FB2 = 256
BIGNEG = -1.0e30


class Em:

    ENGS = ("sync", "act", "dve", "pe", "pool")
    NSTREAM = {"sync": 4, "act": 2, "pool": 2}

    def __init__(self, nc, ctx):
        self.nc = nc
        self.names = list(self.ENGS)
        for e, n in self.NSTREAM.items():
            self.names += ["%s.%d" % (e, j) for j in range(n)]
        self.sems = {x: ctx.enter_context(
            nc.semaphore("sem_" + x.replace(".", "_"))) for x in self.names}
        self.counts = {x: 0 for x in self.names}
        self.rr = {e: 0 for e in self.NSTREAM}
        self.last_w = {}
        self.readers = {}
        self.marks = {e: {p: 0 for p in self.names} for e in self.ENGS}
        self.thunks = {e: [] for e in self.ENGS}

    def _need(self, eng, dep, needs):
        e2, c2, was_dma = dep
        if self.marks[eng][e2] >= c2:
            return
        needs[e2] = max(needs.get(e2, 0), c2)

    def op(self, eng, fn, reads=(), writes=(), dma=False):
        extra = getattr(self, "act_extra_read", None)
        if extra is not None and eng != "sync" and extra in self.last_w:
            reads = tuple(reads) + (extra,)
        needs = {}
        for k in reads:
            d = self.last_w.get(k)
            if d:
                self._need(eng, d, needs)
        for k in writes:
            d = self.last_w.get(k)
            if d:
                self._need(eng, d, needs)
            for d in self.readers.get(k, ()):
                self._need(eng, d, needs)
        if dma:
            tgt = "%s.%d" % (eng, self.rr[eng])
            self.rr[eng] = (self.rr[eng] + 1) % self.NSTREAM[eng]
            if self.counts[tgt] > 0:
                self._need(eng, (tgt, self.counts[tgt], True), needs)
        else:
            tgt = eng
        for e2, c2 in sorted(needs.items()):
            self.marks[eng][e2] = c2
            sem = self.sems[e2]
            self.thunks[eng].append(lambda E, s=sem, c=c2: E.wait_ge(s, c))
        inc = 16 if dma else 1
        self.counts[tgt] += inc
        sem = self.sems[tgt]

        def run(E, fn=fn, sem=sem, inc=inc):
            fn(E).then_inc(sem, inc)

        self.thunks[eng].append(run)
        tok = (tgt, self.counts[tgt], dma)
        for k in writes:
            self.last_w[k] = tok
            self.readers[k] = []
        for k in reads:
            self.readers.setdefault(k, []).append(tok)

    def raw(self, eng, fn):
        self.thunks[eng].append(fn)

    def phase(self):
        snap = dict(self.counts)
        for e in self.ENGS:
            for p in self.names:
                if p == e or snap[p] == 0 or self.marks[e][p] >= snap[p]:
                    continue
                self.marks[e][p] = snap[p]
                sem, c2 = self.sems[p], snap[p]
                self.thunks[e].append(lambda E, s=sem, c=c2: E.wait_ge(s, c))


class Arena:

    def __init__(self, tensor, nbytes):
        self.t = tensor
        self.n = nbytes
        self.off = 0

    def reset(self, off=0):
        self.off = off

    def alloc(self, shape, dt=F32):
        size = {F32: 4, BF16: 2, I32: 4}[dt]
        nb = shape[1] * size
        off = (self.off + 63) & ~63
        assert off + nb <= self.n, "arena overflow: %d + %d > %d" % (off, nb, self.n)
        self.off = off + nb
        view = self.t[0:shape[0], off:off + nb].bitcast(dt)
        return view


def build_nc(nch=NCH, debug=False):
    nc = bass.Bass("TRN2", target_bir_lowering=False)

    def inp(name, shape, dt=F32):
        return nc.dram_tensor(name, list(shape), dt, kind="ExternalInput").ap()

    x_ln = inp("x_ln", (HL, DM))
    x_my = inp("x_my", (4096, DM))
    x_myT = inp("x_myT", (DM, 4096))
    w_inT = inp("w_inT", (2, DM, 2 * DH))
    conv_diag = inp("conv_diag", (2, 4, 128, NDT * 128))
    conv_b = inp("conv_b", (128, 2 * NDT))
    w_xpT = inp("w_xpT", (2, DH, 128))
    w_dtT = inp("w_dtT", (2, DTR, DH))
    dt_b = inp("dt_b", (128, 2 * NDT))
    Alog_r = inp("Alog_r", (128, 2 * NDT * S))
    D_r = inp("D_r", (128, 2 * NDT))
    w_mo = inp("w_mo", (2, DM, DH))
    w_opT = inp("w_opT", (2, DM, DM))
    out_bias = inp("out_bias", (128, NKM))
    gamma = inp("gamma", (1, DM))
    beta = inp("beta", (1, DM))
    gate_wT = inp("gate_wT", (DM, NE))
    eb_row = inp("eb_row", (1, NE))
    w_upT = inp("w_upT", (NE + 1, DM, NHT * 128))
    w_downT = inp("w_downT", (NE + 1, NHT * 128, DM))
    cfg = inp("cfg", (1, 8), I32)
    magics = inp("magics", (1, 8), I32)
    offs = inp("offs", (1, 2048), I32)

    out = nc.dram_tensor("out", [DM, 4096], F32, kind="ExternalOutput").ap()
    dbg = {}
    if debug:
        for nm, shp, dt_ in [("d_n2", (DM, HL), BF16), ("d_xdbl", (2 * 128, L), F32),
                             ("d_delta", (DH, TC), F32),
                             ("d_partial", (NCH * DM, TC), F32),
                             ("d_stream2", (DM, 4096), F32),
                             ("d_wgt", (128, 4 * NE), F32),
                             ("d_y", (DH, TC), F32), ("d_xc", (DH, L), F32),
                             ("d_pm", (8 * DM, TC), F32),
                             ("d_x0", (128, DM), F32), ("d_x1", (128, DM), F32),
                             ("d_x2", (128, DM), F32), ("d_cols", (128, 8), F32),
                             ("d_xi", (DH, TC), BF16), ("d_n2t", (DM, TC), BF16),
                             ("d_wt", (2 * DH, DM), BF16),
                             ("d_wtb", (128, DM), BF16),
                             ("d_mob", (128, NKM * DH), BF16),
                             ("d_opb", (128, NKM * DM), BF16)]:
            dbg[nm] = nc.dram_tensor(nm, list(shp), dt_, kind="ExternalOutput").ap()

    xcd = nc.dram_tensor("xcd", [2, DH, L], BF16)
    szd = nc.dram_tensor("szd", [2, DH, L], BF16)
    xdbld = nc.dram_tensor("xdbld", [2, 128, L], F32)
    partiald = nc.dram_tensor("partiald", [2, NCH, DM, TC], F32)
    n2loc = nc.dram_tensor("n2loc", [DM, HL], BF16)
    xdpart = nc.dram_tensor("xdpart", [2, 128, L], F32)
    xdpart2 = nc.dram_tensor("xdpart2", [2, 128, L], F32)
    pmined = nc.dram_tensor("pmined", [8, DM, TC], F32)
    ppartd = nc.dram_tensor("ppartd", [8, DM, TC], F32)
    s2d = nc.dram_tensor("s2d", [DM, 4096], F32)
    wmoT = nc.dram_tensor("wmoT", [2, DH, DM], BF16)
    sh_n2 = nc.dram_tensor("sh_n2", [2, DM, HL], BF16, addr_space="Shared")
    sh_xdbl = nc.dram_tensor("sh_xdbl", [2, 2, 128, L], F32, addr_space="Shared")
    sh_part = nc.dram_tensor("sh_part", [2, 16, DM, TC], F32, addr_space="Shared")
    sh_flag = nc.dram_tensor("sh_flag", [2, 16], I32, addr_space="Shared")

    identd = nc.inline_tensor(np.eye(128, dtype=ml_dtypes.bfloat16), "identbf").ap()
    onesd = nc.inline_tensor(np.ones((1, 128), dtype=ml_dtypes.bfloat16), "onesbf").ap()
    onescd = nc.inline_tensor(np.ones((128, 1), dtype=ml_dtypes.bfloat16), "onescbf").ap()
    p8_np = np.broadcast_to(np.arange(NE, 0, -1, dtype=np.float32), (128, NE)).copy()
    p8d = nc.inline_tensor(p8_np, "p8rowd").ap()
    cst_np = np.broadcast_to(
        np.array([EPS, 1.0, -1.0, -0.5, 1.0 / DM, 0.0, 0.5, 2.0],
                 dtype=np.float32), (128, 8)).copy()
    cstd = nc.inline_tensor(cst_np, "cstd").ap()

    def sym(full_ap, offset, rows, rstride, cols):
        Vec = type(full_ap.ap)
        return dataclasses.replace(full_ap, offset=offset,
                                   ap=Vec([[rstride, rows], [1, cols]]))

    with ExitStack() as CTX:
        em = Em(nc, CTX)
        sb = lambda name, shape, dt=F32: CTX.enter_context(
            nc.sbuf_tensor(name, list(shape), dt))

        ident = sb("ident", (128, 128), BF16)
        onesb = sb("onesb", (1, 128), BF16)
        onescb = sb("onescb", (128, 1), BF16)
        p8row = sb("p8row", (128, NE))
        cfgS = sb("cfgS", (1, 8), I32)
        magS = sb("magS", (1, 8), I32)
        spoll = sb("spoll", (1, 4), I32)
        convbS = sb("convbS", (128, 2 * NDT))
        dtbS = sb("dtbS", (128, 2 * NDT))
        AbarS = sb("AbarS", (128, 2 * NDT * S))
        DS = sb("DS", (128, 2 * NDT))
        outbS = sb("outbS", (128, NKM))
        gamS = sb("gamS", (128, DM))
        betS = sb("betS", (128, DM))
        ebbS = sb("ebbS", (128, NE))
        carry = sb("carry", (128, NDT * S))
        arena_t = sb("arena", (128, 191488), mybir.dt.uint8)
        ar = Arena(arena_t, 191488)
        PB = [CTX.enter_context(nc.psum_tensor("pb%d" % i, [128, 512]))
              for i in range(7)]
        PT = CTX.enter_context(nc.psum_tensor("ptb", [128, 128], BF16))

        st = CTX.enter_context(nc.semaphore("st_out"))
        psem = CTX.enter_context(nc.semaphore("psem"))

        offS = sb("offS", (1, 2048), I32)
        cstS = sb("cstS", (128, 8))
        for _ci, _cv in enumerate([EPS, 1.0, -1.0, -0.5, 1.0 / DM, 0.0, 0.5, 2.0]):
            nc.const_aps.register(F32, _cv, cstS[:, _ci:_ci + 1])
        em.act_extra_read = "k_cst"
        EngT = mybir.EngineType
        ENGT = {"sync": EngT.SP, "act": EngT.Activation, "dve": EngT.DVE,
                "pe": EngT.PE, "pool": EngT.Pool}
        r_nd = nc.alloc_register(EngT.SP, "r_nd")
        r_pv = nc.alloc_register(EngT.SP, "r_pv")
        r_pc = nc.alloc_register(EngT.SP, "r_pc")
        r_mg = nc.alloc_register(EngT.SP, "r_mg")
        r_off = {e: nc.alloc_register(ENGT[e], "r_off_" + e)
                 for e in ("sync", "act", "pool")}
        r_of2 = {e: nc.alloc_register(ENGT[e], "r_of2_" + e)
                 for e in ("sync", "act", "pool")}
        offspec = []

        def roff(base, const):
            offspec.append((base, const))
            return len(offspec) - 1

        def dma(key_w, out_ap, in_ap, reads=(), eng="sync"):
            em.op(eng, lambda E: E.dma_start(out=out_ap, in_=in_ap),
                  reads=reads, writes=(key_w,), dma=True)

        dma("k_cst", cstS[:], cstd)
        dma("k_ident", ident[:], identd)
        dma("k_ones", onesb[:], onesd)
        dma("k_onesc", onescb[:], onescd)
        dma("k_p8", p8row[:], p8d)
        dma("k_cfg", cfgS[:], cfg)
        dma("k_mag", magS[:], magics)
        dma("k_convb", convbS[:], conv_b)
        dma("k_dtb", dtbS[:], dt_b)
        dma("k_alog", AbarS[:], Alog_r)
        dma("k_D", DS[:], D_r)
        dma("k_outb", outbS[:], out_bias)
        Vec = type(gamma.ap)
        dma("k_gam", gamS[:],
            dataclasses.replace(gamma, offset=0, ap=Vec([[0, 128], [1, DM]])))
        dma("k_bet", betS[:],
            dataclasses.replace(beta, offset=0, ap=Vec([[0, 128], [1, DM]])))
        dma("k_eb", ebbS[:],
            dataclasses.replace(eb_row, offset=0, ap=Vec([[0, 128], [1, NE]])))

        em.op("act", lambda E: E.activation(AbarS[:], AbarS[:], AF.Exp),
              reads=("k_alog",), writes=("k_alog",))
        em.op("dve", lambda E: E.tensor_scalar_mul(AbarS[:], AbarS[:], -1.0),
              reads=("k_alog",), writes=("k_alog",))

        dma("k_offs", offS[:], offs)
        em.op("sync", lambda E: E.reg_mov(r_pc, 0) and None or E.nop(),
              reads=("k_offs",), writes=("k_regs",))

        def dyn_copy(key_w, dst_full, dst_base, dst_c, src_full, src_base,
                     src_c, nelem, reads,
                     engs=("sync", "act", "pool", "sync")):
            per = nelem // len(engs)
            assert per * len(engs) == nelem
            VecD = type(dst_full.ap)
            VecS = type(src_full.ap)
            for pi, eng in enumerate(engs):
                di = roff(dst_base, dst_c + pi * per) if dst_base else None
                si = roff(src_base, src_c + pi * per) if src_base else None

                def fn(E, di=di, si=si, pi=pi, eng=eng):
                    if di is not None:
                        E.load(r_off[eng], offS[0:1, di:di + 1])
                        do = r_off[eng]
                    else:
                        do = dst_c + pi * per
                    if si is not None:
                        E.load(r_of2[eng], offS[0:1, si:si + 1])
                        so = r_of2[eng]
                    else:
                        so = src_c + pi * per
                    dap = dataclasses.replace(dst_full, offset=do,
                                              ap=VecD([[1, per]]))
                    sap = dataclasses.replace(src_full, offset=so,
                                              ap=VecS([[1, per]]))
                    return E.dma_start(out=dap, in_=sap)
                em.op(eng, fn, reads=tuple(reads) + ("k_offs",),
                      writes=(key_w,), dma=True)

        def barrier(phase, data_keys):
            fl = sh_flag.ap()
            Vec = type(fl.ap)
            idx1 = roff("flagw", phase)

            def wrf(E, idx1=idx1):
                E.load(r_off["sync"], offS[0:1, idx1:idx1 + 1])
                myf = dataclasses.replace(fl, offset=r_off["sync"],
                                          ap=Vec([[1, 1]]))
                return E.dma_start(out=myf, in_=magS[0:1, phase:phase + 1])
            em.op("sync", wrf, reads=tuple(data_keys) + ("k_mag", "k_offs"),
                  writes=("k_flg%d" % phase,), dma=True)
            idx2 = roff("flagr", phase)

            def poll(E, idx2=idx2):
                E.load(r_of2["sync"], offS[0:1, idx2:idx2 + 1])
                otf = dataclasses.replace(fl, offset=r_of2["sync"],
                                          ap=Vec([[1, 1]]))
                E.load(r_mg, magS[0:1, phase:phase + 1])
                E.reg_mov(r_nd, 1)
                with E.While(r_nd):
                    E.dma_start(out=spoll[0:1, 0:1], in_=otf).then_inc(psem, 16)
                    E.reg_add(r_pc, r_pc, 16)
                    E.wait_ge(psem, r_pc)
                    E.load(r_pv, spoll[0:1, 0:1])
                    E.reg_alu(r_nd, r_pv, r_mg, AL.bitwise_xor)
            em.raw("sync", poll)
            em.op("sync", lambda E, phase=phase: E.dma_start(
                out=spoll[0:1, 1:2], in_=magS[0:1, phase:phase + 1]),
                reads=(), writes=("k_pair%d" % phase,), dma=True)

        def ln_cols(colA, xt_key, xtile, scr, dst_key_sfx):
            em.op("act", lambda E: E.activation(
                scr[:, 0:DM], xtile[:, 0:DM], AF.Square, accum_out=colA[:, 0:1]),
                reads=(xt_key,), writes=("c0" + dst_key_sfx, "cjunk"))
            em.op("act", lambda E: E.activation(
                scr[:, 0:DM], xtile[:, 0:DM], AF.Identity, accum_out=colA[:, 1:2]),
                reads=(xt_key,), writes=("c1" + dst_key_sfx, "cjunk"))
            em.op("dve", lambda E: E.tensor_scalar_mul(
                colA[:, 2:3], colA[:, 1:2], 1.0 / DM),
                reads=("c1" + dst_key_sfx,), writes=("c2" + dst_key_sfx,))
            em.op("dve", lambda E: E.tensor_tensor(
                colA[:, 3:4], colA[:, 2:3], colA[:, 2:3], AL.mult),
                reads=("c2" + dst_key_sfx,), writes=("c3" + dst_key_sfx,))
            em.op("dve", lambda E: E.tensor_scalar(
                colA[:, 4:5], colA[:, 0:1], 1.0 / DM, None, AL.mult),
                reads=("c0" + dst_key_sfx,), writes=("c4" + dst_key_sfx,))
            em.op("dve", lambda E: E.tensor_tensor(
                colA[:, 4:5], colA[:, 4:5], colA[:, 3:4], AL.subtract),
                reads=("c3" + dst_key_sfx, "c4" + dst_key_sfx),
                writes=("c4" + dst_key_sfx,))
            em.op("act", lambda E: E.activation(
                colA[:, 5:6], colA[:, 4:5], AF.Ln, bias=EPS),
                reads=("c4" + dst_key_sfx,), writes=("c5" + dst_key_sfx,))
            em.op("act", lambda E: E.activation(
                colA[:, 5:6], colA[:, 5:6], AF.Exp, scale=-0.5),
                reads=("c5" + dst_key_sfx,), writes=("c5" + dst_key_sfx,))
            em.op("dve", lambda E: E.tensor_tensor(
                colA[:, 6:7], colA[:, 2:3], colA[:, 5:6], AL.mult),
                reads=("c2" + dst_key_sfx, "c5" + dst_key_sfx),
                writes=("c6" + dst_key_sfx,))
            em.op("dve", lambda E: E.tensor_scalar_mul(
                colA[:, 6:7], colA[:, 6:7], -1.0),
                reads=("c6" + dst_key_sfx,), writes=("c6" + dst_key_sfx,))

        if True:
            ar.reset()
            xt = ar.alloc((128, DM))
            x1t = ar.alloc((128, DM))
            n2t = ar.alloc((128, DM), BF16)
            n2fm = ar.alloc((128, DM), BF16)
            colA = ar.alloc((128, 8))
            lnscr = ar.alloc((128, DM))
            pt = PT

            for i in range(HL // 128):
                dma("p0x", xt[:], sym(x_ln, i * 128 * DM, 128, DM, DM))
                if debug and i == 0:
                    dma("dbg0", dbg["d_x0"], xt[:], reads=("p0x",))
                ln_cols(colA, "p0x", xt, lnscr, "p0")
                if debug and i == 0:
                    dma("dbg1b", dbg["d_cols"], colA[:],
                        reads=tuple("c%dp0" % q for q in range(7)))
                em.op("dve", lambda E: E.tensor_scalar(
                    x1t[:], xt[:], colA[:, 5:6], colA[:, 6:7], AL.mult, AL.add),
                    reads=("p0x", "c5p0", "c6p0"), writes=("p0x1",))
                if debug and i == 0:
                    dma("dbg1", dbg["d_x1"], x1t[:], reads=("p0x1",))
                ln_cols(colA, "p0x1", x1t, lnscr, "p0")
                em.op("dve", lambda E: E.tensor_scalar(
                    x1t[:], x1t[:], colA[:, 5:6], colA[:, 6:7], AL.mult, AL.add),
                    reads=("p0x1", "c5p0", "c6p0"), writes=("p0x1",))
                em.op("dve", lambda E: E.tensor_tensor(
                    x1t[:], x1t[:], gamS[:], AL.mult),
                    reads=("p0x1", "k_gam"), writes=("p0x1",))
                em.op("dve", lambda E: E.tensor_tensor(
                    n2t[:], x1t[:], betS[:], AL.add),
                    reads=("p0x1", "k_bet"), writes=("p0n2",))
                if debug and i == 0:
                    dma("dbg2", dbg["d_x2"], x1t[:], reads=("p0x1",))
                for j in range(NKM):
                    em.op("pe", lambda E, j=j: E.transpose(
                        pt[:], n2t[:, j * 128:(j + 1) * 128], ident[:]),
                        reads=("p0n2", "k_ident"), writes=("p0pt",))
                    em.op("act", lambda E, j=j: E.activation(
                        n2fm[:, j * 128:(j + 1) * 128], pt[:], AF.Copy),
                        reads=("p0pt",), writes=("p0fm%d" % j,))
                for j in range(NKM):
                    dma("k_n2loc", sym(n2loc.ap(), (j * 128) * HL + i * 128,
                                       128, HL, 128),
                        n2fm[:, j * 128:(j + 1) * 128],
                        reads=("p0fm%d" % j,))

        dyn_copy("k_shn2w", sh_n2.ap(), "n2w", 0, n2loc.ap(), None, 0,
                 DM * HL, ("k_n2loc",))
        barrier(0, ["k_shn2w"])

        if True:
            em.phase()
            ar.reset()
            stg = ar.alloc((128, DM))
            mo_b = ar.alloc((128, NKM * DH), BF16)
            op_b = ar.alloc((128, NKM * DM), BF16)
            wtPW = ar.alloc((128, DM), BF16)
            pw = PB[0]
            for d in range(2):
                for m in range(NKM):
                    dma("pwst", stg[:, 0:DH], sym(
                        w_mo, d * DM * DH + m * 128 * DH, 128, DH, DH))
                    em.op("dve", lambda E, m=m: E.tensor_copy(
                        mo_b[:, m * DH:(m + 1) * DH], stg[:, 0:DH]),
                        reads=("pwst",), writes=("pwmo%d" % m,))
                    dma("pwst", stg[:], sym(
                        w_opT, d * DM * DM + m * 128 * DM, 128, DM, DM))
                    em.op("dve", lambda E, m=m: E.tensor_copy(
                        op_b[:, m * DM:(m + 1) * DM], stg[:]),
                        reads=("pwst",), writes=("pwop%d" % m,))
                rd = tuple("pwmo%d" % m for m in range(NKM)) + \
                    tuple("pwop%d" % m for m in range(NKM))
                for kt in range(NDT):
                    for half in range(2):
                        for m in range(NKM):
                            em.op("pe", lambda E, kt=kt, half=half, m=m: E.matmul(
                                pw[:],
                                mo_b[:, m * DH + kt * 128:m * DH + (kt + 1) * 128],
                                op_b[:, m * DM + half * 512:m * DM + (half + 1) * 512],
                                start=(m == 0), stop=(m == NKM - 1)),
                                reads=rd, writes=("pwps",))
                        em.op("act", lambda E, half=half: E.activation(
                            wtPW[:, half * 512:(half + 1) * 512], pw[:], AF.Copy),
                            reads=("pwps",), writes=("pwwt",))
                    dma("pwout", sym(wmoT.ap(), d * DH * DM + kt * 128 * DM,
                                     128, DM, DM), wtPW[:], reads=("pwwt",))
            if debug:
                dma("dbgwtb", dbg["d_wtb"], wtPW[:], reads=("pwwt",))
                dma("dbgmob", dbg["d_mob"], mo_b[:],
                    reads=tuple("pwmo%d" % m for m in range(NKM)))
                dma("dbgopb", dbg["d_opb"], op_b[:],
                    reads=tuple("pwop%d" % m for m in range(NKM)))

        if True:
            em.phase()
            ar.reset()
            w_in_b = ar.alloc((128, NKM * 2 * DH), BF16)
            wstg = ar.alloc((128, 2 * DH))
            cdg = ar.alloc((128, 4 * NDT * 128), BF16)
            cstg = ar.alloc((128, NDT * 128))
            xpw = ar.alloc((128, NDT * 128), BF16)
            xpstg = ar.alloc((128, 128))
            n2bP = [[ar.alloc((128, TC), BF16) for k in range(NKM)]
                    for p in range(2)]
            n2rP = [[ar.alloc((128, TC), BF16) for k in range(NKM)]
                    for p in range(2)]
            xib = [[ar.alloc((128, TC + 4), BF16)
                    for t in range(NDT)] for p in range(2)]
            xcf = ar.alloc((128, TC), BF16)
            xcb = ar.alloc((128, NDT * TC), BF16)
            szf = ar.alloc((128, TC), BF16)
            xdacc = ar.alloc((128, TC))
            p_in = [PB[0], PB[1]]
            p_c = [PB[2], PB[3]]
            p_x = PB[4]

            for d in range(2):
                for kt in range(NKM):
                    dma("m1wstg", wstg[:], sym(
                        w_inT, d * DM * 2 * DH + kt * 128 * 2 * DH, 128,
                        2 * DH, 2 * DH))
                    em.op("dve", lambda E, kt=kt: E.tensor_copy(
                        w_in_b[:, kt * 2 * DH:(kt + 1) * 2 * DH], wstg[:]),
                        reads=("m1wstg",), writes=("m1wb%d" % kt,))
                for tap in range(4):
                    dma("m1cstg", cstg[:], sym(
                        conv_diag, (d * 4 + tap) * 128 * NDT * 128, 128,
                        NDT * 128, NDT * 128))
                    em.op("dve", lambda E, tap=tap: E.tensor_copy(
                        cdg[:, tap * NDT * 128:(tap + 1) * NDT * 128], cstg[:]),
                        reads=("m1cstg",), writes=("m1cdg%d" % tap,))
                for t in range(NDT):
                    dma("m1xpstg", xpstg[:], sym(
                        w_xpT, d * DH * 128 + t * 128 * 128, 128, 128, 128))
                    em.op("dve", lambda E, t=t: E.tensor_copy(
                        xpw[:, t * 128:(t + 1) * 128], xpstg[:]),
                        reads=("m1xpstg",), writes=("m1xpw%d" % t,))
                wrd = tuple("m1wb%d" % kt for kt in range(NKM))
                crd = tuple("m1cdg%d" % t for t in range(4))
                xprd = tuple("m1xpw%d" % t for t in range(NDT))

                for i in range(nch):
                    g = i if d == 0 else (nch - 1 - i)
                    par = i % 2
                    slot = g // (HL // TC)
                    col = (g % (HL // TC)) * TC
                    n2b = n2bP[par]
                    n2r = n2rP[par]
                    n2rd = []
                    for kt in range(NKM):
                        off = slot * DM * HL + kt * 128 * HL + col
                        dma("m1n2_%d_%d" % (par, kt), n2b[kt][:],
                            sym(sh_n2.ap(), off, 128, HL, TC),
                            reads=("k_pair0",))
                        if d == 1:
                            em.op("dve", lambda E, kt=kt, n2b=n2b, n2r=n2r:
                                  E.tensor_copy(
                                      n2r[kt][:], n2b[kt][:, ::-1]),
                                  reads=("m1n2_%d_%d" % (par, kt),),
                                  writes=("m1n2r%d_%d" % (par, kt),))
                            n2rd.append("m1n2r%d_%d" % (par, kt))
                        else:
                            n2rd.append("m1n2_%d_%d" % (par, kt))
                    n2use = n2r if d == 1 else n2b
                    n2rd = tuple(n2rd)
                    if debug and d == 0 and i == 0:
                        for kt in range(NKM):
                            dma("dn2t", sym(dbg["d_n2t"], kt * 128 * TC,
                                            128, TC, TC), n2b[kt][:],
                                reads=("m1n2_%d_%d" % (par, kt),))

                    for t in range(NDT):
                        pb, kpb = p_in[t % 2], "m1pi%d" % (t % 2)
                        for kt in range(NKM):
                            em.op("pe", lambda E, kt=kt, t=t, pb=pb,
                                  n2u=n2use: E.matmul(
                                pb[:], w_in_b[:, kt * 2 * DH + t * 128:
                                              kt * 2 * DH + (t + 1) * 128],
                                n2u[kt][:],
                                start=(kt == 0), stop=(kt == NKM - 1)),
                                reads=wrd + n2rd, writes=(kpb,))
                        if i == 0:
                            em.op("dve", lambda E, par=par, t=t: E.memset(
                                xib[par][t][:, 1:4], 0.0),
                                writes=("m1hal%d_%d" % (par, t),))
                        else:
                            em.op("dve", lambda E, par=par, t=t: E.tensor_copy(
                                xib[par][t][:, 1:4],
                                xib[1 - par][t][:, TC + 1:TC + 4]),
                                reads=("m1xib%d_%d" % (1 - par, t),),
                                writes=("m1hal%d_%d" % (par, t),))
                        em.op("act", lambda E, par=par, t=t, pb=pb: E.activation(
                            xib[par][t][:, 4:TC + 4], pb[:], AF.Copy),
                            reads=(kpb,), writes=("m1xib%d_%d" % (par, t),))
                        if debug and d == 0 and i == 0:
                            dma("dxi", sym(dbg["d_xi"], t * 128 * TC, 128, TC, TC),
                                xib[par][t][:, 4:TC + 4],
                                reads=("m1xib%d_%d" % (par, t),))
                        pc, kpc = p_c[t % 2], "m1pc%d" % (t % 2)
                        for tap in range(4):
                            em.op("pe", lambda E, tap=tap, t=t, par=par, pc=pc:
                                  E.matmul(
                                      pc[:],
                                      cdg[:, (tap * NDT + t) * 128:
                                          (tap * NDT + t + 1) * 128],
                                      xib[par][t][:, 1 + tap:1 + tap + TC],
                                      start=(tap == 0), stop=(tap == 3)),
                                  reads=crd + ("m1xib%d_%d" % (par, t),
                                               "m1hal%d_%d" % (par, t)),
                                  writes=(kpc,))
                        em.op("act", lambda E, d=d, t=t, pc=pc: E.activation(
                            xcf[:], pc[:], AF.Silu,
                            bias=convbS[:, d * NDT + t:d * NDT + t + 1]),
                            reads=(kpc, "k_convb"), writes=("m1xcf",))
                        dma("m1xcw", sym(xcd.ap(),
                                         d * DH * L + t * 128 * L + i * TC,
                                         128, L, TC), xcf[:],
                            reads=("m1xcf",), eng="act")
                        em.op("dve", lambda E, t=t: E.tensor_copy(
                            xcb[:, t * TC:(t + 1) * TC], xcf[:]),
                            reads=("m1xcf",), writes=("m1xcb%d" % t,))
                    for t in range(NDT):
                        pb, kpb = p_in[t % 2], "m1pi%d" % (t % 2)
                        for kt in range(NKM):
                            em.op("pe", lambda E, kt=kt, t=t, pb=pb,
                                  n2u=n2use: E.matmul(
                                pb[:], w_in_b[:, kt * 2 * DH + DH + t * 128:
                                              kt * 2 * DH + DH + (t + 1) * 128],
                                n2u[kt][:],
                                start=(kt == 0), stop=(kt == NKM - 1)),
                                reads=wrd + n2rd, writes=(kpb,))
                        em.op("act", lambda E, pb=pb: E.activation(
                            szf[:], pb[:], AF.Silu),
                            reads=(kpb,), writes=("m1szf",))
                        dma("m1szw", sym(szd.ap(),
                                         d * DH * L + t * 128 * L + i * TC,
                                         128, L, TC), szf[:],
                            reads=("m1szf",), eng="act")
                    xcrd = tuple("m1xcb%d" % t for t in range(NDT))
                    for t in range(NDT):
                        em.op("pe", lambda E, t=t: E.matmul(
                            p_x[:], xpw[:, t * 128:(t + 1) * 128],
                            xcb[:, t * TC:(t + 1) * TC],
                            start=(t == 0), stop=(t == NDT - 1)),
                            reads=xprd + xcrd, writes=("m1px",))
                    em.op("act", lambda E: E.activation(xdacc[:], p_x[:], AF.Copy),
                          reads=("m1px",), writes=("m1xda",))

                    dma("k_xdp", sym(xdpart.ap(), d * 128 * L + i * TC,
                                     128, L, TC), xdacc[:],
                        reads=("m1xda",))

        dyn_copy("k_shxw", sh_xdbl.ap(), "xdw", 0, xdpart.ap(), None, 0,
                 2 * 128 * L, ("k_xdp",))
        barrier(1, ["k_shxw"])

        if True:
            em.phase()
            ar.reset()
            xa = ar.alloc((128, 2048))
            xb = ar.alloc((128, 2048))
            dyn_copy("k_xd2", xdpart2.ap(), None, 0, sh_xdbl.ap(), "xdr", 0,
                     2 * 128 * L, ("k_pair1",))
            for d in range(2):
                for i in range(L // 2048):
                    dma("pxa", xa[:], sym(xdpart.ap(), d * 128 * L + i * 2048,
                                          128, L, 2048), reads=("k_xdp",))
                    dma("pxb", xb[:], sym(xdpart2.ap(), d * 128 * L + i * 2048,
                                          128, L, 2048), reads=("k_xd2",))
                    em.op("dve", lambda E: E.tensor_add(xa[:], xa[:], xb[:]),
                          reads=("pxa", "pxb"), writes=("pxa",))
                    dma("pxw", sym(xdbld.ap(), d * 128 * L + i * 2048,
                                   128, L, 2048), xa[:], reads=("pxa",))

        if True:
            em.phase()
            ar.reset()
            wdt_f = ar.alloc((DTR, DH))
            wdt_b = ar.alloc((DTR, DH), BF16)
            wt_b = ar.alloc((128, NDT * DM), BF16)
            xdb = ar.alloc((128, TC))
            dtb16P = [ar.alloc((DTR, TC), BF16) for p in range(2)]
            bc16 = ar.alloc((48, TC), BF16)
            stgB = ar.alloc((1, S * TC), BF16)
            stgC = ar.alloc((1, S * TC), BF16)
            bcBP = [ar.alloc((128, S * TC), BF16) for p in range(2)]
            bcCP = [ar.alloc((128, S * TC), BF16) for p in range(2)]
            deltP = [ar.alloc((128, TC)) for p in range(2)]
            dubP = [ar.alloc((128, TC), BF16) for p in range(2)]
            xct = [ar.alloc((128, TC), BF16) for t in range(NDT)]
            szt = [ar.alloc((128, TC), BF16) for t in range(NDT)]
            Hbuf = ar.alloc((128, S * TC), BF16)
            atmpS = [ar.alloc((128, TC)) for _ in range(2)]
            btmpS = [ar.alloc((128, TC), BF16) for _ in range(2)]
            httmpS = [ar.alloc((128, TC), BF16) for _ in range(2)]
            t1 = ar.alloc((128, TC))
            sptmp = ar.alloc((128, TC))
            ybf = [ar.alloc((128, TC), BF16) for t in range(NDT)]
            potile = ar.alloc((128, TC))
            aotile = ar.alloc((128, TC))
            p_bc = [PB[0], PB[1]]
            p_d = PB[2]
            p_y = [PB[3], PB[4]]
            p_o = [PB[5], PB[6]]

            def m2_stage(d, i):
                q = d * nch + i
                par = q % 2
                bcB, bcC, dtb16 = bcBP[par], bcCP[par], dtb16P[par]
                dma("m2xdb", xdb[:], sym(
                    xdbld.ap(), d * 128 * L + i * TC, 128, L, TC),
                    reads=("pxw",))
                em.op("pool", lambda E: E.tensor_copy(bc16[:], xdb[0:48, :]),
                      reads=("m2xdb",), writes=("m2bc16",))
                em.op("dve", lambda E, dtb16=dtb16: E.tensor_copy(
                    dtb16[:], xdb[64:128, :]),
                    reads=("m2xdb",), writes=("m2dt16_%d" % par,))
                em.op("sync", lambda E: E.dma_start(
                    out=stgB[:], in_=bc16[0:16, :]),
                    reads=("m2bc16",), writes=("m2stgB",), dma=True)
                em.op("sync", lambda E: E.dma_start(
                    out=stgC[:], in_=bc16[32:48, :]),
                    reads=("m2bc16",), writes=("m2stgC",), dma=True)
                for s in range(S):
                    pb, kpb = p_bc[s % 2], "m2pbc%d" % (s % 2)
                    em.op("pe", lambda E, s=s, pb=pb: E.matmul(
                        pb[:], onesb[:], stgB[0:1, s * TC:(s + 1) * TC],
                        start=True, stop=True),
                        reads=("m2stgB", "k_ones"), writes=(kpb,))
                    em.op("act", lambda E, s=s, pb=pb, bcB=bcB: E.activation(
                        bcB[:, s * TC:(s + 1) * TC], pb[:], AF.Copy),
                        reads=(kpb,), writes=("m2bcB%d_%d" % (par, s),))
                    em.op("pe", lambda E, s=s, pb=pb: E.matmul(
                        pb[:], onesb[:], stgC[0:1, s * TC:(s + 1) * TC],
                        start=True, stop=True),
                        reads=("m2stgC", "k_ones"), writes=(kpb,))
                    em.op("act", lambda E, s=s, pb=pb, bcC=bcC: E.activation(
                        bcC[:, s * TC:(s + 1) * TC], pb[:], AF.Copy),
                        reads=(kpb,), writes=("m2bcC%d_%d" % (par, s),))

            for d in range(2):
                dma("m2wdtf", wdt_f[:], sym(w_dtT, d * DTR * DH, DTR, DH, DH))
                em.op("dve", lambda E: E.tensor_copy(wdt_b[:], wdt_f[:]),
                      reads=("m2wdtf",), writes=("m2wdt",))
                for kt in range(NDT):
                    dma("m2wt%d" % kt, wt_b[:, kt * DM:(kt + 1) * DM], sym(
                        wmoT.ap(), d * DH * DM + kt * 128 * DM, 128, DM, DM),
                        reads=("pwout",))
                wtrd = tuple("m2wt%d" % kt for kt in range(NDT))
                em.op("dve", lambda E: E.memset(carry[:], 0.0),
                      writes=("m2carry",))
                if d == 0:
                    m2_stage(0, 0)

                for i in range(nch):
                    q = d * nch + i
                    cpar = q % 2
                    bcB, bcC, dtb16 = bcBP[cpar], bcCP[cpar], dtb16P[cpar]
                    for t in range(NDT):
                        dma("m2xc%d" % t, xct[t][:], sym(
                            xcd.ap(), d * DH * L + t * 128 * L + i * TC,
                            128, L, TC), reads=("m1xcw",))
                        dma("m2szL%d" % t, szt[t][:], sym(
                            szd.ap(), d * DH * L + t * 128 * L + i * TC,
                            128, L, TC), reads=("m1szw",))

                    def sp(t, d=d, dtb16=dtb16):
                        tp = t % 2
                        em.op("pe", lambda E, t=t, dtb16=dtb16: E.matmul(
                            p_d[:], wdt_b[:, t * 128:(t + 1) * 128], dtb16[:],
                            start=True, stop=True),
                            reads=("m2wdt", "m2dt16_%d" % cpar),
                            writes=("m2pd",))
                        em.op("act", lambda E, d=d, t=t: E.activation(
                            sptmp[:], p_d[:], AF.Relu,
                            bias=dtbS[:, d * NDT + t:d * NDT + t + 1]),
                            reads=("m2pd", "k_dtb"), writes=("m2spr",))
                        em.op("act", lambda E, d=d, t=t, tp=tp: E.activation(
                            deltP[tp][:], p_d[:], AF.Abs,
                            bias=dtbS[:, d * NDT + t:d * NDT + t + 1]),
                            reads=("m2pd", "k_dtb"), writes=("m2del%d" % tp,))
                        em.op("act", lambda E, tp=tp: E.activation(
                            deltP[tp][:], deltP[tp][:], AF.Exp, scale=-1.0),
                            reads=("m2del%d" % tp,), writes=("m2del%d" % tp,))
                        em.op("act", lambda E, tp=tp: E.activation(
                            deltP[tp][:], deltP[tp][:], AF.Ln, bias=1.0),
                            reads=("m2del%d" % tp,), writes=("m2del%d" % tp,))
                        em.op("pool", lambda E, tp=tp: E.tensor_add(
                            deltP[tp][:], deltP[tp][:], sptmp[:]),
                            reads=("m2spr", "m2del%d" % tp),
                            writes=("m2del%d" % tp,))
                        em.op("pool", lambda E, t=t, tp=tp: E.tensor_mul(
                            dubP[tp][:], deltP[tp][:], xct[t][:]),
                            reads=("m2del%d" % tp, "m2xc%d" % t),
                            writes=("m2du%d" % tp,))

                    sp(0)
                    sp(1)
                    if q + 1 < 2 * nch:
                        m2_stage((q + 1) // nch, (q + 1) % nch)

                    for t in range(NDT):
                        tp = t % 2
                        py, kpy = p_y[t % 2], "m2py%d" % (t % 2)
                        for s in range(S):
                            par = s % 2
                            acol = AbarS[:, d * 128 + t * S + s:
                                         d * 128 + t * S + s + 1]
                            em.op("act", lambda E, tp=tp, acol=acol, par=par:
                                  E.activation(
                                      atmpS[par][:], deltP[tp][:], AF.Exp,
                                      scale=acol),
                                  reads=("m2del%d" % tp, "k_alog"),
                                  writes=("m2atmp%d" % par,))
                            em.op("dve", lambda E, tp=tp, s=s, par=par,
                                  bcB=bcB: E.tensor_mul(
                                      btmpS[par][:], dubP[tp][:],
                                      bcB[:, s * TC:(s + 1) * TC]),
                                  reads=("m2du%d" % tp,
                                         "m2bcB%d_%d" % (cpar, s)),
                                  writes=("m2bt%d" % par,))
                            em.op("dve", lambda E, t=t, s=s, par=par:
                                  E.tensor_tensor_scan(
                                      Hbuf[:, s * TC:(s + 1) * TC],
                                      atmpS[par][:], btmpS[par][:],
                                      carry[:, t * S + s:t * S + s + 1],
                                      AL.mult, AL.add),
                                  reads=("m2atmp%d" % par, "m2bt%d" % par,
                                         "m2carry"),
                                  writes=("m2H%d" % s,))
                            em.op("dve", lambda E, s=s, par=par, bcC=bcC:
                                  E.tensor_mul(
                                      httmpS[par][:],
                                      Hbuf[:, s * TC:(s + 1) * TC],
                                      bcC[:, s * TC:(s + 1) * TC]),
                                  reads=("m2H%d" % s,
                                         "m2bcC%d_%d" % (cpar, s)),
                                  writes=("m2htt%d" % par,))
                            em.op("pe", lambda E, s=s, py=py, par=par: E.matmul(
                                py[:], ident[:], httmpS[par][:],
                                start=(s == 0), stop=(s == S - 1)),
                                reads=("m2htt%d" % par, "k_ident"),
                                writes=(kpy,))
                        em.op("dve", lambda E, t=t: E.tensor_copy(
                            carry[:, t * S:(t + 1) * S], Hbuf[:, TC - 1::TC]),
                            reads=tuple("m2H%d" % s for s in range(S)),
                            writes=("m2carry",))
                        em.op("dve", lambda E, t=t, d=d, py=py:
                              E.scalar_tensor_tensor(
                                  t1[:], xct[t][:],
                                  DS[:, d * NDT + t:d * NDT + t + 1],
                                  py[:], AL.mult, AL.add),
                              reads=("m2xc%d" % t, "k_D", kpy), writes=("m2t1",))
                        em.op("dve", lambda E, t=t: E.tensor_mul(
                            ybf[t][:], t1[:], szt[t][:]),
                            reads=("m2t1", "m2szL%d" % t), writes=("m2y%d" % t,))
                        if t + 2 < NDT:
                            sp(t + 2)
                    yrd = tuple("m2y%d" % t for t in range(NDT))
                    for dm in range(NKM):
                        po, kpo = p_o[dm % 2], "m2pou%d" % (dm % 2)
                        for kt in range(NDT):
                            em.op("pe", lambda E, dm=dm, kt=kt, po=po: E.matmul(
                                po[:], wt_b[:, kt * DM + dm * 128:
                                            kt * DM + (dm + 1) * 128],
                                ybf[kt][:], start=(kt == 0), stop=(kt == NDT - 1)),
                                reads=wtrd + yrd, writes=(kpo,))
                        if d == 0:
                            em.op("act", lambda E, po=po: E.activation(
                                potile[:], po[:], AF.Copy),
                                reads=(kpo,), writes=("m2pot",))
                            gc = i
                            dma("m2p0_%d_%d" % (gc, dm),
                                sym(partiald.ap(),
                                    gc * DM * TC + dm * 128 * TC, 128, TC, TC),
                                potile[:], reads=("m2pot",), eng="act")
                        else:
                            em.op("act", lambda E, po=po: E.activation(
                                potile[:, ::-1], po[:], AF.Copy),
                                reads=(kpo,), writes=("m2pot",))
                            gc = nch - 1 - i
                            dma("m2ao", aotile[:],
                                sym(partiald.ap(),
                                    gc * DM * TC + dm * 128 * TC, 128, TC, TC),
                                reads=("m2p0_%d_%d" % (gc, dm),))
                            em.op("dve", lambda E: E.tensor_add(
                                potile[:], potile[:], aotile[:]),
                                reads=("m2pot", "m2ao"), writes=("m2pot",))
                            dma("m2pw",
                                sym(partiald.ap(),
                                    NCH * DM * TC + gc * DM * TC +
                                    dm * 128 * TC, 128, TC, TC),
                                potile[:], reads=("m2pot",), eng="act")

        em.phase()
        dyn_copy("k_shpw", sh_part.ap(), "shw0", 0, partiald.ap(), "tokp", 0,
                 16 * DM * TC, ("m2pw",))
        barrier(2, ["k_shpw"])
        dyn_copy("k_pm", pmined.ap(), None, 0, sh_part.ap(), "f1a0", 0,
                 8 * DM * TC, ("k_pair2",))
        dyn_copy("k_pp", ppartd.ap(), None, 0, sh_part.ap(), "f1b0", 0,
                 8 * DM * TC, ("k_pair2",))

        if True:
            em.phase()
            ar.reset()
            faF1 = ar.alloc((128, FBLK))
            fbtF1 = ar.alloc((128, FBLK))
            xmtF1 = ar.alloc((128, DM))
            x1tF1 = ar.alloc((128, DM))
            x1hiF1 = ar.alloc((128, DM), BF16)
            x1loF1 = ar.alloc((128, DM), BF16)
            colAF1 = ar.alloc((128, 8))
            lnscrF1 = ar.alloc((128, DM))
            n1fmF1 = ar.alloc((128, NKM * FBLK))
            ptbF1 = PT

            for blk in range(4096 // FBLK):
                for tt in range(FBLK // 128):
                    i0 = blk * FBLK + tt * 128
                    dma("f1xm", xmtF1[:], sym(x_my, i0 * DM, 128, DM, DM))
                    ln_cols(colAF1, "f1xm", xmtF1, lnscrF1, "f1")
                    em.op("dve", lambda E: E.tensor_scalar(
                        x1tF1[:], xmtF1[:], colAF1[:, 5:6], colAF1[:, 6:7],
                        AL.mult, AL.add),
                        reads=("f1xm", "c5f1", "c6f1"), writes=("f1x1",))
                    em.op("dve", lambda E: E.tensor_copy(x1hiF1[:], x1tF1[:]),
                          reads=("f1x1",), writes=("f1hi",))
                    em.op("dve", lambda E: E.tensor_sub(x1tF1[:], x1tF1[:], x1hiF1[:]),
                          reads=("f1x1", "f1hi"), writes=("f1x1",))
                    em.op("dve", lambda E: E.tensor_copy(x1loF1[:], x1tF1[:]),
                          reads=("f1x1",), writes=("f1lo",))
                    for j in range(NKM):
                        em.op("pe", lambda E, j=j: E.transpose(
                            ptbF1[:], x1hiF1[:, j * 128:(j + 1) * 128], ident[:]),
                            reads=("f1hi", "k_ident"), writes=("f1pt",))
                        em.op("act", lambda E, j=j, tt=tt: E.activation(
                            n1fmF1[:, j * FBLK + tt * 128:j * FBLK + (tt + 1) * 128],
                            ptbF1[:], AF.Copy),
                            reads=("f1pt",), writes=("f1n1_%d_%d" % (j, tt),))
                        em.op("pe", lambda E, j=j: E.transpose(
                            ptbF1[:], x1loF1[:, j * 128:(j + 1) * 128], ident[:]),
                            reads=("f1lo", "k_ident"), writes=("f1pt",))
                        em.op("dve", lambda E, j=j, tt=tt: E.tensor_tensor(
                            n1fmF1[:, j * FBLK + tt * 128:j * FBLK + (tt + 1) * 128],
                            n1fmF1[:, j * FBLK + tt * 128:j * FBLK + (tt + 1) * 128],
                            ptbF1[:], AL.add),
                            reads=("f1pt", "f1n1_%d_%d" % (j, tt)),
                            writes=("f1n1_%d_%d" % (j, tt),))
                n1rd = tuple("f1n1_%d_%d" % (j, tt)
                             for j in range(NKM) for tt in range(FBLK // 128))
                for dm in range(NKM):
                    dma("f1pa", faF1[:], sym(pmined.ap(),
                                           blk * DM * TC + dm * 128 * TC,
                                           128, TC, TC), reads=("k_pm",))
                    dma("f1pb", fbtF1[:], sym(ppartd.ap(),
                                            blk * DM * TC + dm * 128 * TC,
                                            128, TC, TC), reads=("k_pp",))
                    em.op("dve", lambda E: E.tensor_add(faF1[:], faF1[:], fbtF1[:]),
                          reads=("f1pa", "f1pb"), writes=("f1pa",))
                    em.op("act", lambda E, dm=dm: E.activation(
                        faF1[:], faF1[:], AF.Identity, bias=outbS[:, dm:dm + 1]),
                        reads=("f1pa", "k_outb"), writes=("f1pa",))
                    em.op("dve", lambda E, dm=dm: E.tensor_add(
                        faF1[:], faF1[:], n1fmF1[:, dm * FBLK:(dm + 1) * FBLK]),
                        reads=("f1pa",) + n1rd, writes=("f1pa",))
                    dma("f1pb", fbtF1[:], sym(x_myT, dm * 128 * 4096 + blk * FBLK,
                                            128, 4096, FBLK))
                    em.op("dve", lambda E: E.tensor_add(faF1[:], faF1[:], fbtF1[:]),
                          reads=("f1pa", "f1pb"), writes=("f1pa",))
                    dma("f1s2w", sym(s2d.ap(), dm * 128 * 4096 + blk * FBLK,
                                     128, 4096, FBLK), faF1[:], reads=("f1pa",))
                    if debug:
                        dma("dbgs2", sym(dbg["d_stream2"],
                                         dm * 128 * 4096 + blk * FBLK,
                                         128, 4096, FBLK), faF1[:],
                            reads=("f1pa",))

        em.phase()
        ar.reset()
        x3hi = ar.alloc((128, NKM * 4096), BF16)
        x3lo = ar.alloc((128, NKM * 4096), BF16)
        f2a_base = ar.off
        wgd = nc.dram_tensor("wgd", [2, NE, 4096], BF16)
        hd = nc.dram_tensor("hd", [NE + 1, NHT * 128, 4096], BF16)
        oaccd = nc.dram_tensor("oaccd", [DM, 4096], F32)

        if True:
            s2t = [ar.alloc((128, FBLK)) for j in range(NKM)]
            hi_t = ar.alloc((128, FBLK), BF16)
            lo_t = ar.alloc((128, FBLK), BF16)
            sq_t = ar.alloc((128, FBLK))
            rowA = ar.alloc((1, 4 * FBLK))
            rhib = ar.alloc((1, FBLK), BF16)
            rlob = ar.alloc((1, FBLK), BF16)
            mbc = ar.alloc((128, FBLK))
            rbc = ar.alloc((128, FBLK))
            x3f = ar.alloc((128, FBLK))
            gwh = ar.alloc((128, NKM * NE), BF16)
            gwl = ar.alloc((128, NKM * NE), BF16)
            gstg = ar.alloc((128, NE))
            glr = ar.alloc((128, 2 * NE))
            sc_tm = ar.alloc((128, NE))
            bia = ar.alloc((128, NE))
            m1c = ar.alloc((128, 6))
            msk = ar.alloc((128, 3 * NE))
            wgt = ar.alloc((128, NE))
            wgh = ar.alloc((128, NE), BF16)
            wgl = ar.alloc((128, NE), BF16)
            ghtm = ar.alloc((NE, 128), BF16)
            gltm = ar.alloc((NE, 128), BF16)
            p_s = PB[0][0:1, :]
            p_q = PB[1][0:1, :]
            p_b = PB[2]
            p_g = PB[3][0:NE, :]
            p_t = PT

            def bcast_row_exact(dst_key, dst_tile, rowf32_key, rowf32, width):
                em.op("dve", lambda E: E.tensor_copy(rhib[0:1, 0:width], rowf32),
                      reads=(rowf32_key,), writes=("bchi",))
                em.op("dve", lambda E: E.tensor_sub(rowf32, rowf32,
                                                    rhib[0:1, 0:width]),
                      reads=(rowf32_key, "bchi"), writes=(rowf32_key,))
                em.op("dve", lambda E: E.tensor_copy(rlob[0:1, 0:width], rowf32),
                      reads=(rowf32_key,), writes=("bclo",))
                em.op("pe", lambda E: E.matmul(p_b[:, 0:width], onesb[:],
                                               rhib[0:1, 0:width],
                                               start=True, stop=False),
                      reads=("bchi", "k_ones"), writes=("f2pbk",))
                em.op("pe", lambda E: E.matmul(p_b[:, 0:width], onesb[:],
                                               rlob[0:1, 0:width],
                                               start=False, stop=True),
                      reads=("bclo", "k_ones"), writes=("f2pbk",))
                em.op("act", lambda E: E.activation(dst_tile[:, 0:width],
                                                    p_b[:, 0:width], AF.Copy),
                      reads=("f2pbk",), writes=(dst_key,))

            for j in range(NKM):
                dma("f2gstg", gstg[:], sym(gate_wT, j * 128 * NE, 128, NE, NE))
                em.op("dve", lambda E, j=j: E.tensor_copy(
                    gwh[:, j * NE:(j + 1) * NE], gstg[:]),
                    reads=("f2gstg",), writes=("f2gwh",))
                em.op("dve", lambda E, j=j: E.tensor_sub(
                    gstg[:], gstg[:], gwh[:, j * NE:(j + 1) * NE]),
                    reads=("f2gstg", "f2gwh"), writes=("f2gstg",))
                em.op("dve", lambda E, j=j: E.tensor_copy(
                    gwl[:, j * NE:(j + 1) * NE], gstg[:]),
                    reads=("f2gstg",), writes=("f2gwl",))

            for blk in range(4096 // FBLK):
                s2rd = []
                for j in range(NKM):
                    dma("f2s2_%d" % j, s2t[j][:], sym(
                        s2d.ap(), j * 128 * 4096 + blk * FBLK, 128, 4096, FBLK),
                        reads=("f1s2w",))
                    s2rd.append("f2s2_%d" % j)
                s2rd = tuple(s2rd)
                for j in range(NKM):
                    em.op("dve", lambda E, j=j: E.tensor_copy(hi_t[:], s2t[j][:]),
                          reads=(s2rd[j],), writes=("f2hit",))
                    em.op("dve", lambda E, j=j: E.tensor_sub(
                        sq_t[:], s2t[j][:], hi_t[:]),
                        reads=(s2rd[j], "f2hit"), writes=("f2sq",))
                    em.op("dve", lambda E: E.tensor_copy(lo_t[:], sq_t[:]),
                          reads=("f2sq",), writes=("f2lot",))
                    em.op("pe", lambda E, j=j: E.matmul(
                        p_s[:], onescb[:], hi_t[:], start=(j == 0), stop=False),
                        reads=("f2hit", "k_onesc"), writes=("f2psk",))
                    em.op("pe", lambda E, j=j: E.matmul(
                        p_s[:], onescb[:], lo_t[:], start=False,
                        stop=(j == NKM - 1)),
                        reads=("f2lot", "k_onesc"), writes=("f2psk",))
                    em.op("act", lambda E, j=j: E.activation(
                        sq_t[:], s2t[j][:], AF.Square),
                        reads=(s2rd[j],), writes=("f2sq",))
                    em.op("dve", lambda E: E.tensor_copy(hi_t[:], sq_t[:]),
                          reads=("f2sq",), writes=("f2hit",))
                    em.op("dve", lambda E: E.tensor_sub(sq_t[:], sq_t[:], hi_t[:]),
                          reads=("f2sq", "f2hit"), writes=("f2sq",))
                    em.op("dve", lambda E: E.tensor_copy(lo_t[:], sq_t[:]),
                          reads=("f2sq",), writes=("f2lot",))
                    em.op("pe", lambda E, j=j: E.matmul(
                        p_q[:], onescb[:], hi_t[:], start=(j == 0), stop=False),
                        reads=("f2hit", "k_onesc"), writes=("f2pqk",))
                    em.op("pe", lambda E, j=j: E.matmul(
                        p_q[:], onescb[:], lo_t[:], start=False,
                        stop=(j == NKM - 1)),
                        reads=("f2lot", "k_onesc"), writes=("f2pqk",))
                em.op("act", lambda E: E.activation(
                    rowA[0:1, 0:FBLK], p_s[:], AF.Copy, scale=1.0 / DM),
                    reads=("f2psk",), writes=("f2mu",))
                em.op("act", lambda E: E.activation(
                    rowA[0:1, FBLK:2 * FBLK], p_q[:], AF.Copy, scale=1.0 / DM),
                    reads=("f2pqk",), writes=("f2ex",))
                em.op("dve", lambda E: E.tensor_tensor(
                    rowA[0:1, 2 * FBLK:3 * FBLK], rowA[0:1, 0:FBLK],
                    rowA[0:1, 0:FBLK], AL.mult),
                    reads=("f2mu",), writes=("f2mu2",))
                em.op("dve", lambda E: E.tensor_tensor(
                    rowA[0:1, FBLK:2 * FBLK], rowA[0:1, FBLK:2 * FBLK],
                    rowA[0:1, 2 * FBLK:3 * FBLK], AL.subtract),
                    reads=("f2ex", "f2mu2"), writes=("f2ex",))
                em.op("act", lambda E: E.activation(
                    rowA[0:1, 3 * FBLK:4 * FBLK], rowA[0:1, FBLK:2 * FBLK],
                    AF.Ln, bias=EPS),
                    reads=("f2ex",), writes=("f2rr",))
                em.op("act", lambda E: E.activation(
                    rowA[0:1, 3 * FBLK:4 * FBLK], rowA[0:1, 3 * FBLK:4 * FBLK],
                    AF.Exp, scale=-0.5),
                    reads=("f2rr",), writes=("f2rr",))
                bcast_row_exact("f2mbc", mbc, "f2mu", rowA[0:1, 0:FBLK], FBLK)
                bcast_row_exact("f2rbc", rbc, "f2rr",
                                rowA[0:1, 3 * FBLK:4 * FBLK], FBLK)
                for j in range(NKM):
                    em.op("dve", lambda E, j=j: E.tensor_sub(
                        x3f[:], s2t[j][:], mbc[:]),
                        reads=(s2rd[j], "f2mbc"), writes=("f2x3f",))
                    em.op("dve", lambda E: E.tensor_mul(x3f[:], x3f[:], rbc[:]),
                          reads=("f2x3f", "f2rbc"), writes=("f2x3f",))
                    sl_hi = x3hi[:, j * 4096 + blk * FBLK:
                                 j * 4096 + (blk + 1) * FBLK]
                    sl_lo = x3lo[:, j * 4096 + blk * FBLK:
                                 j * 4096 + (blk + 1) * FBLK]
                    em.op("dve", lambda E, sl_hi=sl_hi: E.tensor_copy(
                        sl_hi, x3f[:]),
                        reads=("f2x3f",), writes=("x3hi_%d_%d" % (j, blk),))
                    em.op("dve", lambda E, sl_hi=sl_hi: E.tensor_sub(
                        x3f[:], x3f[:], sl_hi),
                        reads=("f2x3f", "x3hi_%d_%d" % (j, blk)),
                        writes=("f2x3f",))
                    em.op("dve", lambda E, sl_lo=sl_lo: E.tensor_copy(
                        sl_lo, x3f[:]),
                        reads=("f2x3f",), writes=("x3lo_%d_%d" % (j, blk),))
                x3rd = tuple("x3hi_%d_%d" % (j, blk) for j in range(NKM))
                x3rdl = tuple("x3lo_%d_%d" % (j, blk) for j in range(NKM))
                for j in range(NKM):
                    hi_sl = x3hi[:, j * 4096 + blk * FBLK:
                                 j * 4096 + (blk + 1) * FBLK]
                    lo_sl = x3lo[:, j * 4096 + blk * FBLK:
                                 j * 4096 + (blk + 1) * FBLK]
                    em.op("pe", lambda E, j=j, hi_sl=hi_sl: E.matmul(
                        p_g[:], gwh[:, j * NE:(j + 1) * NE], hi_sl,
                        start=(j == 0), stop=False),
                        reads=("f2gwh",) + x3rd, writes=("f2pgk",))
                    em.op("pe", lambda E, j=j, hi_sl=hi_sl: E.matmul(
                        p_g[:], gwl[:, j * NE:(j + 1) * NE], hi_sl,
                        start=False, stop=False),
                        reads=("f2gwl",) + x3rd, writes=("f2pgk",))
                    em.op("pe", lambda E, j=j, lo_sl=lo_sl: E.matmul(
                        p_g[:], gwh[:, j * NE:(j + 1) * NE], lo_sl,
                        start=False, stop=(j == NKM - 1)),
                        reads=("f2gwh",) + x3rdl, writes=("f2pgk",))
                em.op("act", lambda E: E.activation(
                    sq_t[0:NE, :], p_g[:], AF.Copy),
                    reads=("f2pgk",), writes=("f2sq",))
                em.op("dve", lambda E: E.tensor_copy(hi_t[0:NE, :], sq_t[0:NE, :]),
                      reads=("f2sq",), writes=("f2hit",))
                em.op("dve", lambda E: E.tensor_sub(
                    sq_t[0:NE, :], sq_t[0:NE, :], hi_t[0:NE, :]),
                    reads=("f2sq", "f2hit"), writes=("f2sq",))
                em.op("dve", lambda E: E.tensor_copy(lo_t[0:NE, :], sq_t[0:NE, :]),
                      reads=("f2sq",), writes=("f2lot",))
                for tt in range(FBLK // 128):
                    em.op("pe", lambda E, tt=tt: E.transpose(
                        p_t[:, 0:NE], hi_t[0:NE, tt * 128:(tt + 1) * 128],
                        ident[0:NE, 0:NE]),
                        reads=("f2hit", "k_ident"), writes=("f2ptk",))
                    em.op("act", lambda E: E.activation(
                        glr[:, 0:NE], p_t[:, 0:NE], AF.Copy),
                        reads=("f2ptk",), writes=("f2glr0",))
                    em.op("pe", lambda E, tt=tt: E.transpose(
                        p_t[:, 0:NE], lo_t[0:NE, tt * 128:(tt + 1) * 128],
                        ident[0:NE, 0:NE]),
                        reads=("f2lot", "k_ident"), writes=("f2ptk",))
                    em.op("act", lambda E: E.activation(
                        glr[:, NE:2 * NE], p_t[:, 0:NE], AF.Copy),
                        reads=("f2ptk",), writes=("f2glrl",))
                    em.op("dve", lambda E: E.tensor_add(
                        glr[:, 0:NE], glr[:, 0:NE], glr[:, NE:2 * NE]),
                        reads=("f2glr0", "f2glrl"), writes=("f2glr0",))
                    em.op("act", lambda E: E.activation(
                        sc_tm[:], glr[:, 0:NE], AF.Sigmoid),
                        reads=("f2glr0",), writes=("f2sctm",))
                    em.op("dve", lambda E: E.tensor_add(bia[:], sc_tm[:], ebbS[:]),
                          reads=("f2sctm", "k_eb"), writes=("f2bia",))
                    em.op("dve", lambda E: E.reduce_max(
                        m1c[:, 0:1], bia[:], axis=mybir.AxisListType.X),
                        reads=("f2bia",), writes=("f2m1",))
                    em.op("dve", lambda E: E.tensor_scalar(
                        msk[:, 0:NE], bia[:], m1c[:, 0:1], None, AL.is_equal),
                        reads=("f2bia", "f2m1"), writes=("f2msk1",))
                    em.op("dve", lambda E: E.tensor_mul(
                        msk[:, 0:NE], msk[:, 0:NE], p8row[:]),
                        reads=("f2msk1", "k_p8"), writes=("f2msk1",))
                    em.op("dve", lambda E: E.reduce_max(
                        m1c[:, 1:2], msk[:, 0:NE], axis=mybir.AxisListType.X),
                        reads=("f2msk1",), writes=("f2pm1",))
                    em.op("dve", lambda E: E.tensor_scalar(
                        msk[:, 0:NE], msk[:, 0:NE], m1c[:, 1:2], None,
                        AL.is_equal),
                        reads=("f2msk1", "f2pm1"), writes=("f2msk1",))
                    em.op("dve", lambda E: E.scalar_tensor_tensor(
                        msk[:, NE:2 * NE], msk[:, 0:NE], BIGNEG, bia[:],
                        AL.mult, AL.add),
                        reads=("f2msk1", "f2bia"), writes=("f2msk2",))
                    em.op("dve", lambda E: E.reduce_max(
                        m1c[:, 2:3], msk[:, NE:2 * NE], axis=mybir.AxisListType.X),
                        reads=("f2msk2",), writes=("f2m2",))
                    em.op("dve", lambda E: E.tensor_scalar(
                        msk[:, NE:2 * NE], msk[:, NE:2 * NE], m1c[:, 2:3], None,
                        AL.is_equal),
                        reads=("f2msk2", "f2m2"), writes=("f2msk2",))
                    em.op("dve", lambda E: E.tensor_mul(
                        msk[:, NE:2 * NE], msk[:, NE:2 * NE], p8row[:]),
                        reads=("f2msk2", "k_p8"), writes=("f2msk2",))
                    em.op("dve", lambda E: E.reduce_max(
                        m1c[:, 3:4], msk[:, NE:2 * NE], axis=mybir.AxisListType.X),
                        reads=("f2msk2",), writes=("f2pm2",))
                    em.op("dve", lambda E: E.tensor_scalar(
                        msk[:, NE:2 * NE], msk[:, NE:2 * NE], m1c[:, 3:4], None,
                        AL.is_equal),
                        reads=("f2msk2", "f2pm2"), writes=("f2msk2",))
                    em.op("dve", lambda E: E.tensor_add(
                        msk[:, 2 * NE:3 * NE], msk[:, 0:NE], msk[:, NE:2 * NE]),
                        reads=("f2msk1", "f2msk2"), writes=("f2msks",))
                    em.op("dve", lambda E: E.tensor_mul(
                        wgt[:], msk[:, 2 * NE:3 * NE], sc_tm[:]),
                        reads=("f2msks", "f2sctm"), writes=("f2wgt",))
                    em.op("dve", lambda E: E.tensor_mul(wgt[:], wgt[:], sc_tm[:]),
                          reads=("f2wgt", "f2sctm"), writes=("f2wgt",))
                    if debug and blk == 0 and tt == 0:
                        dma("dbgw", dbg["d_wgt"][:, 0:NE], wgt[:],
                            reads=("f2wgt",))
                    em.op("dve", lambda E: E.tensor_copy(wgh[:], wgt[:]),
                          reads=("f2wgt",), writes=("f2wgh",))
                    em.op("dve", lambda E: E.tensor_sub(wgt[:], wgt[:], wgh[:]),
                          reads=("f2wgt", "f2wgh"), writes=("f2wgt",))
                    em.op("dve", lambda E: E.tensor_copy(wgl[:], wgt[:]),
                          reads=("f2wgt",), writes=("f2wgl",))
                    em.op("pe", lambda E: E.transpose(
                        p_t[0:NE, :], wgh[:], ident[:]),
                        reads=("f2wgh", "k_ident"), writes=("f2ptk",))
                    em.op("act", lambda E: E.activation(
                        ghtm[:], p_t[0:NE, :], AF.Copy),
                        reads=("f2ptk",), writes=("f2wth",))
                    em.op("pe", lambda E: E.transpose(
                        p_t[0:NE, :], wgl[:], ident[:]),
                        reads=("f2wgl", "k_ident"), writes=("f2ptk",))
                    em.op("act", lambda E: E.activation(
                        gltm[:], p_t[0:NE, :], AF.Copy),
                        reads=("f2ptk",), writes=("f2wtl",))
                    t0 = blk * FBLK + tt * 128

                    def wsth(E, t0=t0):
                        dst = sym(wgd.ap(), t0, NE, 4096, 128)
                        return E.dma_start(out=dst, in_=ghtm[:])
                    em.op("sync", wsth, reads=("f2wth",),
                          writes=("wgS_%d_%d" % (blk, tt),), dma=True)

                    def wstl(E, t0=t0):
                        dst = sym(wgd.ap(), NE * 4096 + t0, NE, 4096, 128)
                        return E.dma_start(out=dst, in_=gltm[:])
                    em.op("sync", wstl, reads=("f2wtl",),
                          writes=("wgS_%d_%d" % (blk, tt),), dma=True)

        wgkeys = tuple("wgS_%d_%d" % (b_, t_) for b_ in range(4096 // FBLK)
                       for t_ in range(FBLK // 128))
        x3keys = tuple("x3hi_%d_%d" % (j, b_) for j in range(NKM)
                       for b_ in range(4096 // FBLK))

        if True:
            em.phase()
            ar.reset(65536)
            wupP = [ar.alloc((128, NKM * NHT * 128), BF16) for p in range(2)]
            wstgB = ar.alloc((128, NHT * 128))
            hsq = ar.alloc((128, FBLK))
            hbf = ar.alloc((128, FBLK), BF16)
            wgrow = ar.alloc((1, 2 * FBLK), BF16)
            p_h = [PB[0], PB[1]]
            p_w = PB[2]

            for g in range(NE + 1):
                gp = g % 2
                wup = wupP[gp]
                for j in range(NKM):
                    dma("f2wustg", wstgB[:], sym(
                        w_upT, g * DM * NHT * 128 + j * 128 * NHT * 128,
                        128, NHT * 128, NHT * 128))
                    em.op("dve", lambda E, j=j, wup=wup: E.tensor_copy(
                        wup[:, j * NHT * 128:(j + 1) * NHT * 128], wstgB[:]),
                        reads=("f2wustg",), writes=("f2wup%d_%d" % (gp, j),))
                uprd = tuple("f2wup%d_%d" % (gp, j) for j in range(NKM))
                for blk in range(4096 // FBLK):
                    if g < NE:
                        dma("f2wgrow", wgrow[0:1, 0:FBLK], sym(
                            wgd.ap(), g * 4096 + blk * FBLK, 1, 4096, FBLK),
                            reads=wgkeys)
                        dma("f2wgrow2", wgrow[0:1, FBLK:2 * FBLK], sym(
                            wgd.ap(), NE * 4096 + g * 4096 + blk * FBLK,
                            1, 4096, FBLK), reads=wgkeys)
                        em.op("pe", lambda E: E.matmul(
                            p_w[:], onesb[:], wgrow[0:1, 0:FBLK],
                            start=True, stop=False),
                            reads=("f2wgrow", "k_ones"), writes=("f2pwk",))
                        em.op("pe", lambda E: E.matmul(
                            p_w[:], onesb[:], wgrow[0:1, FBLK:2 * FBLK],
                            start=False, stop=True),
                            reads=("f2wgrow2", "k_ones"), writes=("f2pwk",))
                    for ht in range(NHT):
                        ph, kph = p_h[ht % 2], "f2ph%d" % (ht % 2)
                        for j in range(NKM):
                            hi_sl = x3hi[:, j * 4096 + blk * FBLK:
                                         j * 4096 + (blk + 1) * FBLK]
                            em.op("pe", lambda E, ht=ht, j=j, hi_sl=hi_sl,
                                  ph=ph, wup=wup: E.matmul(
                                      ph[:], wup[:, j * NHT * 128 + ht * 128:
                                                  j * NHT * 128 + (ht + 1) * 128],
                                      hi_sl, start=(j == 0), stop=(j == NKM - 1)),
                                  reads=uprd + x3keys, writes=(kph,))
                        em.op("act", lambda E, ph=ph: E.activation(
                            hsq[:], ph[:], AF.Relu),
                            reads=(kph,), writes=("f2hr",))
                        em.op("act", lambda E: E.activation(
                            hsq[:], hsq[:], AF.Square),
                            reads=("f2hr",), writes=("f2hr",))
                        if g < NE:
                            em.op("dve", lambda E: E.tensor_mul(
                                hbf[:], hsq[:], p_w[:]),
                                reads=("f2hr", "f2pwk"), writes=("f2hbf",))
                        else:
                            em.op("dve", lambda E: E.tensor_copy(hbf[:], hsq[:]),
                                  reads=("f2hr",), writes=("f2hbf",))
                        dma("f2hw", sym(hd.ap(),
                                        g * NHT * 128 * 4096 + ht * 128 * 4096 +
                                        blk * FBLK, 128, 4096, FBLK), hbf[:],
                            reads=("f2hbf",))

        em.op("sync", lambda E: E.dma_start(out=oaccd.ap(), in_=s2d.ap()),
              reads=("f1s2w",), writes=("oacc",), dma=True)
        if True:
            em.phase()
            ar.reset()
            wdnP = [ar.alloc((128, NHT * DM), BF16) for p in range(2)]
            wstgC = ar.alloc((128, DM))
            hsbP = [ar.alloc((128, NHT * FBLK), BF16) for p in range(2)]
            foP = [ar.alloc((128, FBLK)) for p in range(2)]
            aoP = [ar.alloc((128, FBLK)) for p in range(2)]
            p_u = [PB[0], PB[1]]

            for g in range(NE + 1):
                gp = g % 2
                wdn = wdnP[gp]
                for kt in range(NHT):
                    dma("f2wdstg", wstgC[:], sym(
                        w_downT, g * NHT * 128 * DM + kt * 128 * DM,
                        128, DM, DM))
                    em.op("dve", lambda E, kt=kt, wdn=wdn: E.tensor_copy(
                        wdn[:, kt * DM:(kt + 1) * DM], wstgC[:]),
                        reads=("f2wdstg",), writes=("f2wdn%d_%d" % (gp, kt),))
                dnrd = tuple("f2wdn%d_%d" % (gp, kt) for kt in range(NHT))
                for blk in range(4096 // FBLK):
                    bp = blk % 2
                    hsb = hsbP[bp]
                    hrd = []
                    for ht in range(NHT):
                        dma("f2hsb%d_%d" % (bp, ht),
                            hsb[:, ht * FBLK:(ht + 1) * FBLK],
                            sym(hd.ap(), g * NHT * 128 * 4096 + ht * 128 * 4096 +
                                blk * FBLK, 128, 4096, FBLK),
                            reads=("f2hw",))
                        hrd.append("f2hsb%d_%d" % (bp, ht))
                    hrd = tuple(hrd)
                    for dm in range(NKM):
                        pu, kpu = p_u[dm % 2], "f2pu%d" % (dm % 2)
                        fo = foP[dm % 2]
                        ao = aoP[dm % 2]
                        kfo = "f2fo%d" % (dm % 2)
                        kao = "f2ao%d" % (dm % 2)
                        for kt in range(NHT):
                            em.op("pe", lambda E, dm=dm, kt=kt, pu=pu, wdn=wdn,
                                  hsb=hsb: E.matmul(
                                      pu[:], wdn[:, kt * DM + dm * 128:
                                                  kt * DM + (dm + 1) * 128],
                                      hsb[:, kt * FBLK:(kt + 1) * FBLK],
                                      start=(kt == 0), stop=(kt == NHT - 1)),
                                  reads=dnrd + hrd, writes=(kpu,))
                        em.op("act", lambda E, pu=pu, fo=fo: E.activation(
                            fo[:], pu[:], AF.Copy),
                            reads=(kpu,), writes=(kfo,))
                        okey = "oacc_%d_%d" % (blk, dm)
                        dma(kao, ao[:], sym(
                            oaccd.ap(), dm * 128 * 4096 + blk * FBLK,
                            128, 4096, FBLK), reads=("oacc", okey))
                        em.op("dve", lambda E, fo=fo, ao=ao: E.tensor_add(
                            fo[:], fo[:], ao[:]),
                            reads=(kfo, kao), writes=(kfo,))
                        if g < NE:
                            dma(okey, sym(oaccd.ap(),
                                          dm * 128 * 4096 + blk * FBLK,
                                          128, 4096, FBLK), fo[:],
                                reads=(kfo,))
                        else:
                            def wout(E, dm=dm, blk=blk, fo=fo):
                                return E.dma_start(
                                    out=sym(out, dm * 128 * 4096 + blk * FBLK,
                                            128, 4096, FBLK), in_=fo[:])
                            em.op("sync", wout, reads=(kfo,),
                                  writes=("kout",), dma=True)


        if debug:
            Vecw = type(wmoT.ap().ap)
            em.op("sync", lambda E: E.dma_start(
                out=dbg["d_wt"],
                in_=dataclasses.replace(wmoT.ap(), offset=0,
                                        ap=Vecw([[1, 2 * DH * DM]]))),
                reads=("pwout",), writes=("dbgx",), dma=True)
            em.op("sync", lambda E: E.dma_start(out=dbg["d_n2"], in_=n2loc.ap()),
                  reads=("k_n2loc",), writes=("dbgx",), dma=True)
            em.op("sync", lambda E: E.dma_start(out=dbg["d_xdbl"],
                                                in_=xdbld.ap()),
                  reads=("pxw",), writes=("dbgx",), dma=True)
            Vecp = type(partiald.ap().ap)
            em.op("sync", lambda E: E.dma_start(
                out=dbg["d_partial"],
                in_=dataclasses.replace(partiald.ap(), offset=NCH * DM * TC,
                                        ap=Vecp([[1, NCH * DM * TC]]))),
                reads=("m2pw",), writes=("dbgx",), dma=True)
            em.op("sync", lambda E: E.dma_start(
                out=dbg["d_xc"],
                in_=dataclasses.replace(xcd.ap(), offset=0,
                                        ap=Vecp([[1, DH * L]]))),
                reads=("m1xcw",), writes=("dbgx",), dma=True)
            em.op("sync", lambda E: E.dma_start(out=dbg["d_pm"],
                                                in_=pmined.ap()),
                  reads=("k_pm",), writes=("dbgx",), dma=True)

        for x in em.names:
            if x in Em.ENGS or em.counts[x] == 0:
                continue
            em.raw("sync", lambda E, s=em.sems[x], c=em.counts[x]:
                   E.wait_ge(s, c))

        with nc.Block() as block:
            block.sync(lambda E: [t(E) for t in em.thunks["sync"]])
            block.scalar(lambda E: [t(E) for t in em.thunks["act"]])
            block.vector(lambda E: [t(E) for t in em.thunks["dve"]])
            block.tensor(lambda E: [t(E) for t in em.thunks["pe"]])
            block.gpsimd(lambda E: [t(E) for t in em.thunks["pool"]])

    return nc, offspec


def _noinc():
    return None


_MAGICS = np.array([[0x5A5A0101, 0x5A5A0202, 0x5A5A0303, 0x5A5A0404,
                     0, 0, 0, 0]], dtype=np.int32)
_NC_CACHE = {}


def _prep_core_inputs(inputs, c):
    f32 = np.float32
    side = c % 2
    k = c // 2
    batch = k % 2
    dup = k // 2
    tok_off = dup * 8192
    myfin = tok_off + side * 4096

    x = np.ascontiguousarray(np.asarray(inputs["L0_attn_x"], dtype=f32))
    d = {}
    d["x_ln"] = np.ascontiguousarray(x[batch, side * 8192:(side + 1) * 8192])
    xm = np.ascontiguousarray(x[batch, myfin:myfin + 4096])
    d["x_my"] = xm
    d["x_myT"] = np.ascontiguousarray(xm.T)

    w_inT = np.zeros((2, DM, 2 * DH), f32)
    conv_diag = np.zeros((2, 4, 128, NDT * 128), f32)
    conv_bias = np.zeros((128, 2 * NDT), f32)
    w_xpT = np.zeros((2, DH, 128), f32)
    w_dtT = np.zeros((2, DTR, DH), f32)
    dt_bias = np.zeros((128, 2 * NDT), f32)
    Alog_r = np.zeros((128, 2 * NDT * S), f32)
    D_r = np.zeros((128, 2 * NDT), f32)
    w_mo = np.zeros((2, DM, DH), f32)
    w_opT = np.zeros((2, DM, DM), f32)
    out_w = np.asarray(inputs["L0_attn_out_w"], dtype=f32)
    idx = np.arange(128)
    for di, tag in enumerate(("f", "b")):
        in_w = np.asarray(inputs["L0_attn_in_w_" + tag], dtype=f32)
        sl = np.concatenate([in_w[side * DH:(side + 1) * DH],
                             in_w[DI + side * DH:DI + (side + 1) * DH]], axis=0)
        w_inT[di] = sl.T
        cw = np.asarray(inputs["L0_attn_conv_w_" + tag], dtype=f32)[:, 0, :]
        cw = cw[side * DH:(side + 1) * DH]
        for tap in range(4):
            for dt in range(NDT):
                conv_diag[di, tap, idx, dt * 128 + idx] = cw[dt * 128 + idx, tap]
        cb = np.asarray(inputs["L0_attn_conv_b_" + tag], dtype=f32)
        conv_bias[:, di * NDT:(di + 1) * NDT] = \
            cb[side * DH:(side + 1) * DH].reshape(NDT, 128).T
        xp = np.asarray(inputs["L0_attn_xproj_w_" + tag], dtype=f32)
        xps = xp[:, side * DH:(side + 1) * DH]
        w_xpT[di][:, 0:16] = xps[64:80].T
        w_xpT[di][:, 32:48] = xps[80:96].T
        w_xpT[di][:, 64:128] = xps[0:64].T
        dw = np.asarray(inputs["L0_attn_dt_w_" + tag], dtype=f32)
        w_dtT[di] = dw[side * DH:(side + 1) * DH].T
        db = np.asarray(inputs["L0_attn_dt_b_" + tag], dtype=f32)
        dt_bias[:, di * NDT:(di + 1) * NDT] = \
            db[side * DH:(side + 1) * DH].reshape(NDT, 128).T
        Al = np.asarray(inputs["L0_attn_A_log_" + tag], dtype=f32)
        Al = Al[side * DH:(side + 1) * DH]
        Alog_r[:, di * 128:(di + 1) * 128] = \
            Al.reshape(NDT, 128, S).transpose(1, 0, 2).reshape(128, NDT * S)
        Dv = np.asarray(inputs["L0_attn_D_" + tag], dtype=f32)
        D_r[:, di * NDT:(di + 1) * NDT] = \
            Dv[side * DH:(side + 1) * DH].reshape(NDT, 128).T
        mo = np.asarray(inputs["L0_attn_mout_w_" + tag], dtype=f32)
        w_mo[di] = mo[:, side * DH:(side + 1) * DH]
        w_opT[di] = out_w[:, di * DM:(di + 1) * DM].T
    d["w_inT"] = w_inT
    d["conv_diag"] = conv_diag
    d["conv_b"] = conv_bias
    d["w_xpT"] = w_xpT
    d["w_dtT"] = w_dtT
    d["dt_b"] = dt_bias
    d["Alog_r"] = Alog_r
    d["D_r"] = D_r
    d["w_mo"] = w_mo
    d["w_opT"] = w_opT
    ob = np.asarray(inputs["L0_attn_out_b"], dtype=f32)
    d["out_bias"] = np.ascontiguousarray(ob.reshape(NKM, 128).T)
    d["gamma"] = np.asarray(inputs["L0_attn_gamma"], dtype=f32).reshape(1, DM)
    d["beta"] = np.asarray(inputs["L0_attn_beta"], dtype=f32).reshape(1, DM)
    d["gate_wT"] = np.ascontiguousarray(
        np.asarray(inputs["L0_ffn_gate_w"], dtype=f32).T)
    d["eb_row"] = np.asarray(
        inputs["L0_ffn_expert_bias"], dtype=f32).reshape(1, NE)
    wu = np.asarray(inputs["L0_ffn_w_up"], dtype=f32)
    wd = np.asarray(inputs["L0_ffn_w_down"], dtype=f32)
    swu = np.asarray(inputs["L0_ffn_shared_w_up"], dtype=f32)
    swd = np.asarray(inputs["L0_ffn_shared_w_down"], dtype=f32)
    w_upT = np.zeros((NE + 1, DM, NHT * 128), f32)
    w_downT = np.zeros((NE + 1, NHT * 128, DM), f32)
    for g in range(NE):
        w_upT[g] = wu[g].T
        w_downT[g] = wd[g].T
    w_upT[NE] = swu.T
    w_downT[NE] = swd.T
    d["w_upT"] = w_upT
    d["w_downT"] = w_downT
    d["cfg"] = np.array([[side, 1 - side, tok_off, side * 4096,
                          0, 0, 0, 0]], dtype=np.int32)
    d["magics"] = _MAGICS
    return d


_BASEVALS = None


def _base_value(name, c):
    side = c % 2
    k = c // 2
    dup = k // 2
    vals = {
        "n2w": side * (DM * HL),
        "xdw": side * (2 * 128 * L),
        "xdr": (1 - side) * (2 * 128 * L),
        "tokp": NCH * DM * TC + dup * 16 * (DM * TC),
        "shw0": side * (16 * DM * TC),
        "f1a0": side * (16 * DM * TC) + side * 8 * (DM * TC),
        "f1b0": (1 - side) * (16 * DM * TC) + side * 8 * (DM * TC),
        "flagw": side * 16,
        "flagr": (1 - side) * 16,
    }
    return vals[name]


def kernel(**inputs):
    if "nc" not in _NC_CACHE:
        _NC_CACHE["nc"], _NC_CACHE["offspec"] = build_nc()
    nc = _NC_CACHE["nc"]
    offspec = _NC_CACHE["offspec"]
    in_maps = [_prep_core_inputs(inputs, c) for c in range(8)]
    for c in range(8):
        tbl = np.zeros((1, 2048), np.int32)
        for idx, (base, const) in enumerate(offspec):
            tbl[0, idx] = _base_value(base, c) + const
        in_maps[c]["offs"] = tbl
    res = run_bass_kernel_spmd(nc, in_maps, core_ids=list(range(8)))
    outp = np.zeros((B, L, DM), np.float32)
    for c in range(8):
        side = c % 2
        k = c // 2
        batch = k % 2
        dup = k // 2
        myfin = dup * 8192 + side * 4096
        outp[batch, myfin:myfin + 4096, :] = res.results[c]["out"].T
    return outp


# version 08
# speedup vs baseline: 1.0603x; 1.0091x over previous
import dataclasses
import numpy as np
import ml_dtypes
from contextlib import ExitStack

import concourse.bass as bass
import concourse.mybir as mybir
from concourse.bass_utils import run_bass_kernel_spmd

F32 = mybir.dt.float32
BF16 = mybir.dt.bfloat16
I32 = mybir.dt.int32
AL = mybir.AluOpType
AF = mybir.ActivationFunctionType

B = 2
L = 16384
DM = 1024
DI = 2048
DH = 1024
S = 16
DTR = 64
HID = 1408
NE = 8
TC = 512
NCH = L // TC
NDT = DH // 128
NKM = DM // 128
NHT = HID // 128
HL = L // 2
EPS = 1e-5
FBLK = 512
FB2 = 256
BIGNEG = -1.0e30


class Em:

    ENGS = ("sync", "act", "dve", "pe", "pool")
    NSTREAM = {"sync": 4, "act": 2, "pool": 2}

    def __init__(self, nc, ctx):
        self.nc = nc
        self.names = list(self.ENGS)
        for e, n in self.NSTREAM.items():
            self.names += ["%s.%d" % (e, j) for j in range(n)]
        self.sems = {x: ctx.enter_context(
            nc.semaphore("sem_" + x.replace(".", "_"))) for x in self.names}
        self.counts = {x: 0 for x in self.names}
        self.rr = {e: 0 for e in self.NSTREAM}
        self.last_w = {}
        self.readers = {}
        self.marks = {e: {p: 0 for p in self.names} for e in self.ENGS}
        self.thunks = {e: [] for e in self.ENGS}

    def _need(self, eng, dep, needs):
        e2, c2, was_dma = dep
        if self.marks[eng][e2] >= c2:
            return
        needs[e2] = max(needs.get(e2, 0), c2)

    def op(self, eng, fn, reads=(), writes=(), dma=False):
        extra = getattr(self, "act_extra_read", None)
        if extra is not None and eng != "sync" and extra in self.last_w:
            reads = tuple(reads) + (extra,)
        needs = {}
        for k in reads:
            d = self.last_w.get(k)
            if d:
                self._need(eng, d, needs)
        for k in writes:
            d = self.last_w.get(k)
            if d:
                self._need(eng, d, needs)
            for d in self.readers.get(k, ()):
                self._need(eng, d, needs)
        if dma:
            tgt = "%s.%d" % (eng, self.rr[eng])
            self.rr[eng] = (self.rr[eng] + 1) % self.NSTREAM[eng]
            if self.counts[tgt] > 0:
                self._need(eng, (tgt, self.counts[tgt], True), needs)
        else:
            tgt = eng
        for e2, c2 in sorted(needs.items()):
            self.marks[eng][e2] = c2
            sem = self.sems[e2]
            self.thunks[eng].append(lambda E, s=sem, c=c2: E.wait_ge(s, c))
        inc = 16 if dma else 1
        self.counts[tgt] += inc
        sem = self.sems[tgt]

        def run(E, fn=fn, sem=sem, inc=inc):
            fn(E).then_inc(sem, inc)

        self.thunks[eng].append(run)
        tok = (tgt, self.counts[tgt], dma)
        for k in writes:
            self.last_w[k] = tok
            self.readers[k] = []
        for k in reads:
            self.readers.setdefault(k, []).append(tok)

    def raw(self, eng, fn):
        self.thunks[eng].append(fn)

    def phase(self):
        snap = dict(self.counts)
        for e in self.ENGS:
            for p in self.names:
                if p == e or snap[p] == 0 or self.marks[e][p] >= snap[p]:
                    continue
                self.marks[e][p] = snap[p]
                sem, c2 = self.sems[p], snap[p]
                self.thunks[e].append(lambda E, s=sem, c=c2: E.wait_ge(s, c))


class Arena:

    def __init__(self, tensor, nbytes):
        self.t = tensor
        self.n = nbytes
        self.off = 0

    def reset(self, off=0):
        self.off = off

    def alloc(self, shape, dt=F32):
        size = {F32: 4, BF16: 2, I32: 4}[dt]
        nb = shape[1] * size
        off = (self.off + 63) & ~63
        assert off + nb <= self.n, "arena overflow: %d + %d > %d" % (off, nb, self.n)
        self.off = off + nb
        view = self.t[0:shape[0], off:off + nb].bitcast(dt)
        return view


def build_nc(nch=NCH, debug=False):
    nc = bass.Bass("TRN2", target_bir_lowering=False)

    def inp(name, shape, dt=F32):
        return nc.dram_tensor(name, list(shape), dt, kind="ExternalInput").ap()

    x_ln = inp("x_ln", (HL, DM))
    x_my = inp("x_my", (4096, DM))
    x_myT = inp("x_myT", (DM, 4096))
    w_inT = inp("w_inT", (2, DM, 2 * DH))
    conv_diag = inp("conv_diag", (2, 4, 128, NDT * 128))
    conv_b = inp("conv_b", (128, 2 * NDT))
    w_xpT = inp("w_xpT", (2, DH, 128))
    w_dtT = inp("w_dtT", (2, DTR, DH))
    dt_b = inp("dt_b", (128, 2 * NDT))
    Alog_r = inp("Alog_r", (128, 2 * NDT * S))
    D_r = inp("D_r", (128, 2 * NDT))
    w_mo = inp("w_mo", (2, DM, DH))
    w_opT = inp("w_opT", (2, DM, DM))
    out_bias = inp("out_bias", (128, NKM))
    gamma = inp("gamma", (1, DM))
    beta = inp("beta", (1, DM))
    gate_wT = inp("gate_wT", (DM, NE))
    eb_row = inp("eb_row", (1, NE))
    w_upT = inp("w_upT", (NE + 1, DM, NHT * 128))
    w_downT = inp("w_downT", (NE + 1, NHT * 128, DM))
    cfg = inp("cfg", (1, 8), I32)
    magics = inp("magics", (1, 8), I32)
    offs = inp("offs", (1, 2048), I32)

    out = nc.dram_tensor("out", [DM, 4096], F32, kind="ExternalOutput").ap()
    dbg = {}
    if debug:
        for nm, shp, dt_ in [("d_n2", (DM, HL), BF16), ("d_xdbl", (2 * 128, L), F32),
                             ("d_delta", (DH, TC), F32),
                             ("d_partial", (NCH * DM, TC), F32),
                             ("d_stream2", (DM, 4096), F32),
                             ("d_wgt", (128, 4 * NE), F32),
                             ("d_y", (DH, TC), F32), ("d_xc", (DH, L), F32),
                             ("d_pm", (8 * DM, TC), F32),
                             ("d_x0", (128, DM), F32), ("d_x1", (128, DM), F32),
                             ("d_x2", (128, DM), F32), ("d_cols", (128, 8), F32),
                             ("d_xi", (DH, TC), BF16), ("d_n2t", (DM, TC), BF16),
                             ("d_wt", (2 * DH, DM), BF16),
                             ("d_wtb", (128, DM), BF16),
                             ("d_mob", (128, NKM * DH), BF16),
                             ("d_opb", (128, NKM * DM), BF16)]:
            dbg[nm] = nc.dram_tensor(nm, list(shp), dt_, kind="ExternalOutput").ap()

    xcd = nc.dram_tensor("xcd", [2, DH, L], BF16)
    szd = nc.dram_tensor("szd", [2, DH, L], BF16)
    xdbld = nc.dram_tensor("xdbld", [2, 128, L], F32)
    partiald = nc.dram_tensor("partiald", [2, NCH, DM, TC], F32)
    n2loc = nc.dram_tensor("n2loc", [DM, HL], BF16)
    xdpart = nc.dram_tensor("xdpart", [2, 128, L], F32)
    xdpart2 = nc.dram_tensor("xdpart2", [2, 128, L], F32)
    pmined = nc.dram_tensor("pmined", [8, DM, TC], F32)
    ppartd = nc.dram_tensor("ppartd", [8, DM, TC], F32)
    s2d = nc.dram_tensor("s2d", [DM, 4096], F32)
    wmoT = nc.dram_tensor("wmoT", [2, DH, DM], BF16)
    sh_n2 = nc.dram_tensor("sh_n2", [2, DM, HL], BF16, addr_space="Shared")
    sh_xdbl = nc.dram_tensor("sh_xdbl", [2, 2, 128, L], F32, addr_space="Shared")
    sh_part = nc.dram_tensor("sh_part", [2, 16, DM, TC], F32, addr_space="Shared")
    sh_flag = nc.dram_tensor("sh_flag", [2, 16], I32, addr_space="Shared")

    identd = nc.inline_tensor(np.eye(128, dtype=ml_dtypes.bfloat16), "identbf").ap()
    onesd = nc.inline_tensor(np.ones((1, 128), dtype=ml_dtypes.bfloat16), "onesbf").ap()
    onescd = nc.inline_tensor(np.ones((128, 1), dtype=ml_dtypes.bfloat16), "onescbf").ap()
    p8_np = np.broadcast_to(np.arange(NE, 0, -1, dtype=np.float32), (128, NE)).copy()
    p8d = nc.inline_tensor(p8_np, "p8rowd").ap()
    cst_np = np.broadcast_to(
        np.array([EPS, 1.0, -1.0, -0.5, 1.0 / DM, 0.0, 0.5, 2.0],
                 dtype=np.float32), (128, 8)).copy()
    cstd = nc.inline_tensor(cst_np, "cstd").ap()

    def sym(full_ap, offset, rows, rstride, cols):
        Vec = type(full_ap.ap)
        return dataclasses.replace(full_ap, offset=offset,
                                   ap=Vec([[rstride, rows], [1, cols]]))

    with ExitStack() as CTX:
        em = Em(nc, CTX)
        sb = lambda name, shape, dt=F32: CTX.enter_context(
            nc.sbuf_tensor(name, list(shape), dt))

        ident = sb("ident", (128, 128), BF16)
        onesb = sb("onesb", (1, 128), BF16)
        onescb = sb("onescb", (128, 1), BF16)
        p8row = sb("p8row", (128, NE))
        cfgS = sb("cfgS", (1, 8), I32)
        magS = sb("magS", (1, 8), I32)
        spoll = sb("spoll", (1, 4), I32)
        convbS = sb("convbS", (128, 2 * NDT))
        dtbS = sb("dtbS", (128, 2 * NDT))
        AbarS = sb("AbarS", (128, 2 * NDT * S))
        DS = sb("DS", (128, 2 * NDT))
        outbS = sb("outbS", (128, NKM))
        gamS = sb("gamS", (128, DM))
        betS = sb("betS", (128, DM))
        ebbS = sb("ebbS", (128, NE))
        carry = sb("carry", (128, NDT * S))
        arena_t = sb("arena", (128, 191488), mybir.dt.uint8)
        ar = Arena(arena_t, 191488)
        PB = [CTX.enter_context(nc.psum_tensor("pb%d" % i, [128, 512]))
              for i in range(7)]
        PT = CTX.enter_context(nc.psum_tensor("ptb", [128, 128], BF16))

        st = CTX.enter_context(nc.semaphore("st_out"))
        psem = CTX.enter_context(nc.semaphore("psem"))

        offS = sb("offS", (1, 2048), I32)
        cstS = sb("cstS", (128, 8))
        for _ci, _cv in enumerate([EPS, 1.0, -1.0, -0.5, 1.0 / DM, 0.0, 0.5, 2.0]):
            nc.const_aps.register(F32, _cv, cstS[:, _ci:_ci + 1])
        em.act_extra_read = "k_cst"
        EngT = mybir.EngineType
        ENGT = {"sync": EngT.SP, "act": EngT.Activation, "dve": EngT.DVE,
                "pe": EngT.PE, "pool": EngT.Pool}
        r_nd = nc.alloc_register(EngT.SP, "r_nd")
        r_pv = nc.alloc_register(EngT.SP, "r_pv")
        r_pc = nc.alloc_register(EngT.SP, "r_pc")
        r_mg = nc.alloc_register(EngT.SP, "r_mg")
        r_off = {e: nc.alloc_register(ENGT[e], "r_off_" + e)
                 for e in ("sync", "act", "pool")}
        r_of2 = {e: nc.alloc_register(ENGT[e], "r_of2_" + e)
                 for e in ("sync", "act", "pool")}
        offspec = []

        def roff(base, const):
            offspec.append((base, const))
            return len(offspec) - 1

        def dma(key_w, out_ap, in_ap, reads=(), eng="sync"):
            em.op(eng, lambda E: E.dma_start(out=out_ap, in_=in_ap),
                  reads=reads, writes=(key_w,), dma=True)

        dma("k_cst", cstS[:], cstd)
        dma("k_ident", ident[:], identd)
        dma("k_ones", onesb[:], onesd)
        dma("k_onesc", onescb[:], onescd)
        dma("k_p8", p8row[:], p8d)
        dma("k_cfg", cfgS[:], cfg)
        dma("k_mag", magS[:], magics)
        dma("k_convb", convbS[:], conv_b)
        dma("k_dtb", dtbS[:], dt_b)
        dma("k_alog", AbarS[:], Alog_r)
        dma("k_D", DS[:], D_r)
        dma("k_outb", outbS[:], out_bias)
        Vec = type(gamma.ap)
        dma("k_gam", gamS[:],
            dataclasses.replace(gamma, offset=0, ap=Vec([[0, 128], [1, DM]])))
        dma("k_bet", betS[:],
            dataclasses.replace(beta, offset=0, ap=Vec([[0, 128], [1, DM]])))
        dma("k_eb", ebbS[:],
            dataclasses.replace(eb_row, offset=0, ap=Vec([[0, 128], [1, NE]])))

        em.op("act", lambda E: E.activation(AbarS[:], AbarS[:], AF.Exp),
              reads=("k_alog",), writes=("k_alog",))
        em.op("dve", lambda E: E.tensor_scalar_mul(AbarS[:], AbarS[:], -1.0),
              reads=("k_alog",), writes=("k_alog",))

        dma("k_offs", offS[:], offs)
        em.op("sync", lambda E: E.reg_mov(r_pc, 0) and None or E.nop(),
              reads=("k_offs",), writes=("k_regs",))

        def dyn_copy(key_w, dst_full, dst_base, dst_c, src_full, src_base,
                     src_c, nelem, reads,
                     engs=("sync", "act", "pool", "sync")):
            per = nelem // len(engs)
            assert per * len(engs) == nelem
            VecD = type(dst_full.ap)
            VecS = type(src_full.ap)
            for pi, eng in enumerate(engs):
                di = roff(dst_base, dst_c + pi * per) if dst_base else None
                si = roff(src_base, src_c + pi * per) if src_base else None

                def fn(E, di=di, si=si, pi=pi, eng=eng):
                    if di is not None:
                        E.load(r_off[eng], offS[0:1, di:di + 1])
                        do = r_off[eng]
                    else:
                        do = dst_c + pi * per
                    if si is not None:
                        E.load(r_of2[eng], offS[0:1, si:si + 1])
                        so = r_of2[eng]
                    else:
                        so = src_c + pi * per
                    dap = dataclasses.replace(dst_full, offset=do,
                                              ap=VecD([[1, per]]))
                    sap = dataclasses.replace(src_full, offset=so,
                                              ap=VecS([[1, per]]))
                    return E.dma_start(out=dap, in_=sap)
                em.op(eng, fn, reads=tuple(reads) + ("k_offs",),
                      writes=(key_w,), dma=True)

        def barrier(phase, data_keys):
            fl = sh_flag.ap()
            Vec = type(fl.ap)
            idx1 = roff("flagw", phase)

            def wrf(E, idx1=idx1):
                E.load(r_off["sync"], offS[0:1, idx1:idx1 + 1])
                myf = dataclasses.replace(fl, offset=r_off["sync"],
                                          ap=Vec([[1, 1]]))
                return E.dma_start(out=myf, in_=magS[0:1, phase:phase + 1])
            em.op("sync", wrf, reads=tuple(data_keys) + ("k_mag", "k_offs"),
                  writes=("k_flg%d" % phase,), dma=True)
            idx2 = roff("flagr", phase)

            def poll(E, idx2=idx2):
                E.load(r_of2["sync"], offS[0:1, idx2:idx2 + 1])
                otf = dataclasses.replace(fl, offset=r_of2["sync"],
                                          ap=Vec([[1, 1]]))
                E.load(r_mg, magS[0:1, phase:phase + 1])
                E.reg_mov(r_nd, 1)
                with E.While(r_nd):
                    E.dma_start(out=spoll[0:1, 0:1], in_=otf).then_inc(psem, 16)
                    E.reg_add(r_pc, r_pc, 16)
                    E.wait_ge(psem, r_pc)
                    E.load(r_pv, spoll[0:1, 0:1])
                    E.reg_alu(r_nd, r_pv, r_mg, AL.bitwise_xor)
            em.raw("sync", poll)
            em.op("sync", lambda E, phase=phase: E.dma_start(
                out=spoll[0:1, 1:2], in_=magS[0:1, phase:phase + 1]),
                reads=(), writes=("k_pair%d" % phase,), dma=True)

        def ln_cols(colA, xt_key, xtile, scr, dst_key_sfx):
            em.op("act", lambda E: E.activation(
                scr[:, 0:DM], xtile[:, 0:DM], AF.Square, accum_out=colA[:, 0:1]),
                reads=(xt_key,), writes=("c0" + dst_key_sfx, "cjunk"))
            em.op("act", lambda E: E.activation(
                scr[:, 0:DM], xtile[:, 0:DM], AF.Identity, accum_out=colA[:, 1:2]),
                reads=(xt_key,), writes=("c1" + dst_key_sfx, "cjunk"))
            em.op("dve", lambda E: E.tensor_scalar_mul(
                colA[:, 2:3], colA[:, 1:2], 1.0 / DM),
                reads=("c1" + dst_key_sfx,), writes=("c2" + dst_key_sfx,))
            em.op("dve", lambda E: E.tensor_tensor(
                colA[:, 3:4], colA[:, 2:3], colA[:, 2:3], AL.mult),
                reads=("c2" + dst_key_sfx,), writes=("c3" + dst_key_sfx,))
            em.op("dve", lambda E: E.tensor_scalar(
                colA[:, 4:5], colA[:, 0:1], 1.0 / DM, None, AL.mult),
                reads=("c0" + dst_key_sfx,), writes=("c4" + dst_key_sfx,))
            em.op("dve", lambda E: E.tensor_tensor(
                colA[:, 4:5], colA[:, 4:5], colA[:, 3:4], AL.subtract),
                reads=("c3" + dst_key_sfx, "c4" + dst_key_sfx),
                writes=("c4" + dst_key_sfx,))
            em.op("act", lambda E: E.activation(
                colA[:, 5:6], colA[:, 4:5], AF.Ln, bias=EPS),
                reads=("c4" + dst_key_sfx,), writes=("c5" + dst_key_sfx,))
            em.op("act", lambda E: E.activation(
                colA[:, 5:6], colA[:, 5:6], AF.Exp, scale=-0.5),
                reads=("c5" + dst_key_sfx,), writes=("c5" + dst_key_sfx,))
            em.op("dve", lambda E: E.tensor_tensor(
                colA[:, 6:7], colA[:, 2:3], colA[:, 5:6], AL.mult),
                reads=("c2" + dst_key_sfx, "c5" + dst_key_sfx),
                writes=("c6" + dst_key_sfx,))
            em.op("dve", lambda E: E.tensor_scalar_mul(
                colA[:, 6:7], colA[:, 6:7], -1.0),
                reads=("c6" + dst_key_sfx,), writes=("c6" + dst_key_sfx,))

        if True:
            ar.reset()
            xt = ar.alloc((128, DM))
            x1t = ar.alloc((128, DM))
            n2t = ar.alloc((128, DM), BF16)
            n2fm = ar.alloc((128, DM), BF16)
            colA = ar.alloc((128, 8))
            lnscr = ar.alloc((128, DM))
            pt = PT

            for i in range(HL // 128):
                dma("p0x", xt[:], sym(x_ln, i * 128 * DM, 128, DM, DM))
                if debug and i == 0:
                    dma("dbg0", dbg["d_x0"], xt[:], reads=("p0x",))
                ln_cols(colA, "p0x", xt, lnscr, "p0")
                if debug and i == 0:
                    dma("dbg1b", dbg["d_cols"], colA[:],
                        reads=tuple("c%dp0" % q for q in range(7)))
                em.op("dve", lambda E: E.tensor_scalar(
                    x1t[:], xt[:], colA[:, 5:6], colA[:, 6:7], AL.mult, AL.add),
                    reads=("p0x", "c5p0", "c6p0"), writes=("p0x1",))
                if debug and i == 0:
                    dma("dbg1", dbg["d_x1"], x1t[:], reads=("p0x1",))
                ln_cols(colA, "p0x1", x1t, lnscr, "p0")
                em.op("dve", lambda E: E.tensor_scalar(
                    x1t[:], x1t[:], colA[:, 5:6], colA[:, 6:7], AL.mult, AL.add),
                    reads=("p0x1", "c5p0", "c6p0"), writes=("p0x1",))
                em.op("dve", lambda E: E.tensor_tensor(
                    x1t[:], x1t[:], gamS[:], AL.mult),
                    reads=("p0x1", "k_gam"), writes=("p0x1",))
                em.op("dve", lambda E: E.tensor_tensor(
                    n2t[:], x1t[:], betS[:], AL.add),
                    reads=("p0x1", "k_bet"), writes=("p0n2",))
                if debug and i == 0:
                    dma("dbg2", dbg["d_x2"], x1t[:], reads=("p0x1",))
                for j in range(NKM):
                    em.op("pe", lambda E, j=j: E.transpose(
                        pt[:], n2t[:, j * 128:(j + 1) * 128], ident[:]),
                        reads=("p0n2", "k_ident"), writes=("p0pt",))
                    em.op("act", lambda E, j=j: E.activation(
                        n2fm[:, j * 128:(j + 1) * 128], pt[:], AF.Copy),
                        reads=("p0pt",), writes=("p0fm%d" % j,))
                for j in range(NKM):
                    dma("k_n2loc", sym(n2loc.ap(), (j * 128) * HL + i * 128,
                                       128, HL, 128),
                        n2fm[:, j * 128:(j + 1) * 128],
                        reads=("p0fm%d" % j,))

        dyn_copy("k_shn2w", sh_n2.ap(), "n2w", 0, n2loc.ap(), None, 0,
                 DM * HL, ("k_n2loc",))
        barrier(0, ["k_shn2w"])

        if True:
            em.phase()
            ar.reset()
            stg = ar.alloc((128, DM))
            mo_b = ar.alloc((128, NKM * DH), BF16)
            op_b = ar.alloc((128, NKM * DM), BF16)
            wtPW = ar.alloc((128, DM), BF16)
            pw = PB[0]
            for d in range(2):
                for m in range(NKM):
                    dma("pwst", stg[:, 0:DH], sym(
                        w_mo, d * DM * DH + m * 128 * DH, 128, DH, DH))
                    em.op("dve", lambda E, m=m: E.tensor_copy(
                        mo_b[:, m * DH:(m + 1) * DH], stg[:, 0:DH]),
                        reads=("pwst",), writes=("pwmo%d" % m,))
                    dma("pwst", stg[:], sym(
                        w_opT, d * DM * DM + m * 128 * DM, 128, DM, DM))
                    em.op("dve", lambda E, m=m: E.tensor_copy(
                        op_b[:, m * DM:(m + 1) * DM], stg[:]),
                        reads=("pwst",), writes=("pwop%d" % m,))
                rd = tuple("pwmo%d" % m for m in range(NKM)) + \
                    tuple("pwop%d" % m for m in range(NKM))
                for kt in range(NDT):
                    for half in range(2):
                        for m in range(NKM):
                            em.op("pe", lambda E, kt=kt, half=half, m=m: E.matmul(
                                pw[:],
                                mo_b[:, m * DH + kt * 128:m * DH + (kt + 1) * 128],
                                op_b[:, m * DM + half * 512:m * DM + (half + 1) * 512],
                                start=(m == 0), stop=(m == NKM - 1)),
                                reads=rd, writes=("pwps",))
                        em.op("act", lambda E, half=half: E.activation(
                            wtPW[:, half * 512:(half + 1) * 512], pw[:], AF.Copy),
                            reads=("pwps",), writes=("pwwt",))
                    dma("pwout", sym(wmoT.ap(), d * DH * DM + kt * 128 * DM,
                                     128, DM, DM), wtPW[:], reads=("pwwt",))
            if debug:
                dma("dbgwtb", dbg["d_wtb"], wtPW[:], reads=("pwwt",))
                dma("dbgmob", dbg["d_mob"], mo_b[:],
                    reads=tuple("pwmo%d" % m for m in range(NKM)))
                dma("dbgopb", dbg["d_opb"], op_b[:],
                    reads=tuple("pwop%d" % m for m in range(NKM)))

        if True:
            em.phase()
            ar.reset()
            w_in_b = ar.alloc((128, NKM * 2 * DH), BF16)
            wstg = ar.alloc((128, 2 * DH))
            cdg = ar.alloc((128, 4 * NDT * 128), BF16)
            cstg = ar.alloc((128, NDT * 128))
            xpw = ar.alloc((128, NDT * 128), BF16)
            xpstg = ar.alloc((128, 128))
            n2bP = [[ar.alloc((128, TC), BF16) for k in range(NKM)]
                    for p in range(2)]
            n2rP = [[ar.alloc((128, TC), BF16) for k in range(NKM)]
                    for p in range(2)]
            xib = [[ar.alloc((128, TC + 4), BF16)
                    for t in range(NDT)] for p in range(2)]
            xcf = ar.alloc((128, TC), BF16)
            xcb = ar.alloc((128, NDT * TC), BF16)
            szf = ar.alloc((128, TC), BF16)
            xdacc = ar.alloc((128, TC))
            p_in = [PB[0], PB[1]]
            p_c = [PB[2], PB[3]]
            p_x = PB[4]

            for d in range(2):
                for kt in range(NKM):
                    dma("m1wstg", wstg[:], sym(
                        w_inT, d * DM * 2 * DH + kt * 128 * 2 * DH, 128,
                        2 * DH, 2 * DH))
                    em.op("dve", lambda E, kt=kt: E.tensor_copy(
                        w_in_b[:, kt * 2 * DH:(kt + 1) * 2 * DH], wstg[:]),
                        reads=("m1wstg",), writes=("m1wb%d" % kt,))
                for tap in range(4):
                    dma("m1cstg", cstg[:], sym(
                        conv_diag, (d * 4 + tap) * 128 * NDT * 128, 128,
                        NDT * 128, NDT * 128))
                    em.op("dve", lambda E, tap=tap: E.tensor_copy(
                        cdg[:, tap * NDT * 128:(tap + 1) * NDT * 128], cstg[:]),
                        reads=("m1cstg",), writes=("m1cdg%d" % tap,))
                for t in range(NDT):
                    dma("m1xpstg", xpstg[:], sym(
                        w_xpT, d * DH * 128 + t * 128 * 128, 128, 128, 128))
                    em.op("dve", lambda E, t=t: E.tensor_copy(
                        xpw[:, t * 128:(t + 1) * 128], xpstg[:]),
                        reads=("m1xpstg",), writes=("m1xpw%d" % t,))
                wrd = tuple("m1wb%d" % kt for kt in range(NKM))
                crd = tuple("m1cdg%d" % t for t in range(4))
                xprd = tuple("m1xpw%d" % t for t in range(NDT))

                for i in range(nch):
                    g = i if d == 0 else (nch - 1 - i)
                    par = i % 2
                    slot = g // (HL // TC)
                    col = (g % (HL // TC)) * TC
                    n2b = n2bP[par]
                    n2r = n2rP[par]
                    n2rd = []
                    for kt in range(NKM):
                        off = slot * DM * HL + kt * 128 * HL + col
                        dma("m1n2_%d_%d" % (par, kt), n2b[kt][:],
                            sym(sh_n2.ap(), off, 128, HL, TC),
                            reads=("k_pair0",))
                        if d == 1:
                            em.op("dve", lambda E, kt=kt, n2b=n2b, n2r=n2r:
                                  E.tensor_copy(
                                      n2r[kt][:], n2b[kt][:, ::-1]),
                                  reads=("m1n2_%d_%d" % (par, kt),),
                                  writes=("m1n2r%d_%d" % (par, kt),))
                            n2rd.append("m1n2r%d_%d" % (par, kt))
                        else:
                            n2rd.append("m1n2_%d_%d" % (par, kt))
                    n2use = n2r if d == 1 else n2b
                    n2rd = tuple(n2rd)
                    if debug and d == 0 and i == 0:
                        for kt in range(NKM):
                            dma("dn2t", sym(dbg["d_n2t"], kt * 128 * TC,
                                            128, TC, TC), n2b[kt][:],
                                reads=("m1n2_%d_%d" % (par, kt),))

                    for t in range(NDT):
                        pb, kpb = p_in[t % 2], "m1pi%d" % (t % 2)
                        for kt in range(NKM):
                            em.op("pe", lambda E, kt=kt, t=t, pb=pb,
                                  n2u=n2use: E.matmul(
                                pb[:], w_in_b[:, kt * 2 * DH + t * 128:
                                              kt * 2 * DH + (t + 1) * 128],
                                n2u[kt][:],
                                start=(kt == 0), stop=(kt == NKM - 1)),
                                reads=wrd + n2rd, writes=(kpb,))
                        if i == 0:
                            em.op("dve", lambda E, par=par, t=t: E.memset(
                                xib[par][t][:, 1:4], 0.0),
                                writes=("m1hal%d_%d" % (par, t),))
                        else:
                            em.op("dve", lambda E, par=par, t=t: E.tensor_copy(
                                xib[par][t][:, 1:4],
                                xib[1 - par][t][:, TC + 1:TC + 4]),
                                reads=("m1xib%d_%d" % (1 - par, t),),
                                writes=("m1hal%d_%d" % (par, t),))
                        em.op("act", lambda E, par=par, t=t, pb=pb: E.activation(
                            xib[par][t][:, 4:TC + 4], pb[:], AF.Copy),
                            reads=(kpb,), writes=("m1xib%d_%d" % (par, t),))
                        if debug and d == 0 and i == 0:
                            dma("dxi", sym(dbg["d_xi"], t * 128 * TC, 128, TC, TC),
                                xib[par][t][:, 4:TC + 4],
                                reads=("m1xib%d_%d" % (par, t),))
                        pc, kpc = p_c[t % 2], "m1pc%d" % (t % 2)
                        for tap in range(4):
                            em.op("pe", lambda E, tap=tap, t=t, par=par, pc=pc:
                                  E.matmul(
                                      pc[:],
                                      cdg[:, (tap * NDT + t) * 128:
                                          (tap * NDT + t + 1) * 128],
                                      xib[par][t][:, 1 + tap:1 + tap + TC],
                                      start=(tap == 0), stop=(tap == 3)),
                                  reads=crd + ("m1xib%d_%d" % (par, t),
                                               "m1hal%d_%d" % (par, t)),
                                  writes=(kpc,))
                        em.op("act", lambda E, d=d, t=t, pc=pc: E.activation(
                            xcf[:], pc[:], AF.Silu,
                            bias=convbS[:, d * NDT + t:d * NDT + t + 1]),
                            reads=(kpc, "k_convb"), writes=("m1xcf",))
                        dma("m1xcw", sym(xcd.ap(),
                                         d * DH * L + t * 128 * L + i * TC,
                                         128, L, TC), xcf[:],
                            reads=("m1xcf",), eng="act")
                        em.op("dve", lambda E, t=t: E.tensor_copy(
                            xcb[:, t * TC:(t + 1) * TC], xcf[:]),
                            reads=("m1xcf",), writes=("m1xcb%d" % t,))
                    for t in range(NDT):
                        pb, kpb = p_in[t % 2], "m1pi%d" % (t % 2)
                        for kt in range(NKM):
                            em.op("pe", lambda E, kt=kt, t=t, pb=pb,
                                  n2u=n2use: E.matmul(
                                pb[:], w_in_b[:, kt * 2 * DH + DH + t * 128:
                                              kt * 2 * DH + DH + (t + 1) * 128],
                                n2u[kt][:],
                                start=(kt == 0), stop=(kt == NKM - 1)),
                                reads=wrd + n2rd, writes=(kpb,))
                        em.op("act", lambda E, pb=pb: E.activation(
                            szf[:], pb[:], AF.Silu),
                            reads=(kpb,), writes=("m1szf",))
                        dma("m1szw", sym(szd.ap(),
                                         d * DH * L + t * 128 * L + i * TC,
                                         128, L, TC), szf[:],
                            reads=("m1szf",), eng="act")
                    xcrd = tuple("m1xcb%d" % t for t in range(NDT))
                    for t in range(NDT):
                        em.op("pe", lambda E, t=t: E.matmul(
                            p_x[:], xpw[:, t * 128:(t + 1) * 128],
                            xcb[:, t * TC:(t + 1) * TC],
                            start=(t == 0), stop=(t == NDT - 1)),
                            reads=xprd + xcrd, writes=("m1px",))
                    em.op("act", lambda E: E.activation(xdacc[:], p_x[:], AF.Copy),
                          reads=("m1px",), writes=("m1xda",))

                    dma("k_xdp", sym(xdpart.ap(), d * 128 * L + i * TC,
                                     128, L, TC), xdacc[:],
                        reads=("m1xda",))

        dyn_copy("k_shxw", sh_xdbl.ap(), "xdw", 0, xdpart.ap(), None, 0,
                 2 * 128 * L, ("k_xdp",))
        barrier(1, ["k_shxw"])

        if True:
            em.phase()
            ar.reset()
            xa = ar.alloc((128, 2048))
            xb = ar.alloc((128, 2048))
            dyn_copy("k_xd2", xdpart2.ap(), None, 0, sh_xdbl.ap(), "xdr", 0,
                     2 * 128 * L, ("k_pair1",))
            for d in range(2):
                for i in range(L // 2048):
                    dma("pxa", xa[:], sym(xdpart.ap(), d * 128 * L + i * 2048,
                                          128, L, 2048), reads=("k_xdp",))
                    dma("pxb", xb[:], sym(xdpart2.ap(), d * 128 * L + i * 2048,
                                          128, L, 2048), reads=("k_xd2",))
                    em.op("dve", lambda E: E.tensor_add(xa[:], xa[:], xb[:]),
                          reads=("pxa", "pxb"), writes=("pxa",))
                    dma("pxw", sym(xdbld.ap(), d * 128 * L + i * 2048,
                                   128, L, 2048), xa[:], reads=("pxa",))

        if True:
            em.phase()
            ar.reset()
            wdt_f = ar.alloc((DTR, DH))
            wdt_b = ar.alloc((DTR, DH), BF16)
            wt_b = ar.alloc((128, NDT * DM), BF16)
            xdb = ar.alloc((128, TC))
            dtb16P = [ar.alloc((DTR, TC), BF16) for p in range(2)]
            bc16 = ar.alloc((48, TC), BF16)
            stgB = ar.alloc((1, S * TC), BF16)
            stgC = ar.alloc((1, S * TC), BF16)
            bcBP = [ar.alloc((128, S * TC), BF16) for p in range(2)]
            bcCP = [ar.alloc((128, S * TC), BF16) for p in range(2)]
            deltP = [ar.alloc((128, TC)) for p in range(2)]
            dubP = [ar.alloc((128, TC), BF16) for p in range(2)]
            xct = [ar.alloc((128, TC), BF16) for t in range(NDT)]
            szt = [ar.alloc((128, TC), BF16) for t in range(NDT)]
            Hbuf = ar.alloc((128, S * TC), BF16)
            atmpS = [ar.alloc((128, TC)) for _ in range(2)]
            btmpS = [ar.alloc((128, TC), BF16) for _ in range(2)]
            httmpS = [ar.alloc((128, TC), BF16) for _ in range(2)]
            t1 = ar.alloc((128, TC))
            sptmp = ar.alloc((128, TC))
            ybf = [ar.alloc((128, TC), BF16) for t in range(NDT)]
            potile = ar.alloc((128, TC))
            aotile = ar.alloc((128, TC))
            p_bc = [PB[0], PB[1]]
            p_d = PB[2]
            p_y = [PB[3], PB[4]]
            p_o = [PB[5], PB[6]]

            def m2_stage(d, i):
                q = d * nch + i
                par = q % 2
                bcB, bcC, dtb16 = bcBP[par], bcCP[par], dtb16P[par]
                dma("m2xdb", xdb[:], sym(
                    xdbld.ap(), d * 128 * L + i * TC, 128, L, TC),
                    reads=("pxw",))
                em.op("pool", lambda E: E.tensor_copy(bc16[:], xdb[0:48, :]),
                      reads=("m2xdb",), writes=("m2bc16",))
                em.op("dve", lambda E, dtb16=dtb16: E.tensor_copy(
                    dtb16[:], xdb[64:128, :]),
                    reads=("m2xdb",), writes=("m2dt16_%d" % par,))
                em.op("sync", lambda E: E.dma_start(
                    out=stgB[:], in_=bc16[0:16, :]),
                    reads=("m2bc16",), writes=("m2stgB",), dma=True)
                em.op("sync", lambda E: E.dma_start(
                    out=stgC[:], in_=bc16[32:48, :]),
                    reads=("m2bc16",), writes=("m2stgC",), dma=True)

            def m2_stage_bc1(d, i, s):
                q = d * nch + i
                par = q % 2
                bcB, bcC = bcBP[par], bcCP[par]
                pb, kpb = p_bc[s % 2], "m2pbc%d" % (s % 2)
                em.op("pe", lambda E, s=s, pb=pb: E.matmul(
                    pb[:], onesb[:], stgB[0:1, s * TC:(s + 1) * TC],
                    start=True, stop=True),
                    reads=("m2stgB", "k_ones"), writes=(kpb,))
                em.op("act", lambda E, s=s, pb=pb, bcB=bcB: E.activation(
                    bcB[:, s * TC:(s + 1) * TC], pb[:], AF.Copy),
                    reads=(kpb,), writes=("m2bcB%d_%d" % (par, s),))
                em.op("pe", lambda E, s=s, pb=pb: E.matmul(
                    pb[:], onesb[:], stgC[0:1, s * TC:(s + 1) * TC],
                    start=True, stop=True),
                    reads=("m2stgC", "k_ones"), writes=(kpb,))
                em.op("act", lambda E, s=s, pb=pb, bcC=bcC: E.activation(
                    bcC[:, s * TC:(s + 1) * TC], pb[:], AF.Copy),
                    reads=(kpb,), writes=("m2bcC%d_%d" % (par, s),))

            for d in range(2):
                dma("m2wdtf", wdt_f[:], sym(w_dtT, d * DTR * DH, DTR, DH, DH))
                em.op("dve", lambda E: E.tensor_copy(wdt_b[:], wdt_f[:]),
                      reads=("m2wdtf",), writes=("m2wdt",))
                for kt in range(NDT):
                    dma("m2wt%d" % kt, wt_b[:, kt * DM:(kt + 1) * DM], sym(
                        wmoT.ap(), d * DH * DM + kt * 128 * DM, 128, DM, DM),
                        reads=("pwout",))
                wtrd = tuple("m2wt%d" % kt for kt in range(NDT))
                em.op("dve", lambda E: E.memset(carry[:], 0.0),
                      writes=("m2carry",))
                if d == 0:
                    m2_stage(0, 0)
                    for s in range(S):
                        m2_stage_bc1(0, 0, s)

                for i in range(nch):
                    q = d * nch + i
                    cpar = q % 2
                    bcB, bcC, dtb16 = bcBP[cpar], bcCP[cpar], dtb16P[cpar]
                    for t in range(NDT):
                        dma("m2xc%d" % t, xct[t][:], sym(
                            xcd.ap(), d * DH * L + t * 128 * L + i * TC,
                            128, L, TC), reads=("m1xcw",))
                        dma("m2szL%d" % t, szt[t][:], sym(
                            szd.ap(), d * DH * L + t * 128 * L + i * TC,
                            128, L, TC), reads=("m1szw",))

                    def sp(t, d=d, dtb16=dtb16):
                        tp = t % 2
                        em.op("pe", lambda E, t=t, dtb16=dtb16: E.matmul(
                            p_d[:], wdt_b[:, t * 128:(t + 1) * 128], dtb16[:],
                            start=True, stop=True),
                            reads=("m2wdt", "m2dt16_%d" % cpar),
                            writes=("m2pd",))
                        em.op("act", lambda E, d=d, t=t: E.activation(
                            sptmp[:], p_d[:], AF.Relu,
                            bias=dtbS[:, d * NDT + t:d * NDT + t + 1]),
                            reads=("m2pd", "k_dtb"), writes=("m2spr",))
                        em.op("act", lambda E, d=d, t=t, tp=tp: E.activation(
                            deltP[tp][:], p_d[:], AF.Abs,
                            bias=dtbS[:, d * NDT + t:d * NDT + t + 1]),
                            reads=("m2pd", "k_dtb"), writes=("m2del%d" % tp,))
                        em.op("act", lambda E, tp=tp: E.activation(
                            deltP[tp][:], deltP[tp][:], AF.Exp, scale=-1.0),
                            reads=("m2del%d" % tp,), writes=("m2del%d" % tp,))
                        em.op("act", lambda E, tp=tp: E.activation(
                            deltP[tp][:], deltP[tp][:], AF.Ln, bias=1.0),
                            reads=("m2del%d" % tp,), writes=("m2del%d" % tp,))
                        em.op("pool", lambda E, tp=tp: E.tensor_add(
                            deltP[tp][:], deltP[tp][:], sptmp[:]),
                            reads=("m2spr", "m2del%d" % tp),
                            writes=("m2del%d" % tp,))
                        em.op("pool", lambda E, t=t, tp=tp: E.tensor_mul(
                            dubP[tp][:], deltP[tp][:], xct[t][:]),
                            reads=("m2del%d" % tp, "m2xc%d" % t),
                            writes=("m2du%d" % tp,))

                    sp(0)
                    sp(1)
                    if q + 1 < 2 * nch:
                        m2_stage((q + 1) // nch, (q + 1) % nch)

                    for t in range(NDT):
                        tp = t % 2
                        if q + 1 < 2 * nch:
                            m2_stage_bc1((q + 1) // nch, (q + 1) % nch, 2 * t)
                            m2_stage_bc1((q + 1) // nch, (q + 1) % nch,
                                         2 * t + 1)
                        py, kpy = p_y[t % 2], "m2py%d" % (t % 2)
                        for s in range(S):
                            par = s % 2
                            acol = AbarS[:, d * 128 + t * S + s:
                                         d * 128 + t * S + s + 1]
                            em.op("act", lambda E, tp=tp, acol=acol, par=par:
                                  E.activation(
                                      atmpS[par][:], deltP[tp][:], AF.Exp,
                                      scale=acol),
                                  reads=("m2del%d" % tp, "k_alog"),
                                  writes=("m2atmp%d" % par,))
                            em.op("dve", lambda E, tp=tp, s=s, par=par,
                                  bcB=bcB: E.tensor_mul(
                                      btmpS[par][:], dubP[tp][:],
                                      bcB[:, s * TC:(s + 1) * TC]),
                                  reads=("m2du%d" % tp,
                                         "m2bcB%d_%d" % (cpar, s)),
                                  writes=("m2bt%d" % par,))
                            em.op("dve", lambda E, t=t, s=s, par=par:
                                  E.tensor_tensor_scan(
                                      Hbuf[:, s * TC:(s + 1) * TC],
                                      atmpS[par][:], btmpS[par][:],
                                      carry[:, t * S + s:t * S + s + 1],
                                      AL.mult, AL.add),
                                  reads=("m2atmp%d" % par, "m2bt%d" % par,
                                         "m2carry"),
                                  writes=("m2H%d" % s,))
                            em.op("dve", lambda E, s=s, par=par, bcC=bcC:
                                  E.tensor_mul(
                                      httmpS[par][:],
                                      Hbuf[:, s * TC:(s + 1) * TC],
                                      bcC[:, s * TC:(s + 1) * TC]),
                                  reads=("m2H%d" % s,
                                         "m2bcC%d_%d" % (cpar, s)),
                                  writes=("m2htt%d" % par,))
                            em.op("pe", lambda E, s=s, py=py, par=par: E.matmul(
                                py[:], ident[:], httmpS[par][:],
                                start=(s == 0), stop=(s == S - 1)),
                                reads=("m2htt%d" % par, "k_ident"),
                                writes=(kpy,))
                        em.op("dve", lambda E, t=t: E.tensor_copy(
                            carry[:, t * S:(t + 1) * S], Hbuf[:, TC - 1::TC]),
                            reads=tuple("m2H%d" % s for s in range(S)),
                            writes=("m2carry",))
                        em.op("dve", lambda E, t=t, d=d, py=py:
                              E.scalar_tensor_tensor(
                                  t1[:], xct[t][:],
                                  DS[:, d * NDT + t:d * NDT + t + 1],
                                  py[:], AL.mult, AL.add),
                              reads=("m2xc%d" % t, "k_D", kpy), writes=("m2t1",))
                        em.op("dve", lambda E, t=t: E.tensor_mul(
                            ybf[t][:], t1[:], szt[t][:]),
                            reads=("m2t1", "m2szL%d" % t), writes=("m2y%d" % t,))
                        if t + 2 < NDT:
                            sp(t + 2)
                    yrd = tuple("m2y%d" % t for t in range(NDT))
                    for dm in range(NKM):
                        po, kpo = p_o[dm % 2], "m2pou%d" % (dm % 2)
                        for kt in range(NDT):
                            em.op("pe", lambda E, dm=dm, kt=kt, po=po: E.matmul(
                                po[:], wt_b[:, kt * DM + dm * 128:
                                            kt * DM + (dm + 1) * 128],
                                ybf[kt][:], start=(kt == 0), stop=(kt == NDT - 1)),
                                reads=wtrd + yrd, writes=(kpo,))
                        if d == 0:
                            em.op("act", lambda E, po=po: E.activation(
                                potile[:], po[:], AF.Copy),
                                reads=(kpo,), writes=("m2pot",))
                            gc = i
                            dma("m2p0_%d_%d" % (gc, dm),
                                sym(partiald.ap(),
                                    gc * DM * TC + dm * 128 * TC, 128, TC, TC),
                                potile[:], reads=("m2pot",), eng="act")
                        else:
                            em.op("act", lambda E, po=po: E.activation(
                                potile[:, ::-1], po[:], AF.Copy),
                                reads=(kpo,), writes=("m2pot",))
                            gc = nch - 1 - i
                            dma("m2ao", aotile[:],
                                sym(partiald.ap(),
                                    gc * DM * TC + dm * 128 * TC, 128, TC, TC),
                                reads=("m2p0_%d_%d" % (gc, dm),))
                            em.op("dve", lambda E: E.tensor_add(
                                potile[:], potile[:], aotile[:]),
                                reads=("m2pot", "m2ao"), writes=("m2pot",))
                            dma("m2pw",
                                sym(partiald.ap(),
                                    NCH * DM * TC + gc * DM * TC +
                                    dm * 128 * TC, 128, TC, TC),
                                potile[:], reads=("m2pot",), eng="act")

        em.phase()
        dyn_copy("k_shpw", sh_part.ap(), "shw0", 0, partiald.ap(), "tokp", 0,
                 16 * DM * TC, ("m2pw",))
        barrier(2, ["k_shpw"])
        dyn_copy("k_pm", pmined.ap(), None, 0, sh_part.ap(), "f1a0", 0,
                 8 * DM * TC, ("k_pair2",))
        dyn_copy("k_pp", ppartd.ap(), None, 0, sh_part.ap(), "f1b0", 0,
                 8 * DM * TC, ("k_pair2",))

        if True:
            em.phase()
            ar.reset()
            faF1 = ar.alloc((128, FBLK))
            fbtF1 = ar.alloc((128, FBLK))
            xmtF1 = ar.alloc((128, DM))
            x1tF1 = ar.alloc((128, DM))
            x1hiF1 = ar.alloc((128, DM), BF16)
            x1loF1 = ar.alloc((128, DM), BF16)
            colAF1 = ar.alloc((128, 8))
            lnscrF1 = ar.alloc((128, DM))
            n1fmF1 = ar.alloc((128, NKM * FBLK))
            ptbF1 = PT

            for blk in range(4096 // FBLK):
                for tt in range(FBLK // 128):
                    i0 = blk * FBLK + tt * 128
                    dma("f1xm", xmtF1[:], sym(x_my, i0 * DM, 128, DM, DM))
                    ln_cols(colAF1, "f1xm", xmtF1, lnscrF1, "f1")
                    em.op("dve", lambda E: E.tensor_scalar(
                        x1tF1[:], xmtF1[:], colAF1[:, 5:6], colAF1[:, 6:7],
                        AL.mult, AL.add),
                        reads=("f1xm", "c5f1", "c6f1"), writes=("f1x1",))
                    em.op("dve", lambda E: E.tensor_copy(x1hiF1[:], x1tF1[:]),
                          reads=("f1x1",), writes=("f1hi",))
                    em.op("dve", lambda E: E.tensor_sub(x1tF1[:], x1tF1[:], x1hiF1[:]),
                          reads=("f1x1", "f1hi"), writes=("f1x1",))
                    em.op("dve", lambda E: E.tensor_copy(x1loF1[:], x1tF1[:]),
                          reads=("f1x1",), writes=("f1lo",))
                    for j in range(NKM):
                        em.op("pe", lambda E, j=j: E.transpose(
                            ptbF1[:], x1hiF1[:, j * 128:(j + 1) * 128], ident[:]),
                            reads=("f1hi", "k_ident"), writes=("f1pt",))
                        em.op("act", lambda E, j=j, tt=tt: E.activation(
                            n1fmF1[:, j * FBLK + tt * 128:j * FBLK + (tt + 1) * 128],
                            ptbF1[:], AF.Copy),
                            reads=("f1pt",), writes=("f1n1_%d_%d" % (j, tt),))
                        em.op("pe", lambda E, j=j: E.transpose(
                            ptbF1[:], x1loF1[:, j * 128:(j + 1) * 128], ident[:]),
                            reads=("f1lo", "k_ident"), writes=("f1pt",))
                        em.op("dve", lambda E, j=j, tt=tt: E.tensor_tensor(
                            n1fmF1[:, j * FBLK + tt * 128:j * FBLK + (tt + 1) * 128],
                            n1fmF1[:, j * FBLK + tt * 128:j * FBLK + (tt + 1) * 128],
                            ptbF1[:], AL.add),
                            reads=("f1pt", "f1n1_%d_%d" % (j, tt)),
                            writes=("f1n1_%d_%d" % (j, tt),))
                n1rd = tuple("f1n1_%d_%d" % (j, tt)
                             for j in range(NKM) for tt in range(FBLK // 128))
                for dm in range(NKM):
                    dma("f1pa", faF1[:], sym(pmined.ap(),
                                           blk * DM * TC + dm * 128 * TC,
                                           128, TC, TC), reads=("k_pm",))
                    dma("f1pb", fbtF1[:], sym(ppartd.ap(),
                                            blk * DM * TC + dm * 128 * TC,
                                            128, TC, TC), reads=("k_pp",))
                    em.op("dve", lambda E: E.tensor_add(faF1[:], faF1[:], fbtF1[:]),
                          reads=("f1pa", "f1pb"), writes=("f1pa",))
                    em.op("act", lambda E, dm=dm: E.activation(
                        faF1[:], faF1[:], AF.Identity, bias=outbS[:, dm:dm + 1]),
                        reads=("f1pa", "k_outb"), writes=("f1pa",))
                    em.op("dve", lambda E, dm=dm: E.tensor_add(
                        faF1[:], faF1[:], n1fmF1[:, dm * FBLK:(dm + 1) * FBLK]),
                        reads=("f1pa",) + n1rd, writes=("f1pa",))
                    dma("f1pb", fbtF1[:], sym(x_myT, dm * 128 * 4096 + blk * FBLK,
                                            128, 4096, FBLK))
                    em.op("dve", lambda E: E.tensor_add(faF1[:], faF1[:], fbtF1[:]),
                          reads=("f1pa", "f1pb"), writes=("f1pa",))
                    dma("f1s2w", sym(s2d.ap(), dm * 128 * 4096 + blk * FBLK,
                                     128, 4096, FBLK), faF1[:], reads=("f1pa",))
                    if debug:
                        dma("dbgs2", sym(dbg["d_stream2"],
                                         dm * 128 * 4096 + blk * FBLK,
                                         128, 4096, FBLK), faF1[:],
                            reads=("f1pa",))

        em.phase()
        ar.reset()
        x3hi = ar.alloc((128, NKM * 4096), BF16)
        x3lo = ar.alloc((128, NKM * 4096), BF16)
        f2a_base = ar.off
        wgd = nc.dram_tensor("wgd", [2, NE, 4096], BF16)
        hd = nc.dram_tensor("hd", [NE + 1, NHT * 128, 4096], BF16)
        oaccd = nc.dram_tensor("oaccd", [DM, 4096], F32)

        if True:
            s2t = [ar.alloc((128, FBLK)) for j in range(NKM)]
            hi_t = ar.alloc((128, FBLK), BF16)
            lo_t = ar.alloc((128, FBLK), BF16)
            sq_t = ar.alloc((128, FBLK))
            rowA = ar.alloc((1, 4 * FBLK))
            rhib = ar.alloc((1, FBLK), BF16)
            rlob = ar.alloc((1, FBLK), BF16)
            mbc = ar.alloc((128, FBLK))
            rbc = ar.alloc((128, FBLK))
            x3f = ar.alloc((128, FBLK))
            gwh = ar.alloc((128, NKM * NE), BF16)
            gwl = ar.alloc((128, NKM * NE), BF16)
            gstg = ar.alloc((128, NE))
            glr = ar.alloc((128, 2 * NE))
            sc_tm = ar.alloc((128, NE))
            bia = ar.alloc((128, NE))
            m1c = ar.alloc((128, 6))
            msk = ar.alloc((128, 3 * NE))
            wgt = ar.alloc((128, NE))
            wgh = ar.alloc((128, NE), BF16)
            wgl = ar.alloc((128, NE), BF16)
            ghtm = ar.alloc((NE, 128), BF16)
            gltm = ar.alloc((NE, 128), BF16)
            p_s = PB[0][0:1, :]
            p_q = PB[1][0:1, :]
            p_b = PB[2]
            p_g = PB[3][0:NE, :]
            p_t = PT

            def bcast_row_exact(dst_key, dst_tile, rowf32_key, rowf32, width):
                em.op("dve", lambda E: E.tensor_copy(rhib[0:1, 0:width], rowf32),
                      reads=(rowf32_key,), writes=("bchi",))
                em.op("dve", lambda E: E.tensor_sub(rowf32, rowf32,
                                                    rhib[0:1, 0:width]),
                      reads=(rowf32_key, "bchi"), writes=(rowf32_key,))
                em.op("dve", lambda E: E.tensor_copy(rlob[0:1, 0:width], rowf32),
                      reads=(rowf32_key,), writes=("bclo",))
                em.op("pe", lambda E: E.matmul(p_b[:, 0:width], onesb[:],
                                               rhib[0:1, 0:width],
                                               start=True, stop=False),
                      reads=("bchi", "k_ones"), writes=("f2pbk",))
                em.op("pe", lambda E: E.matmul(p_b[:, 0:width], onesb[:],
                                               rlob[0:1, 0:width],
                                               start=False, stop=True),
                      reads=("bclo", "k_ones"), writes=("f2pbk",))
                em.op("act", lambda E: E.activation(dst_tile[:, 0:width],
                                                    p_b[:, 0:width], AF.Copy),
                      reads=("f2pbk",), writes=(dst_key,))

            for j in range(NKM):
                dma("f2gstg", gstg[:], sym(gate_wT, j * 128 * NE, 128, NE, NE))
                em.op("dve", lambda E, j=j: E.tensor_copy(
                    gwh[:, j * NE:(j + 1) * NE], gstg[:]),
                    reads=("f2gstg",), writes=("f2gwh",))
                em.op("dve", lambda E, j=j: E.tensor_sub(
                    gstg[:], gstg[:], gwh[:, j * NE:(j + 1) * NE]),
                    reads=("f2gstg", "f2gwh"), writes=("f2gstg",))
                em.op("dve", lambda E, j=j: E.tensor_copy(
                    gwl[:, j * NE:(j + 1) * NE], gstg[:]),
                    reads=("f2gstg",), writes=("f2gwl",))

            for blk in range(4096 // FBLK):
                s2rd = []
                for j in range(NKM):
                    dma("f2s2_%d" % j, s2t[j][:], sym(
                        s2d.ap(), j * 128 * 4096 + blk * FBLK, 128, 4096, FBLK),
                        reads=("f1s2w",))
                    s2rd.append("f2s2_%d" % j)
                s2rd = tuple(s2rd)
                for j in range(NKM):
                    em.op("dve", lambda E, j=j: E.tensor_copy(hi_t[:], s2t[j][:]),
                          reads=(s2rd[j],), writes=("f2hit",))
                    em.op("dve", lambda E, j=j: E.tensor_sub(
                        sq_t[:], s2t[j][:], hi_t[:]),
                        reads=(s2rd[j], "f2hit"), writes=("f2sq",))
                    em.op("dve", lambda E: E.tensor_copy(lo_t[:], sq_t[:]),
                          reads=("f2sq",), writes=("f2lot",))
                    em.op("pe", lambda E, j=j: E.matmul(
                        p_s[:], onescb[:], hi_t[:], start=(j == 0), stop=False),
                        reads=("f2hit", "k_onesc"), writes=("f2psk",))
                    em.op("pe", lambda E, j=j: E.matmul(
                        p_s[:], onescb[:], lo_t[:], start=False,
                        stop=(j == NKM - 1)),
                        reads=("f2lot", "k_onesc"), writes=("f2psk",))
                    em.op("act", lambda E, j=j: E.activation(
                        sq_t[:], s2t[j][:], AF.Square),
                        reads=(s2rd[j],), writes=("f2sq",))
                    em.op("dve", lambda E: E.tensor_copy(hi_t[:], sq_t[:]),
                          reads=("f2sq",), writes=("f2hit",))
                    em.op("dve", lambda E: E.tensor_sub(sq_t[:], sq_t[:], hi_t[:]),
                          reads=("f2sq", "f2hit"), writes=("f2sq",))
                    em.op("dve", lambda E: E.tensor_copy(lo_t[:], sq_t[:]),
                          reads=("f2sq",), writes=("f2lot",))
                    em.op("pe", lambda E, j=j: E.matmul(
                        p_q[:], onescb[:], hi_t[:], start=(j == 0), stop=False),
                        reads=("f2hit", "k_onesc"), writes=("f2pqk",))
                    em.op("pe", lambda E, j=j: E.matmul(
                        p_q[:], onescb[:], lo_t[:], start=False,
                        stop=(j == NKM - 1)),
                        reads=("f2lot", "k_onesc"), writes=("f2pqk",))
                em.op("act", lambda E: E.activation(
                    rowA[0:1, 0:FBLK], p_s[:], AF.Copy, scale=1.0 / DM),
                    reads=("f2psk",), writes=("f2mu",))
                em.op("act", lambda E: E.activation(
                    rowA[0:1, FBLK:2 * FBLK], p_q[:], AF.Copy, scale=1.0 / DM),
                    reads=("f2pqk",), writes=("f2ex",))
                em.op("dve", lambda E: E.tensor_tensor(
                    rowA[0:1, 2 * FBLK:3 * FBLK], rowA[0:1, 0:FBLK],
                    rowA[0:1, 0:FBLK], AL.mult),
                    reads=("f2mu",), writes=("f2mu2",))
                em.op("dve", lambda E: E.tensor_tensor(
                    rowA[0:1, FBLK:2 * FBLK], rowA[0:1, FBLK:2 * FBLK],
                    rowA[0:1, 2 * FBLK:3 * FBLK], AL.subtract),
                    reads=("f2ex", "f2mu2"), writes=("f2ex",))
                em.op("act", lambda E: E.activation(
                    rowA[0:1, 3 * FBLK:4 * FBLK], rowA[0:1, FBLK:2 * FBLK],
                    AF.Ln, bias=EPS),
                    reads=("f2ex",), writes=("f2rr",))
                em.op("act", lambda E: E.activation(
                    rowA[0:1, 3 * FBLK:4 * FBLK], rowA[0:1, 3 * FBLK:4 * FBLK],
                    AF.Exp, scale=-0.5),
                    reads=("f2rr",), writes=("f2rr",))
                bcast_row_exact("f2mbc", mbc, "f2mu", rowA[0:1, 0:FBLK], FBLK)
                bcast_row_exact("f2rbc", rbc, "f2rr",
                                rowA[0:1, 3 * FBLK:4 * FBLK], FBLK)
                for j in range(NKM):
                    em.op("dve", lambda E, j=j: E.tensor_sub(
                        x3f[:], s2t[j][:], mbc[:]),
                        reads=(s2rd[j], "f2mbc"), writes=("f2x3f",))
                    em.op("dve", lambda E: E.tensor_mul(x3f[:], x3f[:], rbc[:]),
                          reads=("f2x3f", "f2rbc"), writes=("f2x3f",))
                    sl_hi = x3hi[:, j * 4096 + blk * FBLK:
                                 j * 4096 + (blk + 1) * FBLK]
                    sl_lo = x3lo[:, j * 4096 + blk * FBLK:
                                 j * 4096 + (blk + 1) * FBLK]
                    em.op("dve", lambda E, sl_hi=sl_hi: E.tensor_copy(
                        sl_hi, x3f[:]),
                        reads=("f2x3f",), writes=("x3hi_%d_%d" % (j, blk),))
                    em.op("dve", lambda E, sl_hi=sl_hi: E.tensor_sub(
                        x3f[:], x3f[:], sl_hi),
                        reads=("f2x3f", "x3hi_%d_%d" % (j, blk)),
                        writes=("f2x3f",))
                    em.op("dve", lambda E, sl_lo=sl_lo: E.tensor_copy(
                        sl_lo, x3f[:]),
                        reads=("f2x3f",), writes=("x3lo_%d_%d" % (j, blk),))
                x3rd = tuple("x3hi_%d_%d" % (j, blk) for j in range(NKM))
                x3rdl = tuple("x3lo_%d_%d" % (j, blk) for j in range(NKM))
                for j in range(NKM):
                    hi_sl = x3hi[:, j * 4096 + blk * FBLK:
                                 j * 4096 + (blk + 1) * FBLK]
                    lo_sl = x3lo[:, j * 4096 + blk * FBLK:
                                 j * 4096 + (blk + 1) * FBLK]
                    em.op("pe", lambda E, j=j, hi_sl=hi_sl: E.matmul(
                        p_g[:], gwh[:, j * NE:(j + 1) * NE], hi_sl,
                        start=(j == 0), stop=False),
                        reads=("f2gwh",) + x3rd, writes=("f2pgk",))
                    em.op("pe", lambda E, j=j, hi_sl=hi_sl: E.matmul(
                        p_g[:], gwl[:, j * NE:(j + 1) * NE], hi_sl,
                        start=False, stop=False),
                        reads=("f2gwl",) + x3rd, writes=("f2pgk",))
                    em.op("pe", lambda E, j=j, lo_sl=lo_sl: E.matmul(
                        p_g[:], gwh[:, j * NE:(j + 1) * NE], lo_sl,
                        start=False, stop=(j == NKM - 1)),
                        reads=("f2gwh",) + x3rdl, writes=("f2pgk",))
                em.op("act", lambda E: E.activation(
                    sq_t[0:NE, :], p_g[:], AF.Copy),
                    reads=("f2pgk",), writes=("f2sq",))
                em.op("dve", lambda E: E.tensor_copy(hi_t[0:NE, :], sq_t[0:NE, :]),
                      reads=("f2sq",), writes=("f2hit",))
                em.op("dve", lambda E: E.tensor_sub(
                    sq_t[0:NE, :], sq_t[0:NE, :], hi_t[0:NE, :]),
                    reads=("f2sq", "f2hit"), writes=("f2sq",))
                em.op("dve", lambda E: E.tensor_copy(lo_t[0:NE, :], sq_t[0:NE, :]),
                      reads=("f2sq",), writes=("f2lot",))
                for tt in range(FBLK // 128):
                    em.op("pe", lambda E, tt=tt: E.transpose(
                        p_t[:, 0:NE], hi_t[0:NE, tt * 128:(tt + 1) * 128],
                        ident[0:NE, 0:NE]),
                        reads=("f2hit", "k_ident"), writes=("f2ptk",))
                    em.op("act", lambda E: E.activation(
                        glr[:, 0:NE], p_t[:, 0:NE], AF.Copy),
                        reads=("f2ptk",), writes=("f2glr0",))
                    em.op("pe", lambda E, tt=tt: E.transpose(
                        p_t[:, 0:NE], lo_t[0:NE, tt * 128:(tt + 1) * 128],
                        ident[0:NE, 0:NE]),
                        reads=("f2lot", "k_ident"), writes=("f2ptk",))
                    em.op("act", lambda E: E.activation(
                        glr[:, NE:2 * NE], p_t[:, 0:NE], AF.Copy),
                        reads=("f2ptk",), writes=("f2glrl",))
                    em.op("dve", lambda E: E.tensor_add(
                        glr[:, 0:NE], glr[:, 0:NE], glr[:, NE:2 * NE]),
                        reads=("f2glr0", "f2glrl"), writes=("f2glr0",))
                    em.op("act", lambda E: E.activation(
                        sc_tm[:], glr[:, 0:NE], AF.Sigmoid),
                        reads=("f2glr0",), writes=("f2sctm",))
                    em.op("dve", lambda E: E.tensor_add(bia[:], sc_tm[:], ebbS[:]),
                          reads=("f2sctm", "k_eb"), writes=("f2bia",))
                    em.op("dve", lambda E: E.reduce_max(
                        m1c[:, 0:1], bia[:], axis=mybir.AxisListType.X),
                        reads=("f2bia",), writes=("f2m1",))
                    em.op("dve", lambda E: E.tensor_scalar(
                        msk[:, 0:NE], bia[:], m1c[:, 0:1], None, AL.is_equal),
                        reads=("f2bia", "f2m1"), writes=("f2msk1",))
                    em.op("dve", lambda E: E.tensor_mul(
                        msk[:, 0:NE], msk[:, 0:NE], p8row[:]),
                        reads=("f2msk1", "k_p8"), writes=("f2msk1",))
                    em.op("dve", lambda E: E.reduce_max(
                        m1c[:, 1:2], msk[:, 0:NE], axis=mybir.AxisListType.X),
                        reads=("f2msk1",), writes=("f2pm1",))
                    em.op("dve", lambda E: E.tensor_scalar(
                        msk[:, 0:NE], msk[:, 0:NE], m1c[:, 1:2], None,
                        AL.is_equal),
                        reads=("f2msk1", "f2pm1"), writes=("f2msk1",))
                    em.op("dve", lambda E: E.scalar_tensor_tensor(
                        msk[:, NE:2 * NE], msk[:, 0:NE], BIGNEG, bia[:],
                        AL.mult, AL.add),
                        reads=("f2msk1", "f2bia"), writes=("f2msk2",))
                    em.op("dve", lambda E: E.reduce_max(
                        m1c[:, 2:3], msk[:, NE:2 * NE], axis=mybir.AxisListType.X),
                        reads=("f2msk2",), writes=("f2m2",))
                    em.op("dve", lambda E: E.tensor_scalar(
                        msk[:, NE:2 * NE], msk[:, NE:2 * NE], m1c[:, 2:3], None,
                        AL.is_equal),
                        reads=("f2msk2", "f2m2"), writes=("f2msk2",))
                    em.op("dve", lambda E: E.tensor_mul(
                        msk[:, NE:2 * NE], msk[:, NE:2 * NE], p8row[:]),
                        reads=("f2msk2", "k_p8"), writes=("f2msk2",))
                    em.op("dve", lambda E: E.reduce_max(
                        m1c[:, 3:4], msk[:, NE:2 * NE], axis=mybir.AxisListType.X),
                        reads=("f2msk2",), writes=("f2pm2",))
                    em.op("dve", lambda E: E.tensor_scalar(
                        msk[:, NE:2 * NE], msk[:, NE:2 * NE], m1c[:, 3:4], None,
                        AL.is_equal),
                        reads=("f2msk2", "f2pm2"), writes=("f2msk2",))
                    em.op("dve", lambda E: E.tensor_add(
                        msk[:, 2 * NE:3 * NE], msk[:, 0:NE], msk[:, NE:2 * NE]),
                        reads=("f2msk1", "f2msk2"), writes=("f2msks",))
                    em.op("dve", lambda E: E.tensor_mul(
                        wgt[:], msk[:, 2 * NE:3 * NE], sc_tm[:]),
                        reads=("f2msks", "f2sctm"), writes=("f2wgt",))
                    em.op("dve", lambda E: E.tensor_mul(wgt[:], wgt[:], sc_tm[:]),
                          reads=("f2wgt", "f2sctm"), writes=("f2wgt",))
                    if debug and blk == 0 and tt == 0:
                        dma("dbgw", dbg["d_wgt"][:, 0:NE], wgt[:],
                            reads=("f2wgt",))
                    em.op("dve", lambda E: E.tensor_copy(wgh[:], wgt[:]),
                          reads=("f2wgt",), writes=("f2wgh",))
                    em.op("dve", lambda E: E.tensor_sub(wgt[:], wgt[:], wgh[:]),
                          reads=("f2wgt", "f2wgh"), writes=("f2wgt",))
                    em.op("dve", lambda E: E.tensor_copy(wgl[:], wgt[:]),
                          reads=("f2wgt",), writes=("f2wgl",))
                    em.op("pe", lambda E: E.transpose(
                        p_t[0:NE, :], wgh[:], ident[:]),
                        reads=("f2wgh", "k_ident"), writes=("f2ptk",))
                    em.op("act", lambda E: E.activation(
                        ghtm[:], p_t[0:NE, :], AF.Copy),
                        reads=("f2ptk",), writes=("f2wth",))
                    em.op("pe", lambda E: E.transpose(
                        p_t[0:NE, :], wgl[:], ident[:]),
                        reads=("f2wgl", "k_ident"), writes=("f2ptk",))
                    em.op("act", lambda E: E.activation(
                        gltm[:], p_t[0:NE, :], AF.Copy),
                        reads=("f2ptk",), writes=("f2wtl",))
                    t0 = blk * FBLK + tt * 128

                    def wsth(E, t0=t0):
                        dst = sym(wgd.ap(), t0, NE, 4096, 128)
                        return E.dma_start(out=dst, in_=ghtm[:])
                    em.op("sync", wsth, reads=("f2wth",),
                          writes=("wgS_%d_%d" % (blk, tt),), dma=True)

                    def wstl(E, t0=t0):
                        dst = sym(wgd.ap(), NE * 4096 + t0, NE, 4096, 128)
                        return E.dma_start(out=dst, in_=gltm[:])
                    em.op("sync", wstl, reads=("f2wtl",),
                          writes=("wgS_%d_%d" % (blk, tt),), dma=True)

        wgkeys = tuple("wgS_%d_%d" % (b_, t_) for b_ in range(4096 // FBLK)
                       for t_ in range(FBLK // 128))
        x3keys = tuple("x3hi_%d_%d" % (j, b_) for j in range(NKM)
                       for b_ in range(4096 // FBLK))

        if True:
            em.phase()
            ar.reset(65536)
            wupP = [ar.alloc((128, NKM * NHT * 128), BF16) for p in range(2)]
            wstgB = ar.alloc((128, NHT * 128))
            hsq = ar.alloc((128, FBLK))
            hbf = ar.alloc((128, FBLK), BF16)
            wgrow = ar.alloc((1, 2 * FBLK), BF16)
            p_h = [PB[0], PB[1]]
            p_w = PB[2]

            for g in range(NE + 1):
                gp = g % 2
                wup = wupP[gp]
                for j in range(NKM):
                    dma("f2wustg", wstgB[:], sym(
                        w_upT, g * DM * NHT * 128 + j * 128 * NHT * 128,
                        128, NHT * 128, NHT * 128))
                    em.op("dve", lambda E, j=j, wup=wup: E.tensor_copy(
                        wup[:, j * NHT * 128:(j + 1) * NHT * 128], wstgB[:]),
                        reads=("f2wustg",), writes=("f2wup%d_%d" % (gp, j),))
                uprd = tuple("f2wup%d_%d" % (gp, j) for j in range(NKM))
                for blk in range(4096 // FBLK):
                    if g < NE:
                        dma("f2wgrow", wgrow[0:1, 0:FBLK], sym(
                            wgd.ap(), g * 4096 + blk * FBLK, 1, 4096, FBLK),
                            reads=wgkeys)
                        dma("f2wgrow2", wgrow[0:1, FBLK:2 * FBLK], sym(
                            wgd.ap(), NE * 4096 + g * 4096 + blk * FBLK,
                            1, 4096, FBLK), reads=wgkeys)
                        em.op("pe", lambda E: E.matmul(
                            p_w[:], onesb[:], wgrow[0:1, 0:FBLK],
                            start=True, stop=False),
                            reads=("f2wgrow", "k_ones"), writes=("f2pwk",))
                        em.op("pe", lambda E: E.matmul(
                            p_w[:], onesb[:], wgrow[0:1, FBLK:2 * FBLK],
                            start=False, stop=True),
                            reads=("f2wgrow2", "k_ones"), writes=("f2pwk",))
                    for ht in range(NHT):
                        ph, kph = p_h[ht % 2], "f2ph%d" % (ht % 2)
                        for j in range(NKM):
                            hi_sl = x3hi[:, j * 4096 + blk * FBLK:
                                         j * 4096 + (blk + 1) * FBLK]
                            em.op("pe", lambda E, ht=ht, j=j, hi_sl=hi_sl,
                                  ph=ph, wup=wup: E.matmul(
                                      ph[:], wup[:, j * NHT * 128 + ht * 128:
                                                  j * NHT * 128 + (ht + 1) * 128],
                                      hi_sl, start=(j == 0), stop=(j == NKM - 1)),
                                  reads=uprd + x3keys, writes=(kph,))
                        em.op("act", lambda E, ph=ph: E.activation(
                            hsq[:], ph[:], AF.Relu),
                            reads=(kph,), writes=("f2hr",))
                        em.op("act", lambda E: E.activation(
                            hsq[:], hsq[:], AF.Square),
                            reads=("f2hr",), writes=("f2hr",))
                        if g < NE:
                            em.op("dve", lambda E: E.tensor_mul(
                                hbf[:], hsq[:], p_w[:]),
                                reads=("f2hr", "f2pwk"), writes=("f2hbf",))
                        else:
                            em.op("dve", lambda E: E.tensor_copy(hbf[:], hsq[:]),
                                  reads=("f2hr",), writes=("f2hbf",))
                        dma("f2hw", sym(hd.ap(),
                                        g * NHT * 128 * 4096 + ht * 128 * 4096 +
                                        blk * FBLK, 128, 4096, FBLK), hbf[:],
                            reads=("f2hbf",))

        em.op("sync", lambda E: E.dma_start(out=oaccd.ap(), in_=s2d.ap()),
              reads=("f1s2w",), writes=("oacc",), dma=True)
        if True:
            em.phase()
            ar.reset()
            wdnP = [ar.alloc((128, NHT * DM), BF16) for p in range(2)]
            wstgC = ar.alloc((128, DM))
            hsbP = [ar.alloc((128, NHT * FBLK), BF16) for p in range(2)]
            foP = [ar.alloc((128, FBLK)) for p in range(2)]
            aoP = [ar.alloc((128, FBLK)) for p in range(2)]
            p_u = [PB[0], PB[1]]

            for g in range(NE + 1):
                gp = g % 2
                wdn = wdnP[gp]
                for kt in range(NHT):
                    dma("f2wdstg", wstgC[:], sym(
                        w_downT, g * NHT * 128 * DM + kt * 128 * DM,
                        128, DM, DM))
                    em.op("dve", lambda E, kt=kt, wdn=wdn: E.tensor_copy(
                        wdn[:, kt * DM:(kt + 1) * DM], wstgC[:]),
                        reads=("f2wdstg",), writes=("f2wdn%d_%d" % (gp, kt),))
                dnrd = tuple("f2wdn%d_%d" % (gp, kt) for kt in range(NHT))
                for blk in range(4096 // FBLK):
                    bp = blk % 2
                    hsb = hsbP[bp]
                    hrd = []
                    for ht in range(NHT):
                        dma("f2hsb%d_%d" % (bp, ht),
                            hsb[:, ht * FBLK:(ht + 1) * FBLK],
                            sym(hd.ap(), g * NHT * 128 * 4096 + ht * 128 * 4096 +
                                blk * FBLK, 128, 4096, FBLK),
                            reads=("f2hw",))
                        hrd.append("f2hsb%d_%d" % (bp, ht))
                    hrd = tuple(hrd)
                    for dm in range(NKM):
                        pu, kpu = p_u[dm % 2], "f2pu%d" % (dm % 2)
                        fo = foP[dm % 2]
                        ao = aoP[dm % 2]
                        kfo = "f2fo%d" % (dm % 2)
                        kao = "f2ao%d" % (dm % 2)
                        for kt in range(NHT):
                            em.op("pe", lambda E, dm=dm, kt=kt, pu=pu, wdn=wdn,
                                  hsb=hsb: E.matmul(
                                      pu[:], wdn[:, kt * DM + dm * 128:
                                                  kt * DM + (dm + 1) * 128],
                                      hsb[:, kt * FBLK:(kt + 1) * FBLK],
                                      start=(kt == 0), stop=(kt == NHT - 1)),
                                  reads=dnrd + hrd, writes=(kpu,))
                        em.op("act", lambda E, pu=pu, fo=fo: E.activation(
                            fo[:], pu[:], AF.Copy),
                            reads=(kpu,), writes=(kfo,))
                        okey = "oacc_%d_%d" % (blk, dm)
                        dma(kao, ao[:], sym(
                            oaccd.ap(), dm * 128 * 4096 + blk * FBLK,
                            128, 4096, FBLK), reads=("oacc", okey))
                        em.op("dve", lambda E, fo=fo, ao=ao: E.tensor_add(
                            fo[:], fo[:], ao[:]),
                            reads=(kfo, kao), writes=(kfo,))
                        if g < NE:
                            dma(okey, sym(oaccd.ap(),
                                          dm * 128 * 4096 + blk * FBLK,
                                          128, 4096, FBLK), fo[:],
                                reads=(kfo,))
                        else:
                            def wout(E, dm=dm, blk=blk, fo=fo):
                                return E.dma_start(
                                    out=sym(out, dm * 128 * 4096 + blk * FBLK,
                                            128, 4096, FBLK), in_=fo[:])
                            em.op("sync", wout, reads=(kfo,),
                                  writes=("kout",), dma=True)


        if debug:
            Vecw = type(wmoT.ap().ap)
            em.op("sync", lambda E: E.dma_start(
                out=dbg["d_wt"],
                in_=dataclasses.replace(wmoT.ap(), offset=0,
                                        ap=Vecw([[1, 2 * DH * DM]]))),
                reads=("pwout",), writes=("dbgx",), dma=True)
            em.op("sync", lambda E: E.dma_start(out=dbg["d_n2"], in_=n2loc.ap()),
                  reads=("k_n2loc",), writes=("dbgx",), dma=True)
            em.op("sync", lambda E: E.dma_start(out=dbg["d_xdbl"],
                                                in_=xdbld.ap()),
                  reads=("pxw",), writes=("dbgx",), dma=True)
            Vecp = type(partiald.ap().ap)
            em.op("sync", lambda E: E.dma_start(
                out=dbg["d_partial"],
                in_=dataclasses.replace(partiald.ap(), offset=NCH * DM * TC,
                                        ap=Vecp([[1, NCH * DM * TC]]))),
                reads=("m2pw",), writes=("dbgx",), dma=True)
            em.op("sync", lambda E: E.dma_start(
                out=dbg["d_xc"],
                in_=dataclasses.replace(xcd.ap(), offset=0,
                                        ap=Vecp([[1, DH * L]]))),
                reads=("m1xcw",), writes=("dbgx",), dma=True)
            em.op("sync", lambda E: E.dma_start(out=dbg["d_pm"],
                                                in_=pmined.ap()),
                  reads=("k_pm",), writes=("dbgx",), dma=True)

        for x in em.names:
            if x in Em.ENGS or em.counts[x] == 0:
                continue
            em.raw("sync", lambda E, s=em.sems[x], c=em.counts[x]:
                   E.wait_ge(s, c))

        with nc.Block() as block:
            block.sync(lambda E: [t(E) for t in em.thunks["sync"]])
            block.scalar(lambda E: [t(E) for t in em.thunks["act"]])
            block.vector(lambda E: [t(E) for t in em.thunks["dve"]])
            block.tensor(lambda E: [t(E) for t in em.thunks["pe"]])
            block.gpsimd(lambda E: [t(E) for t in em.thunks["pool"]])

    return nc, offspec


def _noinc():
    return None


_MAGICS = np.array([[0x5A5A0101, 0x5A5A0202, 0x5A5A0303, 0x5A5A0404,
                     0, 0, 0, 0]], dtype=np.int32)
_NC_CACHE = {}


def _prep_core_inputs(inputs, c):
    f32 = np.float32
    side = c % 2
    k = c // 2
    batch = k % 2
    dup = k // 2
    tok_off = dup * 8192
    myfin = tok_off + side * 4096

    x = np.ascontiguousarray(np.asarray(inputs["L0_attn_x"], dtype=f32))
    d = {}
    d["x_ln"] = np.ascontiguousarray(x[batch, side * 8192:(side + 1) * 8192])
    xm = np.ascontiguousarray(x[batch, myfin:myfin + 4096])
    d["x_my"] = xm
    d["x_myT"] = np.ascontiguousarray(xm.T)

    w_inT = np.zeros((2, DM, 2 * DH), f32)
    conv_diag = np.zeros((2, 4, 128, NDT * 128), f32)
    conv_bias = np.zeros((128, 2 * NDT), f32)
    w_xpT = np.zeros((2, DH, 128), f32)
    w_dtT = np.zeros((2, DTR, DH), f32)
    dt_bias = np.zeros((128, 2 * NDT), f32)
    Alog_r = np.zeros((128, 2 * NDT * S), f32)
    D_r = np.zeros((128, 2 * NDT), f32)
    w_mo = np.zeros((2, DM, DH), f32)
    w_opT = np.zeros((2, DM, DM), f32)
    out_w = np.asarray(inputs["L0_attn_out_w"], dtype=f32)
    idx = np.arange(128)
    for di, tag in enumerate(("f", "b")):
        in_w = np.asarray(inputs["L0_attn_in_w_" + tag], dtype=f32)
        sl = np.concatenate([in_w[side * DH:(side + 1) * DH],
                             in_w[DI + side * DH:DI + (side + 1) * DH]], axis=0)
        w_inT[di] = sl.T
        cw = np.asarray(inputs["L0_attn_conv_w_" + tag], dtype=f32)[:, 0, :]
        cw = cw[side * DH:(side + 1) * DH]
        for tap in range(4):
            for dt in range(NDT):
                conv_diag[di, tap, idx, dt * 128 + idx] = cw[dt * 128 + idx, tap]
        cb = np.asarray(inputs["L0_attn_conv_b_" + tag], dtype=f32)
        conv_bias[:, di * NDT:(di + 1) * NDT] = \
            cb[side * DH:(side + 1) * DH].reshape(NDT, 128).T
        xp = np.asarray(inputs["L0_attn_xproj_w_" + tag], dtype=f32)
        xps = xp[:, side * DH:(side + 1) * DH]
        w_xpT[di][:, 0:16] = xps[64:80].T
        w_xpT[di][:, 32:48] = xps[80:96].T
        w_xpT[di][:, 64:128] = xps[0:64].T
        dw = np.asarray(inputs["L0_attn_dt_w_" + tag], dtype=f32)
        w_dtT[di] = dw[side * DH:(side + 1) * DH].T
        db = np.asarray(inputs["L0_attn_dt_b_" + tag], dtype=f32)
        dt_bias[:, di * NDT:(di + 1) * NDT] = \
            db[side * DH:(side + 1) * DH].reshape(NDT, 128).T
        Al = np.asarray(inputs["L0_attn_A_log_" + tag], dtype=f32)
        Al = Al[side * DH:(side + 1) * DH]
        Alog_r[:, di * 128:(di + 1) * 128] = \
            Al.reshape(NDT, 128, S).transpose(1, 0, 2).reshape(128, NDT * S)
        Dv = np.asarray(inputs["L0_attn_D_" + tag], dtype=f32)
        D_r[:, di * NDT:(di + 1) * NDT] = \
            Dv[side * DH:(side + 1) * DH].reshape(NDT, 128).T
        mo = np.asarray(inputs["L0_attn_mout_w_" + tag], dtype=f32)
        w_mo[di] = mo[:, side * DH:(side + 1) * DH]
        w_opT[di] = out_w[:, di * DM:(di + 1) * DM].T
    d["w_inT"] = w_inT
    d["conv_diag"] = conv_diag
    d["conv_b"] = conv_bias
    d["w_xpT"] = w_xpT
    d["w_dtT"] = w_dtT
    d["dt_b"] = dt_bias
    d["Alog_r"] = Alog_r
    d["D_r"] = D_r
    d["w_mo"] = w_mo
    d["w_opT"] = w_opT
    ob = np.asarray(inputs["L0_attn_out_b"], dtype=f32)
    d["out_bias"] = np.ascontiguousarray(ob.reshape(NKM, 128).T)
    d["gamma"] = np.asarray(inputs["L0_attn_gamma"], dtype=f32).reshape(1, DM)
    d["beta"] = np.asarray(inputs["L0_attn_beta"], dtype=f32).reshape(1, DM)
    d["gate_wT"] = np.ascontiguousarray(
        np.asarray(inputs["L0_ffn_gate_w"], dtype=f32).T)
    d["eb_row"] = np.asarray(
        inputs["L0_ffn_expert_bias"], dtype=f32).reshape(1, NE)
    wu = np.asarray(inputs["L0_ffn_w_up"], dtype=f32)
    wd = np.asarray(inputs["L0_ffn_w_down"], dtype=f32)
    swu = np.asarray(inputs["L0_ffn_shared_w_up"], dtype=f32)
    swd = np.asarray(inputs["L0_ffn_shared_w_down"], dtype=f32)
    w_upT = np.zeros((NE + 1, DM, NHT * 128), f32)
    w_downT = np.zeros((NE + 1, NHT * 128, DM), f32)
    for g in range(NE):
        w_upT[g] = wu[g].T
        w_downT[g] = wd[g].T
    w_upT[NE] = swu.T
    w_downT[NE] = swd.T
    d["w_upT"] = w_upT
    d["w_downT"] = w_downT
    d["cfg"] = np.array([[side, 1 - side, tok_off, side * 4096,
                          0, 0, 0, 0]], dtype=np.int32)
    d["magics"] = _MAGICS
    return d


_BASEVALS = None


def _base_value(name, c):
    side = c % 2
    k = c // 2
    dup = k // 2
    vals = {
        "n2w": side * (DM * HL),
        "xdw": side * (2 * 128 * L),
        "xdr": (1 - side) * (2 * 128 * L),
        "tokp": NCH * DM * TC + dup * 16 * (DM * TC),
        "shw0": side * (16 * DM * TC),
        "f1a0": side * (16 * DM * TC) + side * 8 * (DM * TC),
        "f1b0": (1 - side) * (16 * DM * TC) + side * 8 * (DM * TC),
        "flagw": side * 16,
        "flagr": (1 - side) * 16,
    }
    return vals[name]


def kernel(**inputs):
    if "nc" not in _NC_CACHE:
        _NC_CACHE["nc"], _NC_CACHE["offspec"] = build_nc()
    nc = _NC_CACHE["nc"]
    offspec = _NC_CACHE["offspec"]
    in_maps = [_prep_core_inputs(inputs, c) for c in range(8)]
    for c in range(8):
        tbl = np.zeros((1, 2048), np.int32)
        for idx, (base, const) in enumerate(offspec):
            tbl[0, idx] = _base_value(base, c) + const
        in_maps[c]["offs"] = tbl
    res = run_bass_kernel_spmd(nc, in_maps, core_ids=list(range(8)))
    outp = np.zeros((B, L, DM), np.float32)
    for c in range(8):
        side = c % 2
        k = c // 2
        batch = k % 2
        dup = k // 2
        myfin = dup * 8192 + side * 4096
        outp[batch, myfin:myfin + 4096, :] = res.results[c]["out"].T
    return outp
